# version 04
# baseline (speedup 1.0000x reference)
import contextlib
import numpy as np
import ml_dtypes
import concourse.bass as bass
import concourse.mybir as mybir
from concourse.bass_utils import run_bass_kernel_spmd

F32 = mybir.dt.float32
BF16 = mybir.dt.bfloat16
I8 = mybir.dt.int8
U8 = mybir.dt.uint8
AF = mybir.ActivationFunctionType
ALU = mybir.AluOpType

D = 2048
DT = 16
S = 16
TOUT = 1024
PAD = 64
T = PAD + TOUT
NO = 8
NG = 6
INTER = 8192
IT = 64
WIN3 = [(0, 512), (512, 1024), (1024, T)]
EPS = 1e-5
SBUF_BYTES = 212832

ENGINES = ("sync", "gpsimd", "scalar", "vector", "tensor")


class Trk:
    def __init__(self, name, step=1):
        self.name, self.step, self.n, self.h = name, step, 0, None


class Alloc:
    def __init__(self, nbytes):
        self.cap = nbytes
        self.items = []

    def pin(self, off, size, t0, t1, name):
        size = (size + 63) & ~63
        for (o, s, a0, a1, n) in self.items:
            if not (a1 <= t0 or t1 <= a0) and not (o + s <= off or off + size <= o):
                raise MemoryError(f"pin clash: {name} vs {n}")
        self.items.append((off, size, t0, t1, name))
        return off

    def alloc(self, size, t0, t1, name):
        size = (size + 63) & ~63
        spans = sorted((o, o + s) for (o, s, a0, a1, _n) in self.items
                       if not (a1 <= t0 or t1 <= a0))
        holes, cur = [], 0
        for (s0, s1) in spans:
            if s0 > cur:
                holes.append((cur, s0))
            cur = max(cur, s1)
        if cur < self.cap:
            holes.append((cur, self.cap))
        cand = [h for h in holes if h[1] - h[0] >= size]
        if not cand:
            raise MemoryError(f"SBUF overflow: {name} {size}B t[{t0},{t1}]")
        off = min(cand, key=lambda h: (h[1] - h[0], h[0]))[0]
        self.items.append((off, size, t0, t1, name))
        return off


def build_program(taps=()):
    nc = bass.Bass("TRN2", target_bir_lowering=False, debug=False)

    d_x = nc.dram_tensor("xT", [128, DT * T], F32, kind="ExternalInput").ap()
    d_win = nc.dram_tensor("w_in_t", [DT, 128, 2048], F32, kind="ExternalInput").ap()
    d_wout = nc.dram_tensor("w_out_t", [DT, 128, 2048], F32, kind="ExternalInput").ap()
    d_bwt = nc.dram_tensor("b_w_t", [128, 256], F32, kind="ExternalInput").ap()
    d_cwt = nc.dram_tensor("c_w_t", [16, 2048], F32, kind="ExternalInput").ap()
    d_a = nc.dram_tensor("a_t", [128, 256], F32, kind="ExternalInput").ap()
    d_dcol = nc.dram_tensor("dcol", [128, 16], F32, kind="ExternalInput").ap()
    d_bin = nc.dram_tensor("bincol", [128, 16], F32, kind="ExternalInput").ap()
    d_bout = nc.dram_tensor("boutcol", [128, 16], F32, kind="ExternalInput").ap()
    d_sd = nc.dram_tensor("sdcol", [128, 16], F32, kind="ExternalInput").ap()
    d_sg = nc.dram_tensor("sgcol", [128, 64], F32, kind="ExternalInput").ap()
    d_su = nc.dram_tensor("sucol", [128, 64], F32, kind="ExternalInput").ap()
    d_wgu = nc.dram_tensor("wgu_t", [IT, 128, 4096], I8, kind="ExternalInput").ap()
    d_wd = nc.dram_tensor("wd_t", [DT, 2, 128, 4096], I8, kind="ExternalInput").ap()
    d_mask = nc.dram_tensor("maskb", [16, T], BF16, kind="ExternalInput").ap()
    d_rep16 = nc.dram_tensor("rep16b", [16, 128], BF16, kind="ExternalInput").ap()
    d_ones_b = nc.dram_tensor("ones_b", [128, 1], BF16, kind="ExternalInput").ap()
    d_ones_f = nc.dram_tensor("ones_f", [1, 128], F32, kind="ExternalInput").ap()
    d_oodf = nc.dram_tensor("ood_f", [1, 128], F32, kind="ExternalInput").ap()
    d_oodb = nc.dram_tensor("ood_b", [1, 128], BF16, kind="ExternalInput").ap()
    d_identf = nc.dram_tensor("ident_f", [128, 128], F32, kind="ExternalInput").ap()
    d_out = nc.dram_tensor("out_xT", [128, DT * TOUT], F32, kind="ExternalOutput").ap()
    tap_out = {}
    for tname, tshape, tdt in taps:
        tap_out[tname] = nc.dram_tensor("tap_" + tname, tshape, tdt,
                                        kind="ExternalOutput").ap()

    al = Alloc(SBUF_BYTES)
    o_x = al.pin(0, DT * T * 4, 0, 12, "x")
    o_scr = al.pin(192512, 1024 * 4, 0, 12, "scratch")
    o_sq = al.pin(196608, 4 * 512 * 2, 0, 12, "sqscr")
    o_srown = al.pin(200704, 2 * 1024 * 2, 4, 12, "sumrows")
    o_bcb = al.pin(204800, 2 * 1024, 5, 12, "bcastB")
    o_onesf = al.pin(206848, 128 * 4 + 256, 0, 12, "onesf+oodb")
    o_const = al.pin(207616, 2 * 64 + 2 * 256 + 64, 0, 12, "consts")
    o_fuseA = al.pin(69632, IT * 512 * 2, 6, 8, "fuseA")
    o_wri = al.pin(163840, 2 * 4096, 0, 2, "wring_in")
    o_srow = al.pin(172032, 2 * T * 2, 0, 2, "sumrows_wide")
    o_wro = al.pin(172032, 2 * 4096, 5, 6, "wring_out")
    o_cwt = al.pin(172032, 2048 * 2, 4, 5, "cwtb")
    o_ypre0 = al.pin(139264, DT * TOUT * 2, 4, 6, "ypre")
    o_ypre = [o_ypre0 + t * 2048 for t in range(DT)]
    o_xn1 = al.pin(69632, DT * T * 2, 0, 6, "xn1b")
    o_xpb = al.pin(104448, DT * T * 2, 1, 6, "x_projb/y_attnb")
    o_pkf = al.pin(104448, 2 * 8192, 0, 1, "pkf")
    o_pk = al.pin(139264, DT * NG * 128 * 2, 0, 4, "packs")
    o_fuseB = al.pin(69632, IT * 512 * 2, 8, 11, "fuseB")
    o_xy1 = al.pin(135168, DT * 512 * 2, 6, 10, "x2na+y2a")
    o_x2na = o_xy1
    o_y2a = o_xy1
    o_xy2 = al.pin(151552, DT * 512 * 2, 8, 12, "x2nb+y2b")
    o_x2nb = o_xy2
    o_y2b = o_xy2
    o_wguA = al.pin(151552, 2 * 8192, 6, 8, "wguA")
    o_wguB = al.pin(167936, 2 * 8192, 8, 11, "wguB")
    o_wrdA = al.pin(167936, 2 * 4096, 6, 8, "wrdA")
    o_wrdB = al.pin(184320, 2 * 4096, 9, 12, "wrdB")
    o_gh = al.pin(163840, NG * 576 * 4, 3, 4, "ghat")
    o_dr = al.pin(177664, T * 4, 2, 4, "deltar")
    o_acc1 = al.pin(182016, 520 * 4, 3, 4, "acc1")
    o_shb = al.pin(184128, TOUT * 2, 3, 5, "S_hb")
    o_bca = al.pin(176384, 2048 + 1024, 0, 2, "bcastA")
    o_mask = al.pin(182016, T * 2, 0, 3, "maskb")
    o_db = al.pin(163840, T * 2, 2, 3, "deltab")
    o_sig = al.pin(120832, 2 * 1024, 0, 1, "sig")
    o_mbf1 = al.pin(180224, 2048, 5, 7, "mbf_w0")
    o_mbf2 = al.pin(184320, 2048, 5, 9, "mbf_w1")
    o_bcb2 = al.pin(186368, 1024, 5, 9, "bcastB2")
    o_bwt = al.pin(184192, 256 * 2, 0, 3, "bwtb")
    o_rep16 = al.pin(184704, 256, 0, 3, "rep16")
    o_dcol = al.pin(186688, 64, 0, 5, "dcol")
    o_binc = al.pin(179456, 64, 0, 2, "binc")
    o_identf = al.pin(186176, 128 * 4, 0, 4, "identf")
    import os as _os
    if _os.environ.get("KDEBUG_ALLOC"):
        for (o, s, a0, a1, n) in sorted(al.items):
            print(f"{o:7d} +{s:6d} t[{a0:2d},{a1:2d}] {n}")
        raise SystemExit(0)

    plan = {e: [] for e in ENGINES}

    def Praw(eng, fn):
        plan[eng].append(fn)

    t_pe, t_vec, t_scl = Trk("t_pe"), Trk("t_vec"), Trk("t_scl")
    s_const = Trk("s_const", 16)
    s_x = [Trk(f"s_x{i}", 16) for i in range(3)]
    s_bwt, s_cwt = Trk("s_bwt", 16), Trk("s_cwt", 16)
    s_wri = [Trk(f"s_wri{i}", 16) for i in range(2)]
    s_wro = [Trk(f"s_wro{i}", 16) for i in range(2)]
    s_wrf = [Trk(f"s_wrf{i}", 16) for i in range(4)]
    s_wrd = [Trk(f"s_wrd{i}", 16) for i in range(2)]
    s_out, s_tap = Trk("s_out", 16), Trk("s_tap", 16)
    DMAT = ([s_const] + s_x + [s_bwt, s_cwt] + s_wri + s_wro + s_wrf + s_wrd
            + [s_out, s_tap])
    TRKS = [t_pe, t_vec, t_scl] + DMAT
    ETRK = {"tensor": t_pe, "vector": t_vec, "scalar": t_scl}

    def DMA(queue, trk, build):
        trk.n += trk.step
        plan[queue].append(lambda e, b=build, t=trk: b(e).then_inc(t.h, t.step))
        return (trk, trk.n)

    def W(eng, trk, val=None):
        if val is None:
            trk, val = trk
        if val and val > 0:
            Praw(eng, lambda e, t=trk, v=val: e.wait_ge(t.h, v))

    def P(eng, fn):
        trk = ETRK[eng]
        trk.n += trk.step
        plan[eng].append(lambda e, b=fn, t=trk: b(e).then_inc(t.h, t.step))
        return trk.n

    def E(eng, build, inc=False):
        if inc:
            return P(eng, build)
        Praw(eng, build)
        return None

    def M(eng):
        return ETRK[eng].n

    def CB(eng):
        trk = ETRK[eng]
        Praw(eng, lambda e, t=trk, v=trk.n: e.wait_ge(t.h, v))

    def BAR():
        for eng in ("tensor", "vector", "scalar"):
            for trk in (t_pe, t_vec, t_scl):
                if trk.n > 0:
                    Praw(eng, lambda e, t=trk, v=trk.n: e.wait_ge(t.h, v))

    class Bank:
        def __init__(self, idx):
            self.idx, self.w_pe, self.r_trk = idx, 0, None

    banks = [Bank(i) for i in range(8)]

    def pe_acquire(bk):
        if bk.r_trk is not None:
            W("tensor", bk.r_trk[0], bk.r_trk[1])

    def mm(out_ap, lhsT, rhs, start, stop, inc=True):
        b = (lambda e, o=out_ap, l=lhsT, r=rhs, st=start, sp=stop:
             e.matmul(o, l, r, start=st, stop=sp))
        t_pe.n += 1
        plan["tensor"].append(lambda e, bb=b: bb(e).then_inc(t_pe.h, 1))
        return t_pe.n

    def act_table(set_name):
        from concourse.hw_specs import get_activation_tables
        sid = list(get_activation_tables(nc.m.arch).keys()).index(set_name)

        Praw("scalar", lambda e, sid=sid: e.add_instruction(
            mybir.InstLoadActFuncSet(
                name=nc.get_next_instruction_name(), act_func_set_id=sid,
                ins=[], outs=[])))
        for _ in range(12):
            P("scalar", lambda e: e.copy(out=DLY, in_=DLY))
            CB("scalar")

    es = contextlib.ExitStack()
    nc._kernel_es = es
    ar = es.enter_context(nc.sbuf_tensor("arena", [128, SBUF_BYTES], U8))
    pb = [es.enter_context(nc.psum_tensor(f"pb{i}", [128, 512], F32))
          for i in range(8)]
    for trk in TRKS:
        trk.h = es.enter_context(nc.semaphore(trk.name))

    arena_base = None
    for a in nc.m.functions[0].allocations:
        mls = getattr(a, "memorylocations", None)
        if not mls:
            continue
        if mls[0].name == "arena":
            arena_base = mls[0].addr
    assert arena_base is not None

    def REG(name, off, part, cols, dtype):
        return nc.alloc_sbuf_tensor_at(name, [part, cols], dtype,
                                       offset=arena_base + off).ap()

    XT = REG("XT", o_x, 128, DT * T, F32)
    XN1T = REG("XN1T", o_xn1, 128, DT * T, BF16)
    XPBT = REG("XPBT", o_xpb, 128, DT * T, BF16)
    PKT = REG("PKT", o_pk, 128, DT * NG * 128, BF16)
    GHT = REG("GHT", o_gh, 128, NG * 576, F32)
    DR = REG("DRT", o_dr, 128, T, F32)
    DB = REG("DBT", o_db, 16, T, BF16)
    MASK = REG("MASKT", o_mask, 16, T, BF16)
    ACC1 = REG("ACC1T", o_acc1, 128, 520, F32)
    SHB = REG("SHBT", o_shb, 16, TOUT, BF16)
    SCRT = REG("SCRT", o_scr, 128, 1024, F32)
    SCR = SCRT
    SRT = REG("SRT", o_srow, 1, 2 * T, BF16)
    SRA1 = SRT[:, 0:T]
    SRB1 = SRT[:, T:2 * T]
    SRNT = REG("SRNT", o_srown, 1, 2048, BF16)
    SRA = SRNT[:, 0:1024]
    SRB = SRNT[:, 1024:2048]
    CHS = SCRT[:, 0:512]
    CH2 = SCRT[:, 512:1024]
    GS = SCRT[:, 0:512]
    SQTT = REG("SQTT", o_sq, 128, 2048, BF16)

    def SQT(i):
        return SQTT[:, i * 512:(i + 1) * 512]

    DLY = SQTT[:, 0:128]

    MB_F = REG("MBF1", o_mbf1, 128, 512, F32)
    MB_F2 = REG("MBF2", o_mbf2, 128, 512, F32)
    INVBT = REG("INVBT", o_bcb, 128, 1024, BF16)
    INVB = INVBT[:, 0:512]
    MB_B = INVBT[:, 512:1024]
    INVB2 = REG("INVB2", o_bcb2, 128, 512, BF16)
    MB1F = REG("MB1F", o_bca, 128, 512, F32)
    INV1B = REG("INV1B", o_bca + 2048, 128, 512, BF16)
    WRIT = REG("WRIT", o_wri, 128, 4096, BF16)
    WROT = REG("WROT", o_wro, 128, 4096, BF16)
    WGUA = REG("WGUA", o_wguA, 128, 8192, BF16)
    WGUB = REG("WGUB", o_wguB, 128, 8192, BF16)

    def WRI(i):
        return WRIT[:, i * 2048:(i + 1) * 2048]

    def WRO(i):
        return WROT[:, i * 2048:(i + 1) * 2048]

    BWTB = REG("BWTT", o_bwt, 128, 256, BF16)
    CWTB = REG("CWTT", o_cwt, 16, 2048, BF16)
    oc = o_const
    DCOL = REG("DCOL", o_dcol, 128, 16, F32)
    BINC = REG("BINC", o_binc, 128, 16, F32)
    BOUTC = REG("BOUTC", oc, 128, 16, F32); oc += 64
    SDC = REG("SDC", oc, 128, 16, F32); oc += 64
    SGC = REG("SGC", oc, 128, 64, F32); oc += 256
    SUC = REG("SUC", oc, 128, 64, F32); oc += 256
    REP16 = REG("REP16", o_rep16, 16, 128, BF16)
    ONESB = REG("ONESB", oc, 128, 1, BF16); oc += 64
    ONESF = REG("ONESF", o_onesf, 1, 128, F32)
    OODB = REG("OODB", o_onesf + 512, 1, 128, BF16)
    IDENTF = REG("IDENTF", o_identf, 128, 128, F32)
    SIGA = REG("SIGA", o_sig, 128, 256, F32)
    SIG8 = REG("SIG8", o_sig + 1024, 128, 256, F32)
    PKF = REG("PKF", o_pkf, 128, 2048, F32)
    S8R = REG("S8R", o_pkf + 8192, 128, 2048, F32)
    YPRET = [REG(f"YPRE{t}", o_ypre[t], 128, TOUT, BF16) for t in range(DT)]
    FUSETA = REG("FUSETA", o_fuseA, 128, IT * 512, BF16)
    FUSETB = REG("FUSETB", o_fuseB, 128, IT * 512, BF16)
    X2NAT = REG("X2NAT", o_x2na, 128, DT * 512, BF16)
    WRDAT = REG("WRDAT", o_wrdA, 128, 4096, BF16)
    WRDBT = REG("WRDBT", o_wrdB, 128, 4096, BF16)
    X2NBT = REG("X2NBT", o_x2nb, 128, DT * 512, BF16)
    Y2AT = REG("Y2AT", o_y2a, 128, DT * 512, BF16)
    Y2BT = REG("Y2BT", o_y2b, 128, DT * 512, BF16)

    def Xj(j, c0, c1):
        return XT[:, j * T + c0:j * T + c1]

    def XN1j(j, c0, c1):
        return XN1T[:, j * T + c0:j * T + c1]

    def XPBj(j, c0, c1):
        return XPBT[:, j * T + c0:j * T + c1]

    def YATT(j, l0, l1):
        return XPBT[:, j * T + l0:j * T + l1]

    def PACK(j, g):
        return PKT[:, (j * NG + g) * 128:(j * NG + g + 1) * 128]

    def GH(g, c0, c1):
        return GHT[:, g * 576 + c0:g * 576 + c1]

    def YPRE(t, l0, l1):
        return YPRET[t][:, l0:l1]

    def FUSE(w, i):
        t = FUSETA if w == 0 else FUSETB
        return t[:, i * 512:(i + 1) * 512]

    def X2NW(k, j):
        t = X2NAT if k == 0 else X2NBT
        return t[:, j * 512:(j + 1) * 512]

    def Y2W(k, j):
        t = Y2AT if k == 0 else Y2BT
        return t[:, j * 512:(j + 1) * 512]

    def tap(name, view):
        if name not in tap_out:
            return
        W("sync", t_vec, t_vec.n)
        W("sync", t_scl, t_scl.n)
        W("sync", t_pe, t_pe.n)
        DMA("sync", s_tap, lambda e, d=tap_out[name], v=view:
            e.dma_start(out=d, in_=v))
        for eng in ("vector", "scalar", "tensor", "gpsimd"):
            W(eng, s_tap, s_tap.n)

    for (dst, src) in [(SIGA, d_a), (DCOL, d_dcol), (BINC, d_bin),
                       (BOUTC, d_bout), (SDC, d_sd), (SGC, d_sg), (SUC, d_su),
                       (MASK, d_mask), (REP16, d_rep16), (ONESB, d_ones_b),
                       (ONESF, d_ones_f), (OODB, d_oodb),
                       (IDENTF, d_identf)]:
        DMA("sync", s_const, lambda e, d=dst, s=src: e.dma_start(out=d, in_=s))
    v_consts = (s_const, s_const.n)
    d_x3 = d_x.rearrange("p (j c) -> p j c", j=DT)
    x_ar3 = XT.rearrange("p (j c) -> p j c", j=DT)
    v_xw = []
    for wi, (w0, w1) in enumerate(WIN3):
        v_xw.append(DMA("sync", s_x[wi],
                        lambda e, a=x_ar3[:, :, w0:w1], s=d_x3[:, :, w0:w1]:
                        e.dma_start(out=a, in_=s)))
    v_bwt = DMA("gpsimd", s_bwt, lambda e: e.dma_start(out=BWTB, in_=d_bwt))
    v_win = {}
    for t in range(2):
        v_win[t] = DMA("gpsimd", s_wri[t % 2],
                       lambda e, t=t: e.dma_start(out=WRI(t % 2), in_=d_win[t]))

    P("vector", lambda e: e.memset(DLY, 0.0))
    v_dly = M("vector")
    W("scalar", t_vec, v_dly)
    act_table("natural_log_exp_and_others")

    W("vector", v_consts)
    W("scalar", v_consts)
    P("scalar", lambda e: e.activation(out=SCR[:, 0:256], in_=SIGA, func=AF.Exp,
                                       scale=-1.0))
    v_expA = M("scalar")
    W("vector", t_scl, v_expA)
    P("vector", lambda e: e.tensor_scalar_add(out=SCR[:, 0:256],
                                              in0=SCR[:, 0:256], scalar1=1.0))
    CB("vector")
    P("vector", lambda e: e.reciprocal(out=SIGA, in_=SCR[:, 0:256]))
    CB("vector")
    P("vector", lambda e: e.tensor_mul(out=SCR[:, 0:256], in0=SIGA, in1=SIGA))
    CB("vector")
    P("vector", lambda e: e.tensor_mul(out=SCR[:, 256:512], in0=SCR[:, 0:256],
                                       in1=SCR[:, 0:256]))
    CB("vector")
    P("vector", lambda e: e.tensor_mul(out=SIG8, in0=SCR[:, 256:512],
                                       in1=SCR[:, 256:512]))
    CB("vector")
    s8r4 = S8R.rearrange("p (j o s) -> p j o s", j=16, o=8)
    sig8_3 = SIG8.rearrange("p (j s) -> p j s", j=16)
    for o in range(NO):
        P("vector", lambda e, o=o: e.tensor_copy(out=s8r4[:, :, o, :], in_=sig8_3))
    CB("vector")
    pkf4 = PKF.rearrange("p (j o s) -> p j o s", j=16, o=8)
    siga3 = SIGA.rearrange("p (j s) -> p j s", j=16)
    P("vector", lambda e: e.memset(pkf4[:, :, 0, :], 1.0))
    CB("vector")
    for o in range(1, NO):
        P("vector", lambda e, o=o: e.tensor_tensor(
            out=pkf4[:, :, o, :], in0=pkf4[:, :, o - 1, :], in1=siga3,
            op=ALU.mult))
        CB("vector")
    v_pkf0 = M("vector")
    pk5 = PKT.rearrange("p (j g c) -> p j g c", j=DT, g=NG)
    pkf3 = PKF.rearrange("p (j c) -> p j c", j=16)
    W("scalar", t_vec, v_pkf0)
    for g in range(NG):
        P("scalar", lambda e, g=g: e.copy(out=pk5[:, :, g, :], in_=pkf3))
        if g < NG - 1:
            vv = M("scalar")
            W("vector", t_scl, vv)
            P("vector", lambda e: e.tensor_mul(out=PKF, in0=PKF, in1=S8R))
            CB("vector")
            vv2 = M("vector")
            W("scalar", t_vec, vv2)
    M("scalar")

    ln_state = {"mb_busy": (t_vec, 0)}

    def emit_ln(windows, src_fn, src_is_f32, outs, apply_fn, wait_src):
        for wi, (c0, c1) in enumerate(windows):
            wl = c1 - c0
            pe_acquire(banks[6])
            pe_acquire(banks[7])
            pe_j = {-2: t_pe.n, -1: t_pe.n}
            if wait_src is not None:
                ws = wait_src if isinstance(wait_src, list) else [wait_src]
                for pair in ws:
                    W("scalar", pair)
            for j in range(DT):
                W("scalar", t_pe, pe_j[j - 2])
                if src_is_f32:
                    sl = (j % 2) * 2
                    P("scalar", lambda e, j=j, sl=sl, c0=c0, c1=c1:
                      e.copy(out=SQT(sl)[:, 0:c1 - c0], in_=src_fn(j, c0, c1)))
                    P("scalar", lambda e, j=j, sl=sl, c0=c0, c1=c1:
                      e.activation(out=SQT(sl + 1)[:, 0:c1 - c0],
                                   in_=src_fn(j, c0, c1), func=AF.Square))
                    rhs_sum, rhs_sq = SQT(sl)[:, 0:wl], SQT(sl + 1)[:, 0:wl]
                else:
                    sl = j % 2
                    P("scalar", lambda e, j=j, sl=sl, c0=c0, c1=c1:
                      e.activation(out=SQT(sl)[:, 0:c1 - c0],
                                   in_=src_fn(j, c0, c1), func=AF.Square))
                    rhs_sum, rhs_sq = src_fn(j, c0, c1), SQT(sl)[:, 0:wl]
                vs = M("scalar")
                W("tensor", t_scl, vs)
                mm(pb[6][0:1, 0:wl], ONESB, rhs_sum, j == 0, j == DT - 1)
                pe_j[j] = mm(pb[7][0:1, 0:wl], ONESB, rhs_sq,
                             j == 0, j == DT - 1, inc=True)
            banks[6].w_pe = banks[7].w_pe = t_pe.n
            W("vector", t_pe, t_pe.n)
            P("vector", lambda e, c0=c0, c1=c1, wl=wl: e.tensor_copy(
                out=SRA1[0:1, c0:c1], in_=pb[6][0:1, 0:wl]))
            P("vector", lambda e, c0=c0, c1=c1, wl=wl: e.tensor_copy(
                out=SRB1[0:1, c0:c1], in_=pb[7][0:1, 0:wl]))
            vv = M("vector")
            banks[6].r_trk = banks[7].r_trk = (t_vec, vv)
        v_rows = t_vec.n
        emit_ln_chain(windows, outs, apply_fn, v_rows, SRA1, SRB1)

    def emit_ln_chain(windows, outs, apply_fn, v_rows, ra=None, rb=None,
                      ba=6, bb=7):
        ra = SRA if ra is None else ra
        rb = SRB if rb is None else rb
        ood = OODB
        for wi, (c0, c1) in enumerate(windows):
            wl = c1 - c0
            mbf, mbb, invb = outs[wi]
            W("tensor", t_vec, v_rows)
            pe_acquire(banks[ba])
            mm(pb[ba][:, 0:wl], ood, ra[0:1, c0:c1], True, True, inc=True)
            banks[ba].w_pe = t_pe.n
            W("scalar", t_pe, banks[ba].w_pe)
            W("scalar", ln_state["mb_busy"][0], ln_state["mb_busy"][1])
            CB("scalar")
            P("scalar", lambda e, wl=wl, ba=ba: e.copy(out=CHS[:, 0:wl],
                                                       in_=pb[ba][:, 0:wl]))
            if mbf is not None:
                P("scalar", lambda e, d=mbf, wl=wl, ba=ba: e.copy(
                    out=d[:, 0:wl], in_=pb[ba][:, 0:wl]))
            if mbb is not None:
                P("scalar", lambda e, d=mbb, wl=wl, ba=ba: e.copy(
                    out=d[:, 0:wl], in_=pb[ba][:, 0:wl]))
            v_m = M("scalar")
            banks[ba].r_trk = (t_scl, v_m)
            pe_acquire(banks[bb])
            mm(pb[bb][:, 0:wl], ood, rb[0:1, c0:c1], True, True, inc=True)
            banks[bb].w_pe = t_pe.n
            W("vector", t_scl, v_m)
            P("vector", lambda e, wl=wl: e.tensor_mul(
                out=CH2[:, 0:wl], in0=CHS[:, 0:wl], in1=CHS[:, 0:wl]))
            CB("vector")
            W("vector", t_pe, banks[bb].w_pe)
            P("vector", lambda e, wl=wl, bb=bb: e.tensor_sub(
                out=CHS[:, 0:wl], in0=pb[bb][:, 0:wl], in1=CH2[:, 0:wl]))
            CB("vector")
            P("vector", lambda e, wl=wl: e.tensor_scalar_add(
                out=CHS[:, 0:wl], in0=CHS[:, 0:wl], scalar1=EPS))
            vv2 = M("vector")
            banks[bb].r_trk = (t_vec, vv2)
            W("scalar", t_vec, vv2)
            P("scalar", lambda e, wl=wl: e.activation(
                out=CHS[:, 0:wl], in_=CHS[:, 0:wl], func=AF.Ln))
            CB("scalar")
            P("scalar", lambda e, d=invb, wl=wl: e.activation(
                out=d[:, 0:wl], in_=CHS[:, 0:wl], func=AF.Exp, scale=-0.5))
            v_chain = M("scalar")
            if apply_fn is not None:
                apply_fn(wi, c0, c1, v_chain)

    def ln1_apply(wi, c0, c1, v_chain):
        wl = c1 - c0
        W("vector", t_scl, v_chain)
        for j in range(DT):
            P("vector", lambda e, j=j, c0=c0, c1=c1, wl=wl: e.tensor_sub(
                out=SQT(3)[:, 0:wl], in0=Xj(j, c0, c1), in1=MB1F[:, 0:wl]))
            CB("vector")
            P("vector", lambda e, j=j, c0=c0, c1=c1, wl=wl: e.tensor_mul(
                out=XN1j(j, c0, c1), in0=SQT(3)[:, 0:wl], in1=INV1B[:, 0:wl]))
            CB("vector")
        ln_state["mb_busy"] = (t_vec, M("vector"))

    emit_ln(WIN3, lambda j, c0, c1: Xj(j, c0, c1), True,
            [(MB1F, None, INV1B)] * 3, ln1_apply, v_xw)
    v_xn1_done = ln_state["mb_busy"][1]
    tap("xn1b", XN1T)

    BAR()
    W("tensor", t_vec, v_xn1_done)
    for t in range(DT):
        W("tensor", v_win[t])
        for wi, (w0, w1) in enumerate(WIN3):
            wl = w1 - w0
            bk = banks[(t * 3 + wi) % 4]
            pe_acquire(bk)
            for j in range(DT):
                mm(pb[bk.idx][:, 0:wl], WRI(t % 2)[:, j * 128:(j + 1) * 128],
                   XN1j(j, w0, w1), j == 0, j == DT - 1, inc=(j == DT - 1))
            bk.w_pe = t_pe.n
            W("scalar", t_pe, bk.w_pe)
            P("scalar", lambda e, t=t, w0=w0, w1=w1, wl=wl, b=bk.idx:
              e.activation(out=XPBj(t, w0, w1), in_=pb[b][:, 0:wl],
                           func=AF.Identity, bias=BINC[:, t:t + 1]))
            vv = M("scalar")
            bk.r_trk = (t_scl, vv)
        if t + 2 < DT:
            W("gpsimd", t_pe, t_pe.n)
            v_win[t + 2] = DMA("gpsimd", s_wri[t % 2],
                               lambda e, t=t: e.dma_start(out=WRI(t % 2),
                                                          in_=d_win[t + 2]))
    v_xpb_all = t_scl.n
    tap("x_projb", XPBT)

    BAR()
    W("tensor", v_bwt)
    W("tensor", t_scl, v_xpb_all)
    for wi, (w0, w1) in enumerate(WIN3):
        wl = w1 - w0
        bk = banks[4 + (wi % 2)]
        pe_acquire(bk)
        for j in range(DT):
            mm(pb[bk.idx][0:16, 0:wl], BWTB[:, j * 16:(j + 1) * 16],
               XPBj(j, w0, w1), j == 0, j == DT - 1, inc=(j == DT - 1))
        bk.w_pe = t_pe.n
        W("scalar", t_pe, bk.w_pe)
        P("scalar", lambda e, b=bk.idx, wl=wl: e.activation(
            out=SCR[0:16, 0:wl], in_=pb[b][0:16, 0:wl], func=AF.Exp))
        CB("scalar")
        P("scalar", lambda e, wl=wl: e.activation(
            out=SCR[0:16, 512:512 + wl], in_=SCR[0:16, 0:wl], func=AF.Ln,
            bias=1.0))
        CB("scalar")
        P("scalar", lambda e, w0=w0, w1=w1, wl=wl: e.copy(
            out=DB[:, w0:w1], in_=SCR[0:16, 512:512 + wl]))
        vv = M("scalar")
        bk.r_trk = (t_scl, vv)
    W("vector", t_scl, t_scl.n)
    P("vector", lambda e: e.tensor_mul(out=DB, in0=DB, in1=MASK))
    v_dbm = M("vector")
    W("tensor", t_vec, v_dbm)
    for wi, (w0, w1) in enumerate(WIN3):
        wl = w1 - w0
        bk = banks[4 + (wi % 2)]
        pe_acquire(bk)
        mm(pb[bk.idx][:, 0:wl], REP16, DB[:, w0:w1], True, True, inc=True)
        bk.w_pe = t_pe.n
        W("scalar", t_pe, bk.w_pe)
        P("scalar", lambda e, b=bk.idx, w0=w0, w1=w1, wl=wl: e.copy(
            out=DR[:, w0:w1], in_=pb[b][:, 0:wl]))
        vv = M("scalar")
        bk.r_trk = (t_scl, vv)
    v_dr = t_scl.n
    tap("deltab", DB)

    BAR()
    W("tensor", t_scl, v_dr)
    v_stageB = 0
    for k in range(2):
        gbase = 512 * k
        pieces = [(gbase, gbase + 512), (gbase + 512, min(gbase + 576, T))]
        if k == 1:
            W("tensor", t_vec, v_stageB)
        for g in range(NG):
            for pi, (p0, p1) in enumerate(pieces):
                pl = p1 - p0
                bk = banks[(g * 2 + pi) % 4]
                pe_acquire(bk)
                for j in range(DT):
                    mm(pb[bk.idx][:, 0:pl], PACK(j, g), XPBj(j, p0, p1),
                       j == 0, j == DT - 1, inc=(j == DT - 1))
                bk.w_pe = t_pe.n
                W("vector", t_pe, bk.w_pe)
                P("vector", lambda e, g=g, p0=p0, p1=p1, pl=pl, b=bk.idx,
                  gb=gbase: e.tensor_tensor(
                      out=GH(g, p0 - gb, p1 - gb), in0=pb[b][:, 0:pl],
                      in1=DR[:, p0:p1], op=ALU.mult))
                vv = M("vector")
                bk.r_trk = (t_vec, vv)
        CB("vector")
        P("vector", lambda e: e.tensor_add(
            out=ACC1[:, 0:519], in0=GH(0, 57, 576), in1=GH(1, 49, 568)))
        CB("vector")
        for g in range(2, NG):
            c0 = 57 - 8 * g
            P("vector", lambda e, c0=c0, g=g: e.tensor_add(
                out=ACC1[:, 0:519], in0=ACC1[:, 0:519], in1=GH(g, c0, c0 + 519)))
            CB("vector")
        v_stageA = M("vector")
        bkS = banks[4 + k % 2]
        W("tensor", t_vec, v_stageA)
        pe_acquire(bkS)
        for o in range(NO):
            mm(pb[bkS.idx][0:16, 0:512], IDENTF[:, 16 * o:16 * o + 16],
               ACC1[:, 7 - o:519 - o], o == 0, o == NO - 1, inc=(o == NO - 1))
        bkS.w_pe = t_pe.n
        W("vector", t_pe, bkS.w_pe)
        P("vector", lambda e, k=k, b=bkS.idx: e.tensor_copy(
            out=SHB[:, 512 * k:512 * (k + 1)], in_=pb[b][0:16, 0:512]))
        v_stageB = M("vector")
        bkS.r_trk = (t_vec, v_stageB)
    CB("vector")
    v_shb = t_vec.n
    tap("S_h", SHB)

    BAR()
    W("gpsimd", t_pe, t_pe.n)
    v_cwt = DMA("gpsimd", s_cwt, lambda e: e.dma_start(out=CWTB, in_=d_cwt))
    W("tensor", v_cwt)
    W("tensor", t_vec, v_shb)
    v_dscr_read = 0
    for t in range(DT):
        for wi in range(2):
            l0 = wi * 512
            bk = banks[(t * 2 + wi) % 4]
            pe_acquire(bk)
            mm(pb[bk.idx][:, 0:512], CWTB[:, t * 128:(t + 1) * 128],
               SHB[:, l0:l0 + 512], True, True, inc=True)
            bk.w_pe = t_pe.n
            W("scalar", t_vec, v_dscr_read)
            P("scalar", lambda e, t=t, l0=l0: e.activation(
                out=SCR[:, 0:512], in_=XPBj(t, PAD + l0, PAD + l0 + 512),
                func=AF.Copy, scale=DCOL[:, t:t + 1]))
            vsc = M("scalar")
            W("vector", t_pe, bk.w_pe)
            W("vector", t_scl, vsc)
            P("vector", lambda e, t=t, l0=l0, b=bk.idx: e.tensor_add(
                out=YPRE(t, l0, l0 + 512), in0=pb[b][:, 0:512],
                in1=SCR[:, 0:512]))
            vv = M("vector")
            bk.r_trk = (t_vec, vv)
            v_dscr_read = vv
    v_ypre = t_vec.n
    tap("ypre0", YPRET[0])

    BAR()
    WL2 = [(0, 512), (512, 1024)]
    v_wout = {}
    W("gpsimd", t_pe, t_pe.n)
    for t in range(2):
        v_wout[t] = DMA("gpsimd", s_wro[t % 2],
                        lambda e, t=t: e.dma_start(out=WRO(t % 2),
                                                   in_=d_wout[t]))
    W("tensor", t_vec, v_ypre)
    for b_ in (4, 5, 6, 7):
        pe_acquire(banks[b_])
    v_yatt_prev = {0: 0, 1: 0}
    sq_prev = {0: 0, 1: 0}
    for t in range(DT):
        W("tensor", v_wout[t])
        for wi in range(2):
            l0 = wi * 512
            bk = banks[(t * 2 + wi) % 4]
            pe_acquire(bk)
            for j in range(DT):
                mm(pb[bk.idx][:, 0:512], WRO(t % 2)[:, j * 128:(j + 1) * 128],
                   YPRE(j, l0, l0 + 512), j == 0, j == DT - 1)
            bk.w_pe = t_pe.n
            sl = (t * 2 + wi) % 2
            W("scalar", t_pe, bk.w_pe)
            W("scalar", t_vec, v_yatt_prev[sl])
            P("scalar", lambda e, t=t, b=bk.idx, sl=sl: e.activation(
                out=SQT(sl)[:, 0:512], in_=pb[b][:, 0:512], func=AF.Identity,
                bias=BOUTC[:, t:t + 1]))
            vsc = M("scalar")
            bk.r_trk = (t_scl, vsc)
            W("vector", t_scl, vsc)
            P("vector", lambda e, t=t, l0=l0, sl=sl: e.tensor_add(
                out=YATT(t, l0, l0 + 512), in0=SQT(sl)[:, 0:512],
                in1=XN1j(t, PAD + l0, PAD + l0 + 512)))
            vv = M("vector")
            v_yatt_prev[sl] = vv
            W("scalar", t_vec, vv)
            W("scalar", t_pe, sq_prev[sl])
            P("scalar", lambda e, t=t, l0=l0, sl=sl: e.activation(
                out=SQT(2 + sl)[:, 0:512], in_=YATT(t, l0, l0 + 512),
                func=AF.Square))
            vsq = M("scalar")
            W("tensor", t_vec, vv)
            bsum = 6 if wi == 0 else 7
            bsq = 4 if wi == 0 else 5
            mm(pb[bsum][0:1, 0:512], ONESB, YATT(t, l0, l0 + 512),
               t == 0, t == DT - 1)
            W("tensor", t_scl, vsq)
            sq_prev[sl] = mm(pb[bsq][0:1, 0:512], ONESB, SQT(2 + sl)[:, 0:512],
                             t == 0, t == DT - 1)
        if t + 2 < DT:
            W("gpsimd", t_pe, t_pe.n)
            v_wout[t + 2] = DMA("gpsimd", s_wro[t % 2],
                                lambda e, t=t: e.dma_start(out=WRO(t % 2),
                                                           in_=d_wout[t + 2]))
    for b_ in (4, 5, 6, 7):
        banks[b_].w_pe = t_pe.n
    W("vector", t_pe, t_pe.n)
    P("vector", lambda e: e.tensor_copy(out=SRA[0:1, 0:512], in_=pb[6][0:1, 0:512]))
    P("vector", lambda e: e.tensor_copy(out=SRB[0:1, 0:512], in_=pb[4][0:1, 0:512]))
    P("vector", lambda e: e.tensor_copy(out=SRA[0:1, 512:1024], in_=pb[7][0:1, 0:512]))
    P("vector", lambda e: e.tensor_copy(out=SRB[0:1, 512:1024], in_=pb[5][0:1, 0:512]))
    v_rows1 = M("vector")
    for b_ in (4, 5, 6, 7):
        banks[b_].r_trk = (t_vec, v_rows1)
    tap("yatt", XPBT)

    ln2_rows_done = []

    def outer1_apply(wi, c0, c1, v_chain):
        wl = c1 - c0
        W("vector", t_scl, v_chain)
        pe_acquire(banks[4])
        pe_acquire(banks[5])
        pe_j = {-1: t_pe.n}
        for j in range(DT):
            P("vector", lambda e, j=j, c0=c0, wl=wl: e.tensor_sub(
                out=SQT(3)[:, 0:wl], in0=YATT(j, c0, c0 + wl),
                in1=MB_B[:, 0:wl]))
            CB("vector")
            P("vector", lambda e, j=j, c0=c0, wl=wl: e.tensor_mul(
                out=CH2[:, 0:wl], in0=SQT(3)[:, 0:wl], in1=INVB[:, 0:wl]))
            CB("vector")
            P("vector", lambda e, j=j, c0=c0, wl=wl: e.tensor_add(
                out=Xj(j, PAD + c0, PAD + c0 + wl),
                in0=Xj(j, PAD + c0, PAD + c0 + wl), in1=CH2[:, 0:wl]))
            vxj = M("vector")
            CB("vector")
            W("scalar", t_vec, vxj)
            W("scalar", t_pe, pe_j[j - 1])
            P("scalar", lambda e, j=j, c0=c0, wl=wl: e.copy(
                out=SQT(0)[:, 0:wl], in_=Xj(j, PAD + c0, PAD + c0 + wl)))
            P("scalar", lambda e, j=j, c0=c0, wl=wl: e.activation(
                out=SQT(1)[:, 0:wl], in_=Xj(j, PAD + c0, PAD + c0 + wl),
                func=AF.Square))
            vs2 = M("scalar")
            W("tensor", t_scl, vs2)
            mm(pb[4][0:1, 0:wl], ONESB, SQT(0)[:, 0:wl], j == 0, j == DT - 1)
            pe_j[j] = mm(pb[5][0:1, 0:wl], ONESB, SQT(1)[:, 0:wl],
                         j == 0, j == DT - 1)
        banks[4].w_pe = banks[5].w_pe = t_pe.n
        ln_state["mb_busy"] = (t_vec, t_vec.n)
        W("vector", t_pe, t_pe.n)
        P("vector", lambda e, c0=c0, c1=c1, wl=wl: e.tensor_copy(
            out=SRA[0:1, c0:c1], in_=pb[4][0:1, 0:wl]))
        P("vector", lambda e, c0=c0, c1=c1, wl=wl: e.tensor_copy(
            out=SRB[0:1, c0:c1], in_=pb[5][0:1, 0:wl]))
        vr = M("vector")
        banks[4].r_trk = banks[5].r_trk = (t_vec, vr)
        ln2_rows_done.append(vr)

    emit_ln_chain(WL2, [(None, MB_B, INVB)] * 2, outer1_apply, v_rows1)
    tap("x2", XT)

    emit_ln_chain(WL2, [(MB_F, None, INVB), (MB_F2, None, INVB2)], None,
                  ln2_rows_done[-1])
    v_ln2bc = t_scl.n

    act_table("silu_and_others")

    d_out3 = d_out.rearrange("p (j c) -> p j c", j=DT)
    v_out_parts = []
    v_rows_f = {}
    v_final_apply = {}

    def final_apply(wi, c0, c1, v_chain):
        wl = c1 - c0
        W("vector", t_scl, v_chain)
        for j in range(DT):
            P("vector", lambda e, j=j, wi=wi, wl=wl: e.tensor_sub(
                out=SQT(2)[:, 0:wl], in0=Y2W(wi, j), in1=MB_B[:, 0:wl]))
            CB("vector")
            P("vector", lambda e, j=j, wl=wl: e.tensor_mul(
                out=CH2[:, 0:wl], in0=SQT(2)[:, 0:wl], in1=INVB[:, 0:wl]))
            CB("vector")
            P("vector", lambda e, j=j, c0=c0, wl=wl: e.tensor_add(
                out=Xj(j, PAD + c0, PAD + c0 + wl),
                in0=Xj(j, PAD + c0, PAD + c0 + wl), in1=CH2[:, 0:wl]))
            CB("vector")
        v_apply = M("vector")
        v_final_apply[wi] = v_apply
        ln_state["mb_busy"] = (t_vec, v_apply)
        W("sync", t_vec, v_apply)
        v_out_parts.append(DMA("sync", s_out, lambda e, c0=c0, c1=c1:
                               e.dma_start(out=d_out3[:, :, c0:c1],
                                           in_=x_ar3[:, :, PAD + c0:PAD + c1])))

    def emit_final(wi, v_rows, ba=6, bb=7):
        c0 = 512 * wi
        emit_ln_chain([(c0, c0 + 512)], [(None, MB_B, INVB)],
                      lambda _w, a, b, v: final_apply(wi, a, b, v), v_rows,
                      ba=ba, bb=bb)

    v_wgu = {}
    v_wd = {}

    def WGUPW(w, slot):
        t = WGUA if w == 0 else WGUB
        return t[:, slot * 4096:(slot + 1) * 4096]

    def WRDW(w, slot):
        t = WRDAT if w == 0 else WRDBT
        return t[:, slot * 2048:(slot + 1) * 2048]

    for k in range(2):
        l0 = 512 * k
        mbf = MB_F if k == 0 else MB_F2
        invb = INVB if k == 0 else INVB2
        W("vector", t_scl, v_ln2bc)
        v_x2n_j = []
        for j in range(DT):
            P("vector", lambda e, j=j, l0=l0, mbf=mbf: e.tensor_sub(
                out=SQT(3)[:, 0:512], in0=Xj(j, PAD + l0, PAD + l0 + 512),
                in1=mbf[:, 0:512]))
            CB("vector")
            P("vector", lambda e, j=j, k=k, invb=invb: e.tensor_mul(
                out=X2NW(k, j), in0=SQT(3)[:, 0:512], in1=invb[:, 0:512]))
            v_x2n_j.append(M("vector"))
            CB("vector")
        W("gpsimd", t_pe, t_pe.n)
        for i in range(2):
            v_wgu[i] = DMA("gpsimd", s_wrf[i % 2],
                           lambda e, i=i, k=k: e.dma_start(out=WGUPW(k, i % 2),
                                                           in_=d_wgu[i]))
        v_fuse_last = 0
        v_scr_gate = t_scl.n
        for i in range(IT):
            W("tensor", v_wgu[i])
            bg = banks[i % 2]
            bu = banks[2 + i % 2]
            pe_acquire(bg)
            pe_acquire(bu)
            for j in range(DT):
                if i == 0:
                    W("tensor", t_vec, v_x2n_j[j])
                mm(pb[bg.idx][:, 0:512],
                   WGUPW(k, i % 2)[:, j * 128:(j + 1) * 128],
                   X2NW(k, j), j == 0, j == DT - 1)
            bg.w_pe = t_pe.n
            for j in range(DT):
                mm(pb[bu.idx][:, 0:512],
                   WGUPW(k, i % 2)[:, 2048 + j * 128:2048 + (j + 1) * 128],
                   X2NW(k, j), j == 0, j == DT - 1)
            bu.w_pe = t_pe.n
            W("scalar", t_pe, bg.w_pe)
            if i == 0:
                W("scalar", t_scl, v_scr_gate)
            if i >= 1:
                W("scalar", t_vec, v_fuse_last)
            P("scalar", lambda e, i=i, b=bg.idx: e.activation(
                out=GS, in_=pb[b][:, 0:512], func=AF.Silu,
                scale=SGC[:, i:i + 1]))
            vsc = M("scalar")
            bg.r_trk = (t_scl, vsc)
            W("vector", t_scl, vsc)
            W("vector", t_pe, bu.w_pe)
            P("vector", lambda e, i=i, k=k, b=bu.idx: e.scalar_tensor_tensor(
                out=FUSE(k, i), in0=pb[b][:, 0:512], scalar=SUC[:, i:i + 1],
                in1=GS, op0=ALU.mult, op1=ALU.mult))
            v_fuse_last = M("vector")
            bu.r_trk = (t_vec, v_fuse_last)
            if i + 2 < IT:
                W("gpsimd", t_pe, t_pe.n)
                v_wgu[i + 2] = DMA("gpsimd", s_wrf[i % 2],
                                   lambda e, i=i, k=k: e.dma_start(
                                       out=WGUPW(k, i % 2), in_=d_wgu[i + 2]))
            if i == 48 and k == 0:
                W("gpsimd", t_pe, t_pe.n)
                for (tt, qq) in [(0, 0), (0, 1)]:
                    v_wd[(tt, qq)] = DMA(
                        "gpsimd", s_wrd[qq % 2],
                        lambda e, t=tt, q=qq, k=k: e.dma_start(
                            out=WRDW(k, q % 2),
                            in_=d_wd[t, q // 2, :, (q % 2) * 2048:
                                     (q % 2 + 1) * 2048]))
        if k == 1:
            W("gpsimd", t_pe, t_pe.n)
            W("gpsimd", t_vec, v_fuse_last)
            for (tt, qq) in [(0, 0), (0, 1)]:
                v_wd[(tt, qq)] = DMA(
                    "gpsimd", s_wrd[qq % 2],
                    lambda e, t=tt, q=qq, k=k: e.dma_start(
                        out=WRDW(k, q % 2),
                        in_=d_wd[t, q // 2, :, (q % 2) * 2048:
                                 (q % 2 + 1) * 2048]))
        W("tensor", t_vec, v_fuse_last)
        pe_acquire(banks[6 if k == 0 else 7])
        pe_acquire(banks[0 if k == 0 else 1])
        if k == 1:
            act_table("natural_log_exp_and_others")
            ln_state["mb_busy"] = (t_vec, v_fuse_last)
            emit_final(0, v_fuse_last, ba=2, bb=3)
        sqd_prev = {0: 0, 1: 0}
        for t in range(DT):
            bk = banks[4 + t % 2]
            pe_acquire(bk)
            for q in range(4):
                W("tensor", v_wd[(t, q)])
                for kk in range(16):
                    ki = q * 16 + kk
                    mm(pb[bk.idx][:, 0:512],
                       WRDW(k, q % 2)[:, kk * 128:(kk + 1) * 128],
                       FUSE(k, ki), ki == 0, ki == IT - 1)
                pre = (t + (q + 2) // 4, (q + 2) % 4)
                if pre[0] < DT and pre not in v_wd:
                    W("gpsimd", t_pe, t_pe.n)
                    v_wd[pre] = DMA(
                        "gpsimd", s_wrd[pre[1] % 2],
                        lambda e, p=pre, k=k: e.dma_start(
                            out=WRDW(k, p[1] % 2),
                            in_=d_wd[p[0], p[1] // 2, :, (p[1] % 2) * 2048:
                                     (p[1] % 2 + 1) * 2048]))
            bk.w_pe = t_pe.n
            sl = t % 2
            W("scalar", t_pe, bk.w_pe)
            W("scalar", t_pe, sqd_prev[sl])
            P("scalar", lambda e, t=t, k=k, b=bk.idx: e.activation(
                out=Y2W(k, t), in_=pb[b][:, 0:512], func=AF.Identity,
                scale=SDC[:, t:t + 1]))
            CB("scalar")
            P("scalar", lambda e, t=t, k=k, sl=sl: e.activation(
                out=SQT(sl)[:, 0:512], in_=Y2W(k, t), func=AF.Square))
            vsc = M("scalar")
            bk.r_trk = (t_scl, vsc)
            W("tensor", t_scl, vsc)
            mm(pb[6 if k == 0 else 7][0:1, 0:512], ONESB, Y2W(k, t),
               t == 0, t == DT - 1)
            sqd_prev[sl] = mm(pb[0 if k == 0 else 1][0:1, 0:512], ONESB,
                              SQT(sl)[:, 0:512], t == 0, t == DT - 1)
        banks[6 if k == 0 else 7].w_pe = t_pe.n
        banks[0 if k == 0 else 1].w_pe = t_pe.n
        W("vector", t_pe, t_pe.n)
        P("vector", lambda e, k=k, l0=l0: e.tensor_copy(
            out=SRA[0:1, l0:l0 + 512], in_=pb[6 if k == 0 else 7][0:1, 0:512]))
        P("vector", lambda e, k=k, l0=l0: e.tensor_copy(
            out=SRB[0:1, l0:l0 + 512], in_=pb[0 if k == 0 else 1][0:1, 0:512]))
        vrf = M("vector")
        banks[6 if k == 0 else 7].r_trk = (t_vec, vrf)
        banks[0 if k == 0 else 1].r_trk = (t_vec, vrf)
        v_wd = {}
        v_rows_f[k] = t_vec.n
        if k == 0:
            tap("y2b", Y2AT)
        else:
            emit_final(1, v_rows_f[1])
    v_y2_all = t_scl.n

    W("gpsimd", s_out, s_out.n)
    if s_tap.n:
        W("gpsimd", s_tap, s_tap.n)

    with nc.Block() as block:
        @block.sync
        def _(e):
            for f in plan["sync"]:
                f(e)

        @block.gpsimd
        def _(e):
            for f in plan["gpsimd"]:
                f(e)

        @block.scalar
        def _(e):
            for f in plan["scalar"]:
                f(e)

        @block.vector
        def _(e):
            for f in plan["vector"]:
                f(e)

        @block.tensor
        def _(e):
            for f in plan["tensor"]:
                f(e)

    return nc



def _prep_shared(inp):
    f32 = np.float32
    bf = ml_dtypes.bfloat16

    def tile_kblocks(w, nblk):
        K = w.shape[0]
        out = np.empty((nblk, 128, K), w.dtype)
        for t in range(nblk):
            blk = w[:, t * 128:(t + 1) * 128]
            out[t] = blk.reshape(K // 128, 128, 128).transpose(1, 0, 2).reshape(
                128, K)
        return np.ascontiguousarray(out)

    w_in = np.asarray(inp["L0_attn_proj_in_w"], f32)
    w_out = np.asarray(inp["L0_attn_proj_out_w"], f32)
    sh = {}
    sh["w_in_t"] = tile_kblocks(np.ascontiguousarray(w_in.T), DT)
    sh["w_out_t"] = tile_kblocks(np.ascontiguousarray(w_out.T), DT)
    bwt = np.ascontiguousarray(np.asarray(inp["L0_attn_B_w"], f32).T)
    sh["b_w_t"] = np.ascontiguousarray(
        bwt.reshape(16, 128, 16).transpose(1, 0, 2).reshape(128, 256))
    sh["c_w_t"] = np.ascontiguousarray(np.asarray(inp["L0_attn_C_w"], f32).T)
    a = np.asarray(inp["L0_attn_A"], f32)
    sh["a_t"] = np.ascontiguousarray(
        a.reshape(16, 128, 16).transpose(1, 0, 2).reshape(128, 256))
    for nm, key in [("dcol", "L0_attn_D_param"), ("bincol", "L0_attn_proj_in_b"),
                    ("boutcol", "L0_attn_proj_out_b"), ("sdcol", "L0_ffn_s_down")]:
        sh[nm] = np.ascontiguousarray(np.asarray(inp[key], f32).reshape(16, 128).T)
    for nm, key in [("sgcol", "L0_ffn_s_gate"), ("sucol", "L0_ffn_s_up")]:
        sh[nm] = np.ascontiguousarray(np.asarray(inp[key], f32).reshape(64, 128).T)
    wg_t = tile_kblocks(np.asarray(inp["L0_ffn_w_gate"]), IT)
    wu_t = tile_kblocks(np.asarray(inp["L0_ffn_w_up"]), IT)
    sh["wgu_t"] = np.ascontiguousarray(
        np.concatenate([wg_t, wu_t], axis=2))
    wd = np.asarray(inp["L0_ffn_w_down"])
    wd_t = np.empty((DT, 2, 128, 4096), np.int8)
    for t in range(DT):
        for hf in range(2):
            blk = wd[hf * 4096:(hf + 1) * 4096, t * 128:(t + 1) * 128]
            wd_t[t, hf] = blk.reshape(32, 128, 128).transpose(1, 0, 2).reshape(
                128, 4096)
    sh["wd_t"] = np.ascontiguousarray(wd_t)
    rep = np.zeros((16, 128), bf)
    for c in range(128):
        rep[c % 16, c] = 1.0
    sh["rep16b"] = rep
    sh["ones_b"] = np.ones((128, 1), bf)
    sh["ones_f"] = np.ones((1, 128), f32)
    sh["ood_f"] = np.full((1, 128), 1.0 / D, f32)
    sh["ood_b"] = np.full((1, 128), 1.0 / D, bf)
    sh["ident_f"] = np.eye(128, dtype=f32)
    return sh


def _prep_core(x, b, h):
    f32 = np.float32
    bf = ml_dtypes.bfloat16
    xt = np.ascontiguousarray(x[b].T.astype(f32))
    if h == 0:
        sl = np.concatenate([np.zeros((D, PAD), f32), xt[:, 0:TOUT]], axis=1)
    else:
        sl = xt[:, TOUT - PAD:2 * TOUT]
    xT = np.ascontiguousarray(
        sl.reshape(16, 128, T).transpose(1, 0, 2).reshape(128, DT * T))
    mask = np.ones((16, T), bf)
    if h == 0:
        mask[:, 0:PAD] = 0
    return {"xT": xT, "maskb": mask}


def kernel(**inputs):
    x = np.asarray(inputs["L0_attn_x"], np.float32)
    sh = _prep_shared(inputs)
    in_maps = []
    for c in range(8):
        b, h = divmod(c, 2)
        m = dict(sh)
        m.update(_prep_core(x, b, h))
        in_maps.append(m)
    nc = build_program()
    res = run_bass_kernel_spmd(nc, in_maps, core_ids=list(range(8)))
    out = np.empty((4, 2048, 2048), np.float32)
    for c in range(8):
        b, h = divmod(c, 2)
        o = np.asarray(res.results[c]["out_xT"])
        ot = o.reshape(128, DT, TOUT).transpose(1, 0, 2).reshape(D, TOUT)
        out[b, h * TOUT:(h + 1) * TOUT, :] = ot.T
    return out


# version 05
# speedup vs baseline: 1.2601x; 1.2601x over previous
import contextlib
import numpy as np
import ml_dtypes
import concourse.bass as bass
import concourse.mybir as mybir
from concourse.bass_utils import run_bass_kernel_spmd

F32 = mybir.dt.float32
BF16 = mybir.dt.bfloat16
I8 = mybir.dt.int8
U8 = mybir.dt.uint8
AF = mybir.ActivationFunctionType
ALU = mybir.AluOpType

D = 2048
DT = 16
S = 16
TOUT = 1024
PAD = 64
T = PAD + TOUT
NO = 8
NG = 6
INTER = 8192
IT = 64
WIN3 = [(0, 512), (512, 1024), (1024, T)]
EPS = 1e-5
SBUF_BYTES = 212832

ENGINES = ("sync", "gpsimd", "scalar", "vector", "tensor")


class Trk:
    def __init__(self, name, step=1):
        self.name, self.step, self.n, self.h = name, step, 0, None


class Alloc:
    def __init__(self, nbytes):
        self.cap = nbytes
        self.items = []

    def pin(self, off, size, t0, t1, name):
        size = (size + 63) & ~63
        for (o, s, a0, a1, n) in self.items:
            if not (a1 <= t0 or t1 <= a0) and not (o + s <= off or off + size <= o):
                raise MemoryError(f"pin clash: {name} vs {n}")
        self.items.append((off, size, t0, t1, name))
        return off

    def alloc(self, size, t0, t1, name):
        size = (size + 63) & ~63
        spans = sorted((o, o + s) for (o, s, a0, a1, _n) in self.items
                       if not (a1 <= t0 or t1 <= a0))
        holes, cur = [], 0
        for (s0, s1) in spans:
            if s0 > cur:
                holes.append((cur, s0))
            cur = max(cur, s1)
        if cur < self.cap:
            holes.append((cur, self.cap))
        cand = [h for h in holes if h[1] - h[0] >= size]
        if not cand:
            raise MemoryError(f"SBUF overflow: {name} {size}B t[{t0},{t1}]")
        off = min(cand, key=lambda h: (h[1] - h[0], h[0]))[0]
        self.items.append((off, size, t0, t1, name))
        return off


def build_program(taps=()):
    nc = bass.Bass("TRN2", target_bir_lowering=False, debug=False)

    d_x = nc.dram_tensor("xT", [128, DT * T], F32, kind="ExternalInput").ap()
    d_win = nc.dram_tensor("w_in_t", [DT, 128, 2048], F32, kind="ExternalInput").ap()
    d_wout = nc.dram_tensor("w_out_t", [DT, 128, 2048], F32, kind="ExternalInput").ap()
    d_bwt = nc.dram_tensor("b_w_t", [128, 256], F32, kind="ExternalInput").ap()
    d_cwt = nc.dram_tensor("c_w_t", [16, 2048], F32, kind="ExternalInput").ap()
    d_a = nc.dram_tensor("a_t", [128, 256], F32, kind="ExternalInput").ap()
    d_dcol = nc.dram_tensor("dcol", [128, 16], F32, kind="ExternalInput").ap()
    d_bin = nc.dram_tensor("bincol", [128, 16], F32, kind="ExternalInput").ap()
    d_bout = nc.dram_tensor("boutcol", [128, 16], F32, kind="ExternalInput").ap()
    d_sd = nc.dram_tensor("sdcol", [128, 16], F32, kind="ExternalInput").ap()
    d_sg = nc.dram_tensor("sgcol", [128, 64], F32, kind="ExternalInput").ap()
    d_su = nc.dram_tensor("sucol", [128, 64], F32, kind="ExternalInput").ap()
    d_wgu = nc.dram_tensor("wgu_t", [IT, 128, 4096], I8, kind="ExternalInput").ap()
    d_wd = nc.dram_tensor("wd_t", [DT, 2, 128, 4096], I8, kind="ExternalInput").ap()
    d_mask = nc.dram_tensor("maskb", [16, T], BF16, kind="ExternalInput").ap()
    d_rep16 = nc.dram_tensor("rep16b", [16, 128], BF16, kind="ExternalInput").ap()
    d_ones_b = nc.dram_tensor("ones_b", [128, 1], BF16, kind="ExternalInput").ap()
    d_ones_f = nc.dram_tensor("ones_f", [1, 128], F32, kind="ExternalInput").ap()
    d_oodf = nc.dram_tensor("ood_f", [1, 128], F32, kind="ExternalInput").ap()
    d_oodb = nc.dram_tensor("ood_b", [1, 128], BF16, kind="ExternalInput").ap()
    d_identf = nc.dram_tensor("ident_f", [128, 128], F32, kind="ExternalInput").ap()
    d_out = nc.dram_tensor("out_xT", [128, DT * TOUT], F32, kind="ExternalOutput").ap()
    tap_out = {}
    for tname, tshape, tdt in taps:
        tap_out[tname] = nc.dram_tensor("tap_" + tname, tshape, tdt,
                                        kind="ExternalOutput").ap()

    al = Alloc(SBUF_BYTES)
    o_x = al.pin(0, DT * T * 4, 0, 12, "x")
    o_scr = al.pin(192512, 1024 * 4, 0, 12, "scratch")
    o_sq = al.pin(196608, 4 * 512 * 2, 0, 12, "sqscr")
    o_srown = al.pin(200704, 2 * 1024 * 2, 4, 12, "sumrows")
    o_bcb = al.pin(204800, 2 * 1024, 5, 12, "bcastB")
    o_onesf = al.pin(206848, 128 * 4 + 256, 0, 12, "onesf+oodb")
    o_const = al.pin(207616, 2 * 64 + 2 * 256 + 64, 0, 12, "consts")
    o_fuseA = al.pin(69632, IT * 512 * 2, 6, 8, "fuseA")
    o_wri = al.pin(163840, 2 * 4096, 0, 2, "wring_in")
    o_srow = al.pin(172032, 2 * T * 2, 0, 2, "sumrows_wide")
    o_wro = al.pin(172032, 2 * 4096, 5, 6, "wring_out")
    o_cwt = al.pin(172032, 2048 * 2, 4, 5, "cwtb")
    o_ypre0 = al.pin(139264, DT * TOUT * 2, 4, 6, "ypre")
    o_ypre = [o_ypre0 + t * 2048 for t in range(DT)]
    o_xn1 = al.pin(69632, DT * T * 2, 0, 6, "xn1b")
    o_xpb = al.pin(104448, DT * T * 2, 1, 6, "x_projb/y_attnb")
    o_pkf = al.pin(104448, 2 * 8192, 0, 1, "pkf")
    o_pk = al.pin(139264, DT * NG * 128 * 2, 0, 4, "packs")
    o_fuseB = al.pin(69632, IT * 512 * 2, 8, 11, "fuseB")
    o_xy1 = al.pin(135168, DT * 512 * 2, 6, 10, "x2na+y2a")
    o_x2na = o_xy1
    o_y2a = o_xy1
    o_xy2 = al.pin(151552, DT * 512 * 2, 8, 12, "x2nb+y2b")
    o_x2nb = o_xy2
    o_y2b = o_xy2
    o_wguA = al.pin(151552, 2 * 8192, 6, 8, "wguA")
    o_wguB = al.pin(167936, 8192, 8, 11, "wguB_s0")
    al.pin(176128, 8192, 7, 11, "wguB_s1")
    o_wrdA = al.pin(167936, 2 * 4096, 6, 8, "wrdA")
    o_wrdB = al.pin(184320, 2 * 4096, 9, 12, "wrdB")
    o_gh = al.pin(163840, NG * 576 * 4, 3, 4, "ghat")
    o_dr = al.pin(177664, T * 4, 2, 4, "deltar")
    o_acc1 = al.pin(182016, 520 * 4, 3, 4, "acc1")
    o_shb = al.pin(184128, TOUT * 2, 3, 5, "S_hb")
    o_bca = al.pin(176384, 2048 + 1024, 0, 2, "bcastA")
    o_mask = al.pin(182016, T * 2, 0, 3, "maskb")
    o_db = al.pin(163840, T * 2, 2, 3, "deltab")
    o_sig = al.pin(120832, 2 * 1024, 0, 1, "sig")
    o_mbf1 = al.pin(180224, 2048, 5, 7, "mbf_w0")
    o_mbf2 = al.pin(184320, 2048, 5, 9, "mbf_w1")
    o_bcb2 = al.pin(186368, 1024, 5, 9, "bcastB2")
    o_bwt = al.pin(184192, 256 * 2, 0, 3, "bwtb")
    o_rep16 = al.pin(184704, 256, 0, 3, "rep16")
    o_dcol = al.pin(186688, 64, 0, 5, "dcol")
    o_binc = al.pin(179456, 64, 0, 2, "binc")
    o_identf = al.pin(186176, 128 * 4, 0, 4, "identf")
    import os as _os
    if _os.environ.get("KDEBUG_ALLOC"):
        for (o, s, a0, a1, n) in sorted(al.items):
            print(f"{o:7d} +{s:6d} t[{a0:2d},{a1:2d}] {n}")
        raise SystemExit(0)

    plan = {e: [] for e in ENGINES}

    def Praw(eng, fn):
        plan[eng].append(fn)

    t_pe, t_vec, t_scl = Trk("t_pe"), Trk("t_vec"), Trk("t_scl")
    s_const = Trk("s_const", 16)
    s_x = [Trk(f"s_x{i}", 16) for i in range(3)]
    s_bwt, s_cwt = Trk("s_bwt", 16), Trk("s_cwt", 16)
    s_wri = [Trk(f"s_wri{i}", 16) for i in range(2)]
    s_wro = [Trk(f"s_wro{i}", 16) for i in range(2)]
    s_wrf = [Trk(f"s_wrf{i}", 16) for i in range(4)]
    s_wrd = [Trk(f"s_wrd{i}", 16) for i in range(2)]
    s_out, s_tap = Trk("s_out", 16), Trk("s_tap", 16)
    DMAT = ([s_const] + s_x + [s_bwt, s_cwt] + s_wri + s_wro + s_wrf + s_wrd
            + [s_out, s_tap])
    TRKS = [t_pe, t_vec, t_scl] + DMAT
    ETRK = {"tensor": t_pe, "vector": t_vec, "scalar": t_scl}

    def DMA(queue, trk, build):
        trk.n += trk.step
        plan[queue].append(lambda e, b=build, t=trk: b(e).then_inc(t.h, t.step))
        return (trk, trk.n)

    def W(eng, trk, val=None):
        if val is None:
            trk, val = trk
        if val and val > 0:
            Praw(eng, lambda e, t=trk, v=val: e.wait_ge(t.h, v))

    def P(eng, fn):
        trk = ETRK[eng]
        trk.n += trk.step
        plan[eng].append(lambda e, b=fn, t=trk: b(e).then_inc(t.h, t.step))
        return trk.n

    def E(eng, build, inc=False):
        if inc:
            return P(eng, build)
        Praw(eng, build)
        return None

    def M(eng):
        return ETRK[eng].n

    def CB(eng):
        trk = ETRK[eng]
        Praw(eng, lambda e, t=trk, v=trk.n: e.wait_ge(t.h, v))

    def BAR():
        for eng in ("tensor", "vector", "scalar"):
            for trk in (t_pe, t_vec, t_scl):
                if trk.n > 0:
                    Praw(eng, lambda e, t=trk, v=trk.n: e.wait_ge(t.h, v))

    class Bank:
        def __init__(self, idx):
            self.idx, self.w_pe, self.r_trk = idx, 0, None

    banks = [Bank(i) for i in range(8)]

    def pe_acquire(bk):
        if bk.r_trk is not None:
            W("tensor", bk.r_trk[0], bk.r_trk[1])

    def mm(out_ap, lhsT, rhs, start, stop, inc=True):
        b = (lambda e, o=out_ap, l=lhsT, r=rhs, st=start, sp=stop:
             e.matmul(o, l, r, start=st, stop=sp))
        t_pe.n += 1
        plan["tensor"].append(lambda e, bb=b: bb(e).then_inc(t_pe.h, 1))
        return t_pe.n

    def act_table(set_name):
        from concourse.hw_specs import get_activation_tables
        sid = list(get_activation_tables(nc.m.arch).keys()).index(set_name)

        Praw("scalar", lambda e, sid=sid: e.add_instruction(
            mybir.InstLoadActFuncSet(
                name=nc.get_next_instruction_name(), act_func_set_id=sid,
                ins=[], outs=[])))
        for _ in range(12):
            P("scalar", lambda e: e.copy(out=DLY, in_=DLY))
            CB("scalar")

    es = contextlib.ExitStack()
    nc._kernel_es = es
    ar = es.enter_context(nc.sbuf_tensor("arena", [128, SBUF_BYTES], U8))
    pb = [es.enter_context(nc.psum_tensor(f"pb{i}", [128, 512], F32))
          for i in range(8)]
    for trk in TRKS:
        trk.h = es.enter_context(nc.semaphore(trk.name))

    arena_base = None
    for a in nc.m.functions[0].allocations:
        mls = getattr(a, "memorylocations", None)
        if not mls:
            continue
        if mls[0].name == "arena":
            arena_base = mls[0].addr
    assert arena_base is not None

    def REG(name, off, part, cols, dtype):
        return nc.alloc_sbuf_tensor_at(name, [part, cols], dtype,
                                       offset=arena_base + off).ap()

    XT = REG("XT", o_x, 128, DT * T, F32)
    XN1T = REG("XN1T", o_xn1, 128, DT * T, BF16)
    XPBT = REG("XPBT", o_xpb, 128, DT * T, BF16)
    PKT = REG("PKT", o_pk, 128, DT * NG * 128, BF16)
    GHT = REG("GHT", o_gh, 128, NG * 576, F32)
    DR = REG("DRT", o_dr, 128, T, F32)
    DB = REG("DBT", o_db, 16, T, BF16)
    MASK = REG("MASKT", o_mask, 16, T, BF16)
    ACC1 = REG("ACC1T", o_acc1, 128, 520, F32)
    SHB = REG("SHBT", o_shb, 16, TOUT, BF16)
    SCRT = REG("SCRT", o_scr, 128, 1024, F32)
    SCR = SCRT
    SRT = REG("SRT", o_srow, 1, 2 * T, BF16)
    SRA1 = SRT[:, 0:T]
    SRB1 = SRT[:, T:2 * T]
    SRNT = REG("SRNT", o_srown, 1, 2048, BF16)
    SRA = SRNT[:, 0:1024]
    SRB = SRNT[:, 1024:2048]
    CHS = SCRT[:, 0:512]
    CH2 = SCRT[:, 512:1024]
    GS = SCRT[:, 0:512]
    SQTT = REG("SQTT", o_sq, 128, 2048, BF16)

    def SQT(i):
        return SQTT[:, i * 512:(i + 1) * 512]

    DLY = SQTT[:, 0:128]

    MB_F = REG("MBF1", o_mbf1, 128, 512, F32)
    MB_F2 = REG("MBF2", o_mbf2, 128, 512, F32)
    INVBT = REG("INVBT", o_bcb, 128, 1024, BF16)
    INVB = INVBT[:, 0:512]
    MB_B = INVBT[:, 512:1024]
    INVB2 = REG("INVB2", o_bcb2, 128, 512, BF16)
    MB1F = REG("MB1F", o_bca, 128, 512, F32)
    INV1B = REG("INV1B", o_bca + 2048, 128, 512, BF16)
    WRIT = REG("WRIT", o_wri, 128, 4096, BF16)
    WROT = REG("WROT", o_wro, 128, 4096, BF16)
    WGUA = REG("WGUA", o_wguA, 128, 8192, BF16)
    WGUB = REG("WGUB", o_wguB, 128, 8192, BF16)

    def WRI(i):
        return WRIT[:, i * 2048:(i + 1) * 2048]

    def WRO(i):
        return WROT[:, i * 2048:(i + 1) * 2048]

    BWTB = REG("BWTT", o_bwt, 128, 256, BF16)
    CWTB = REG("CWTT", o_cwt, 16, 2048, BF16)
    oc = o_const
    DCOL = REG("DCOL", o_dcol, 128, 16, F32)
    BINC = REG("BINC", o_binc, 128, 16, F32)
    BOUTC = REG("BOUTC", oc, 128, 16, F32); oc += 64
    SDC = REG("SDC", oc, 128, 16, F32); oc += 64
    SGC = REG("SGC", oc, 128, 64, F32); oc += 256
    SUC = REG("SUC", oc, 128, 64, F32); oc += 256
    REP16 = REG("REP16", o_rep16, 16, 128, BF16)
    ONESB = REG("ONESB", oc, 128, 1, BF16); oc += 64
    ONESF = REG("ONESF", o_onesf, 1, 128, F32)
    OODB = REG("OODB", o_onesf + 512, 1, 128, BF16)
    IDENTF = REG("IDENTF", o_identf, 128, 128, F32)
    SIGA = REG("SIGA", o_sig, 128, 256, F32)
    SIG8 = REG("SIG8", o_sig + 1024, 128, 256, F32)
    PKF = REG("PKF", o_pkf, 128, 2048, F32)
    S8R = REG("S8R", o_pkf + 8192, 128, 2048, F32)
    YPRET = [REG(f"YPRE{t}", o_ypre[t], 128, TOUT, BF16) for t in range(DT)]
    FUSETA = REG("FUSETA", o_fuseA, 128, IT * 512, BF16)
    FUSETB = REG("FUSETB", o_fuseB, 128, IT * 512, BF16)
    X2NAT = REG("X2NAT", o_x2na, 128, DT * 512, BF16)
    WRDAT = REG("WRDAT", o_wrdA, 128, 4096, BF16)
    WRDBT = REG("WRDBT", o_wrdB, 128, 4096, BF16)
    X2NBT = REG("X2NBT", o_x2nb, 128, DT * 512, BF16)
    Y2AT = REG("Y2AT", o_y2a, 128, DT * 512, BF16)
    Y2BT = REG("Y2BT", o_y2b, 128, DT * 512, BF16)

    def Xj(j, c0, c1):
        return XT[:, j * T + c0:j * T + c1]

    def XN1j(j, c0, c1):
        return XN1T[:, j * T + c0:j * T + c1]

    def XPBj(j, c0, c1):
        return XPBT[:, j * T + c0:j * T + c1]

    def YATT(j, l0, l1):
        return XPBT[:, j * T + l0:j * T + l1]

    def PACK(j, g):
        return PKT[:, (j * NG + g) * 128:(j * NG + g + 1) * 128]

    def GH(g, c0, c1):
        return GHT[:, g * 576 + c0:g * 576 + c1]

    def YPRE(t, l0, l1):
        return YPRET[t][:, l0:l1]

    def FUSE(w, i):
        t = FUSETA if w == 0 else FUSETB
        return t[:, i * 512:(i + 1) * 512]

    def X2NW(k, j):
        t = X2NAT if k == 0 else X2NBT
        return t[:, j * 512:(j + 1) * 512]

    def Y2W(k, j):
        t = Y2AT if k == 0 else Y2BT
        return t[:, j * 512:(j + 1) * 512]

    def tap(name, view):
        if name not in tap_out:
            return
        W("sync", t_vec, t_vec.n)
        W("sync", t_scl, t_scl.n)
        W("sync", t_pe, t_pe.n)
        DMA("sync", s_tap, lambda e, d=tap_out[name], v=view:
            e.dma_start(out=d, in_=v))
        for eng in ("vector", "scalar", "tensor", "gpsimd"):
            W(eng, s_tap, s_tap.n)

    for (dst, src) in [(SIGA, d_a), (DCOL, d_dcol), (BINC, d_bin),
                       (BOUTC, d_bout), (SDC, d_sd), (SGC, d_sg), (SUC, d_su),
                       (MASK, d_mask), (REP16, d_rep16), (ONESB, d_ones_b),
                       (ONESF, d_ones_f), (OODB, d_oodb),
                       (IDENTF, d_identf)]:
        DMA("sync", s_const, lambda e, d=dst, s=src: e.dma_start(out=d, in_=s))
    v_consts = (s_const, s_const.n)
    d_x3 = d_x.rearrange("p (j c) -> p j c", j=DT)
    x_ar3 = XT.rearrange("p (j c) -> p j c", j=DT)
    v_xw = []
    for wi, (w0, w1) in enumerate(WIN3):
        v_xw.append(DMA("sync", s_x[wi],
                        lambda e, a=x_ar3[:, :, w0:w1], s=d_x3[:, :, w0:w1]:
                        e.dma_start(out=a, in_=s)))
    v_bwt = DMA("gpsimd", s_bwt, lambda e: e.dma_start(out=BWTB, in_=d_bwt))
    v_win = {}
    for t in range(2):
        v_win[t] = DMA("gpsimd", s_wri[t % 2],
                       lambda e, t=t: e.dma_start(out=WRI(t % 2), in_=d_win[t]))

    P("vector", lambda e: e.memset(DLY, 0.0))
    v_dly = M("vector")
    W("scalar", t_vec, v_dly)
    act_table("natural_log_exp_and_others")

    W("vector", v_consts)
    W("scalar", v_consts)
    P("scalar", lambda e: e.activation(out=SCR[:, 0:256], in_=SIGA, func=AF.Exp,
                                       scale=-1.0))
    v_expA = M("scalar")
    W("vector", t_scl, v_expA)
    P("vector", lambda e: e.tensor_scalar_add(out=SCR[:, 0:256],
                                              in0=SCR[:, 0:256], scalar1=1.0))
    CB("vector")
    P("vector", lambda e: e.reciprocal(out=SIGA, in_=SCR[:, 0:256]))
    CB("vector")
    P("vector", lambda e: e.tensor_mul(out=SCR[:, 0:256], in0=SIGA, in1=SIGA))
    CB("vector")
    P("vector", lambda e: e.tensor_mul(out=SCR[:, 256:512], in0=SCR[:, 0:256],
                                       in1=SCR[:, 0:256]))
    CB("vector")
    P("vector", lambda e: e.tensor_mul(out=SIG8, in0=SCR[:, 256:512],
                                       in1=SCR[:, 256:512]))
    CB("vector")
    s8r4 = S8R.rearrange("p (j o s) -> p j o s", j=16, o=8)
    sig8_3 = SIG8.rearrange("p (j s) -> p j s", j=16)
    for o in range(NO):
        P("vector", lambda e, o=o: e.tensor_copy(out=s8r4[:, :, o, :], in_=sig8_3))
    CB("vector")
    pkf4 = PKF.rearrange("p (j o s) -> p j o s", j=16, o=8)
    siga3 = SIGA.rearrange("p (j s) -> p j s", j=16)
    P("vector", lambda e: e.memset(pkf4[:, :, 0, :], 1.0))
    CB("vector")
    for o in range(1, NO):
        P("vector", lambda e, o=o: e.tensor_tensor(
            out=pkf4[:, :, o, :], in0=pkf4[:, :, o - 1, :], in1=siga3,
            op=ALU.mult))
        CB("vector")
    v_pkf0 = M("vector")
    pk5 = PKT.rearrange("p (j g c) -> p j g c", j=DT, g=NG)
    pkf3 = PKF.rearrange("p (j c) -> p j c", j=16)
    W("scalar", t_vec, v_pkf0)
    for g in range(NG):
        P("scalar", lambda e, g=g: e.copy(out=pk5[:, :, g, :], in_=pkf3))
        if g < NG - 1:
            vv = M("scalar")
            W("vector", t_scl, vv)
            P("vector", lambda e: e.tensor_mul(out=PKF, in0=PKF, in1=S8R))
            CB("vector")
            vv2 = M("vector")
            W("scalar", t_vec, vv2)
    M("scalar")

    ln_state = {"mb_busy": (t_vec, 0)}

    def emit_ln(windows, src_fn, src_is_f32, outs, apply_fn, wait_src):
        for wi, (c0, c1) in enumerate(windows):
            wl = c1 - c0
            pe_acquire(banks[6])
            pe_acquire(banks[7])
            pe_j = {-2: t_pe.n, -1: t_pe.n}
            if wait_src is not None:
                ws = wait_src if isinstance(wait_src, list) else [wait_src]
                for pair in ws:
                    W("scalar", pair)
            for j in range(DT):
                W("scalar", t_pe, pe_j[j - 2])
                if src_is_f32:
                    sl = (j % 2) * 2
                    P("scalar", lambda e, j=j, sl=sl, c0=c0, c1=c1:
                      e.copy(out=SQT(sl)[:, 0:c1 - c0], in_=src_fn(j, c0, c1)))
                    P("scalar", lambda e, j=j, sl=sl, c0=c0, c1=c1:
                      e.activation(out=SQT(sl + 1)[:, 0:c1 - c0],
                                   in_=src_fn(j, c0, c1), func=AF.Square))
                    rhs_sum, rhs_sq = SQT(sl)[:, 0:wl], SQT(sl + 1)[:, 0:wl]
                else:
                    sl = j % 2
                    P("scalar", lambda e, j=j, sl=sl, c0=c0, c1=c1:
                      e.activation(out=SQT(sl)[:, 0:c1 - c0],
                                   in_=src_fn(j, c0, c1), func=AF.Square))
                    rhs_sum, rhs_sq = src_fn(j, c0, c1), SQT(sl)[:, 0:wl]
                vs = M("scalar")
                W("tensor", t_scl, vs)
                mm(pb[6][0:1, 0:wl], ONESB, rhs_sum, j == 0, j == DT - 1)
                pe_j[j] = mm(pb[7][0:1, 0:wl], ONESB, rhs_sq,
                             j == 0, j == DT - 1, inc=True)
            banks[6].w_pe = banks[7].w_pe = t_pe.n
            W("vector", t_pe, t_pe.n)
            P("vector", lambda e, c0=c0, c1=c1, wl=wl: e.tensor_copy(
                out=SRA1[0:1, c0:c1], in_=pb[6][0:1, 0:wl]))
            P("vector", lambda e, c0=c0, c1=c1, wl=wl: e.tensor_copy(
                out=SRB1[0:1, c0:c1], in_=pb[7][0:1, 0:wl]))
            vv = M("vector")
            banks[6].r_trk = banks[7].r_trk = (t_vec, vv)
        v_rows = t_vec.n
        emit_ln_chain(windows, outs, apply_fn, v_rows, SRA1, SRB1)

    def emit_ln_chain(windows, outs, apply_fn, v_rows, ra=None, rb=None,
                      ba=6, bb=7):
        ra = SRA if ra is None else ra
        rb = SRB if rb is None else rb
        ood = OODB
        for wi, (c0, c1) in enumerate(windows):
            wl = c1 - c0
            mbf, mbb, invb = outs[wi]
            W("tensor", t_vec, v_rows)
            pe_acquire(banks[ba])
            mm(pb[ba][:, 0:wl], ood, ra[0:1, c0:c1], True, True, inc=True)
            banks[ba].w_pe = t_pe.n
            W("scalar", t_pe, banks[ba].w_pe)
            W("scalar", ln_state["mb_busy"][0], ln_state["mb_busy"][1])
            CB("scalar")
            P("scalar", lambda e, wl=wl, ba=ba: e.copy(out=CHS[:, 0:wl],
                                                       in_=pb[ba][:, 0:wl]))
            if mbf is not None:
                P("scalar", lambda e, d=mbf, wl=wl, ba=ba: e.copy(
                    out=d[:, 0:wl], in_=pb[ba][:, 0:wl]))
            if mbb is not None:
                P("scalar", lambda e, d=mbb, wl=wl, ba=ba: e.copy(
                    out=d[:, 0:wl], in_=pb[ba][:, 0:wl]))
            v_m = M("scalar")
            banks[ba].r_trk = (t_scl, v_m)
            pe_acquire(banks[bb])
            mm(pb[bb][:, 0:wl], ood, rb[0:1, c0:c1], True, True, inc=True)
            banks[bb].w_pe = t_pe.n
            W("vector", t_scl, v_m)
            P("vector", lambda e, wl=wl: e.tensor_mul(
                out=CH2[:, 0:wl], in0=CHS[:, 0:wl], in1=CHS[:, 0:wl]))
            CB("vector")
            W("vector", t_pe, banks[bb].w_pe)
            P("vector", lambda e, wl=wl, bb=bb: e.tensor_sub(
                out=CHS[:, 0:wl], in0=pb[bb][:, 0:wl], in1=CH2[:, 0:wl]))
            CB("vector")
            P("vector", lambda e, wl=wl: e.tensor_scalar_add(
                out=CHS[:, 0:wl], in0=CHS[:, 0:wl], scalar1=EPS))
            vv2 = M("vector")
            banks[bb].r_trk = (t_vec, vv2)
            W("scalar", t_vec, vv2)
            P("scalar", lambda e, wl=wl: e.activation(
                out=CHS[:, 0:wl], in_=CHS[:, 0:wl], func=AF.Ln))
            CB("scalar")
            P("scalar", lambda e, d=invb, wl=wl: e.activation(
                out=d[:, 0:wl], in_=CHS[:, 0:wl], func=AF.Exp, scale=-0.5))
            v_chain = M("scalar")
            if apply_fn is not None:
                apply_fn(wi, c0, c1, v_chain)

    def ln1_apply(wi, c0, c1, v_chain):
        wl = c1 - c0
        W("vector", t_scl, v_chain)
        for j in range(DT):
            P("vector", lambda e, j=j, c0=c0, c1=c1, wl=wl: e.tensor_sub(
                out=SQT(3)[:, 0:wl], in0=Xj(j, c0, c1), in1=MB1F[:, 0:wl]))
            CB("vector")
            P("vector", lambda e, j=j, c0=c0, c1=c1, wl=wl: e.tensor_mul(
                out=XN1j(j, c0, c1), in0=SQT(3)[:, 0:wl], in1=INV1B[:, 0:wl]))
            CB("vector")
        ln_state["mb_busy"] = (t_vec, M("vector"))

    emit_ln(WIN3, lambda j, c0, c1: Xj(j, c0, c1), True,
            [(MB1F, None, INV1B)] * 3, ln1_apply, v_xw)
    v_xn1_done = ln_state["mb_busy"][1]
    tap("xn1b", XN1T)

    BAR()
    W("tensor", t_vec, v_xn1_done)
    for t in range(DT):
        W("tensor", v_win[t])
        for wi, (w0, w1) in enumerate(WIN3):
            wl = w1 - w0
            bk = banks[(t * 3 + wi) % 4]
            pe_acquire(bk)
            for j in range(DT):
                mm(pb[bk.idx][:, 0:wl], WRI(t % 2)[:, j * 128:(j + 1) * 128],
                   XN1j(j, w0, w1), j == 0, j == DT - 1, inc=(j == DT - 1))
            bk.w_pe = t_pe.n
            W("scalar", t_pe, bk.w_pe)
            P("scalar", lambda e, t=t, w0=w0, w1=w1, wl=wl, b=bk.idx:
              e.activation(out=XPBj(t, w0, w1), in_=pb[b][:, 0:wl],
                           func=AF.Identity, bias=BINC[:, t:t + 1]))
            vv = M("scalar")
            bk.r_trk = (t_scl, vv)
        if t + 2 < DT:
            W("gpsimd", t_pe, t_pe.n)
            v_win[t + 2] = DMA("gpsimd", s_wri[t % 2],
                               lambda e, t=t: e.dma_start(out=WRI(t % 2),
                                                          in_=d_win[t + 2]))
    v_xpb_all = t_scl.n
    tap("x_projb", XPBT)

    BAR()
    W("tensor", v_bwt)
    W("tensor", t_scl, v_xpb_all)
    for wi, (w0, w1) in enumerate(WIN3):
        wl = w1 - w0
        bk = banks[4 + (wi % 2)]
        pe_acquire(bk)
        for j in range(DT):
            mm(pb[bk.idx][0:16, 0:wl], BWTB[:, j * 16:(j + 1) * 16],
               XPBj(j, w0, w1), j == 0, j == DT - 1, inc=(j == DT - 1))
        bk.w_pe = t_pe.n
        W("scalar", t_pe, bk.w_pe)
        P("scalar", lambda e, b=bk.idx, wl=wl: e.activation(
            out=SCR[0:16, 0:wl], in_=pb[b][0:16, 0:wl], func=AF.Exp))
        CB("scalar")
        P("scalar", lambda e, wl=wl: e.activation(
            out=SCR[0:16, 512:512 + wl], in_=SCR[0:16, 0:wl], func=AF.Ln,
            bias=1.0))
        CB("scalar")
        P("scalar", lambda e, w0=w0, w1=w1, wl=wl: e.copy(
            out=DB[:, w0:w1], in_=SCR[0:16, 512:512 + wl]))
        vv = M("scalar")
        bk.r_trk = (t_scl, vv)
    W("vector", t_scl, t_scl.n)
    P("vector", lambda e: e.tensor_mul(out=DB, in0=DB, in1=MASK))
    v_dbm = M("vector")
    W("tensor", t_vec, v_dbm)
    for wi, (w0, w1) in enumerate(WIN3):
        wl = w1 - w0
        bk = banks[4 + (wi % 2)]
        pe_acquire(bk)
        mm(pb[bk.idx][:, 0:wl], REP16, DB[:, w0:w1], True, True, inc=True)
        bk.w_pe = t_pe.n
        W("scalar", t_pe, bk.w_pe)
        P("scalar", lambda e, b=bk.idx, w0=w0, w1=w1, wl=wl: e.copy(
            out=DR[:, w0:w1], in_=pb[b][:, 0:wl]))
        vv = M("scalar")
        bk.r_trk = (t_scl, vv)
    v_dr = t_scl.n
    tap("deltab", DB)

    BAR()
    W("tensor", t_scl, v_dr)
    v_stageB = 0
    for k in range(2):
        gbase = 512 * k
        pieces = [(gbase, gbase + 512), (gbase + 512, min(gbase + 576, T))]
        if k == 1:
            W("tensor", t_vec, v_stageB)
        for g in range(NG):
            for pi, (p0, p1) in enumerate(pieces):
                pl = p1 - p0
                bk = banks[(g * 2 + pi) % 4]
                pe_acquire(bk)
                for j in range(DT):
                    mm(pb[bk.idx][:, 0:pl], PACK(j, g), XPBj(j, p0, p1),
                       j == 0, j == DT - 1, inc=(j == DT - 1))
                bk.w_pe = t_pe.n
                W("vector", t_pe, bk.w_pe)
                P("vector", lambda e, g=g, p0=p0, p1=p1, pl=pl, b=bk.idx,
                  gb=gbase: e.tensor_tensor(
                      out=GH(g, p0 - gb, p1 - gb), in0=pb[b][:, 0:pl],
                      in1=DR[:, p0:p1], op=ALU.mult))
                vv = M("vector")
                bk.r_trk = (t_vec, vv)
        CB("vector")
        P("vector", lambda e: e.tensor_add(
            out=ACC1[:, 0:519], in0=GH(0, 57, 576), in1=GH(1, 49, 568)))
        CB("vector")
        for g in range(2, NG):
            c0 = 57 - 8 * g
            P("vector", lambda e, c0=c0, g=g: e.tensor_add(
                out=ACC1[:, 0:519], in0=ACC1[:, 0:519], in1=GH(g, c0, c0 + 519)))
            CB("vector")
        v_stageA = M("vector")
        bkS = banks[4 + k % 2]
        W("tensor", t_vec, v_stageA)
        pe_acquire(bkS)
        for o in range(NO):
            mm(pb[bkS.idx][0:16, 0:512], IDENTF[:, 16 * o:16 * o + 16],
               ACC1[:, 7 - o:519 - o], o == 0, o == NO - 1, inc=(o == NO - 1))
        bkS.w_pe = t_pe.n
        W("vector", t_pe, bkS.w_pe)
        P("vector", lambda e, k=k, b=bkS.idx: e.tensor_copy(
            out=SHB[:, 512 * k:512 * (k + 1)], in_=pb[b][0:16, 0:512]))
        v_stageB = M("vector")
        bkS.r_trk = (t_vec, v_stageB)
    CB("vector")
    v_shb = t_vec.n
    tap("S_h", SHB)

    BAR()
    W("gpsimd", t_pe, t_pe.n)
    v_cwt = DMA("gpsimd", s_cwt, lambda e: e.dma_start(out=CWTB, in_=d_cwt))
    W("tensor", v_cwt)
    W("tensor", t_vec, v_shb)
    v_dscr_read = 0
    for t in range(DT):
        for wi in range(2):
            l0 = wi * 512
            bk = banks[(t * 2 + wi) % 4]
            pe_acquire(bk)
            mm(pb[bk.idx][:, 0:512], CWTB[:, t * 128:(t + 1) * 128],
               SHB[:, l0:l0 + 512], True, True, inc=True)
            bk.w_pe = t_pe.n
            W("scalar", t_vec, v_dscr_read)
            P("scalar", lambda e, t=t, l0=l0: e.activation(
                out=SCR[:, 0:512], in_=XPBj(t, PAD + l0, PAD + l0 + 512),
                func=AF.Copy, scale=DCOL[:, t:t + 1]))
            vsc = M("scalar")
            W("vector", t_pe, bk.w_pe)
            W("vector", t_scl, vsc)
            P("vector", lambda e, t=t, l0=l0, b=bk.idx: e.tensor_add(
                out=YPRE(t, l0, l0 + 512), in0=pb[b][:, 0:512],
                in1=SCR[:, 0:512]))
            vv = M("vector")
            bk.r_trk = (t_vec, vv)
            v_dscr_read = vv
    v_ypre = t_vec.n
    tap("ypre0", YPRET[0])

    BAR()
    WL2 = [(0, 512), (512, 1024)]
    v_wout = {}
    W("gpsimd", t_pe, t_pe.n)
    for t in range(2):
        v_wout[t] = DMA("gpsimd", s_wro[t % 2],
                        lambda e, t=t: e.dma_start(out=WRO(t % 2),
                                                   in_=d_wout[t]))
    W("tensor", t_vec, v_ypre)
    for b_ in (4, 5, 6, 7):
        pe_acquire(banks[b_])
    v_yatt_prev = {0: 0, 1: 0}
    sq_prev = {0: 0, 1: 0}
    pend_sum = None
    for t in range(DT):
        W("tensor", v_wout[t])
        for wi in range(2):
            l0 = wi * 512
            bk = banks[(t * 2 + wi) % 4]
            pe_acquire(bk)
            for j in range(DT):
                mm(pb[bk.idx][:, 0:512], WRO(t % 2)[:, j * 128:(j + 1) * 128],
                   YPRE(j, l0, l0 + 512), j == 0, j == DT - 1)
            bk.w_pe = t_pe.n
            if pend_sum is not None:
                pend_sum()
            sl = (t * 2 + wi) % 2
            W("scalar", t_pe, bk.w_pe)
            W("scalar", t_vec, v_yatt_prev[sl])
            P("scalar", lambda e, t=t, b=bk.idx, sl=sl: e.activation(
                out=SQT(sl)[:, 0:512], in_=pb[b][:, 0:512], func=AF.Identity,
                bias=BOUTC[:, t:t + 1]))
            vsc = M("scalar")
            bk.r_trk = (t_scl, vsc)
            W("vector", t_scl, vsc)
            P("vector", lambda e, t=t, l0=l0, sl=sl: e.tensor_add(
                out=YATT(t, l0, l0 + 512), in0=SQT(sl)[:, 0:512],
                in1=XN1j(t, PAD + l0, PAD + l0 + 512)))
            vv = M("vector")
            v_yatt_prev[sl] = vv
            W("scalar", t_vec, vv)
            W("scalar", t_pe, sq_prev[sl])
            P("scalar", lambda e, t=t, l0=l0, sl=sl: e.activation(
                out=SQT(2 + sl)[:, 0:512], in_=YATT(t, l0, l0 + 512),
                func=AF.Square))
            vsq = M("scalar")

            def pend_sum(t=t, wi=wi, l0=l0, sl=sl, vv=vv, vsq=vsq):
                W("tensor", t_vec, vv)
                mm(pb[6 if wi == 0 else 7][0:1, 0:512], ONESB,
                   YATT(t, l0, l0 + 512), t == 0, t == DT - 1)
                W("tensor", t_scl, vsq)
                sq_prev[sl] = mm(pb[4 if wi == 0 else 5][0:1, 0:512], ONESB,
                                 SQT(2 + sl)[:, 0:512], t == 0, t == DT - 1)
        if t + 2 < DT:
            W("gpsimd", t_pe, t_pe.n)
            v_wout[t + 2] = DMA("gpsimd", s_wro[t % 2],
                                lambda e, t=t: e.dma_start(out=WRO(t % 2),
                                                           in_=d_wout[t + 2]))
    pend_sum()
    for b_ in (4, 5, 6, 7):
        banks[b_].w_pe = t_pe.n
    v_wgu = {}
    W("gpsimd", t_pe, t_pe.n)
    for i in range(2):
        v_wgu[i] = DMA("gpsimd", s_wrf[i % 2],
                       lambda e, i=i: e.dma_start(out=WGUPW(0, i % 2),
                                                  in_=d_wgu[i]))
    W("vector", t_pe, t_pe.n)
    P("vector", lambda e: e.tensor_copy(out=SRA[0:1, 0:512], in_=pb[6][0:1, 0:512]))
    P("vector", lambda e: e.tensor_copy(out=SRB[0:1, 0:512], in_=pb[4][0:1, 0:512]))
    P("vector", lambda e: e.tensor_copy(out=SRA[0:1, 512:1024], in_=pb[7][0:1, 0:512]))
    P("vector", lambda e: e.tensor_copy(out=SRB[0:1, 512:1024], in_=pb[5][0:1, 0:512]))
    v_rows1 = M("vector")
    for b_ in (4, 5, 6, 7):
        banks[b_].r_trk = (t_vec, v_rows1)
    tap("yatt", XPBT)

    ln2_rows_done = []

    def outer1_apply(wi, c0, c1, v_chain):
        wl = c1 - c0
        W("vector", t_scl, v_chain)
        pe_acquire(banks[4])
        pe_acquire(banks[5])
        pe_j = {-1: t_pe.n}
        for j in range(DT):
            P("vector", lambda e, j=j, c0=c0, wl=wl: e.tensor_sub(
                out=SQT(3)[:, 0:wl], in0=YATT(j, c0, c0 + wl),
                in1=MB_B[:, 0:wl]))
            CB("vector")
            P("vector", lambda e, j=j, c0=c0, wl=wl: e.tensor_mul(
                out=CH2[:, 0:wl], in0=SQT(3)[:, 0:wl], in1=INVB[:, 0:wl]))
            CB("vector")
            P("vector", lambda e, j=j, c0=c0, wl=wl: e.tensor_add(
                out=Xj(j, PAD + c0, PAD + c0 + wl),
                in0=Xj(j, PAD + c0, PAD + c0 + wl), in1=CH2[:, 0:wl]))
            vxj = M("vector")
            CB("vector")
            W("scalar", t_vec, vxj)
            W("scalar", t_pe, pe_j[j - 1])
            P("scalar", lambda e, j=j, c0=c0, wl=wl: e.copy(
                out=SQT(0)[:, 0:wl], in_=Xj(j, PAD + c0, PAD + c0 + wl)))
            P("scalar", lambda e, j=j, c0=c0, wl=wl: e.activation(
                out=SQT(1)[:, 0:wl], in_=Xj(j, PAD + c0, PAD + c0 + wl),
                func=AF.Square))
            vs2 = M("scalar")
            W("tensor", t_scl, vs2)
            mm(pb[4][0:1, 0:wl], ONESB, SQT(0)[:, 0:wl], j == 0, j == DT - 1)
            pe_j[j] = mm(pb[5][0:1, 0:wl], ONESB, SQT(1)[:, 0:wl],
                         j == 0, j == DT - 1)
        banks[4].w_pe = banks[5].w_pe = t_pe.n
        ln_state["mb_busy"] = (t_vec, t_vec.n)
        W("vector", t_pe, t_pe.n)
        P("vector", lambda e, c0=c0, c1=c1, wl=wl: e.tensor_copy(
            out=SRA[0:1, c0:c1], in_=pb[4][0:1, 0:wl]))
        P("vector", lambda e, c0=c0, c1=c1, wl=wl: e.tensor_copy(
            out=SRB[0:1, c0:c1], in_=pb[5][0:1, 0:wl]))
        vr = M("vector")
        banks[4].r_trk = banks[5].r_trk = (t_vec, vr)
        ln2_rows_done.append(vr)

    emit_ln_chain(WL2, [(None, MB_B, INVB)] * 2, outer1_apply, v_rows1)
    tap("x2", XT)

    emit_ln_chain(WL2, [(MB_F, None, INVB), (MB_F2, None, INVB2)], None,
                  ln2_rows_done[-1])
    v_ln2bc = t_scl.n

    act_table("silu_and_others")

    d_out3 = d_out.rearrange("p (j c) -> p j c", j=DT)
    v_out_parts = []
    v_rows_f = {}
    v_final_apply = {}

    def final_apply(wi, c0, c1, v_chain):
        wl = c1 - c0
        W("vector", t_scl, v_chain)
        for j in range(DT):
            P("vector", lambda e, j=j, wi=wi, wl=wl: e.tensor_sub(
                out=SQT(2)[:, 0:wl], in0=Y2W(wi, j), in1=MB_B[:, 0:wl]))
            CB("vector")
            P("vector", lambda e, j=j, wl=wl: e.tensor_mul(
                out=CH2[:, 0:wl], in0=SQT(2)[:, 0:wl], in1=INVB[:, 0:wl]))
            CB("vector")
            P("vector", lambda e, j=j, c0=c0, wl=wl: e.tensor_add(
                out=Xj(j, PAD + c0, PAD + c0 + wl),
                in0=Xj(j, PAD + c0, PAD + c0 + wl), in1=CH2[:, 0:wl]))
            CB("vector")
        v_apply = M("vector")
        v_final_apply[wi] = v_apply
        ln_state["mb_busy"] = (t_vec, v_apply)
        W("sync", t_vec, v_apply)
        v_out_parts.append(DMA("sync", s_out, lambda e, c0=c0, c1=c1:
                               e.dma_start(out=d_out3[:, :, c0:c1],
                                           in_=x_ar3[:, :, PAD + c0:PAD + c1])))

    def emit_final(wi, v_rows, ba=6, bb=7):
        c0 = 512 * wi
        emit_ln_chain([(c0, c0 + 512)], [(None, MB_B, INVB)],
                      lambda _w, a, b, v: final_apply(wi, a, b, v), v_rows,
                      ba=ba, bb=bb)

    v_wd = {}

    def WGUPW(w, slot):
        t = WGUA if w == 0 else WGUB
        return t[:, slot * 4096:(slot + 1) * 4096]

    def WRDW(w, slot):
        t = WRDAT if w == 0 else WRDBT
        return t[:, slot * 2048:(slot + 1) * 2048]

    for k in range(2):
        l0 = 512 * k
        mbf = MB_F if k == 0 else MB_F2
        invb = INVB if k == 0 else INVB2
        W("vector", t_scl, v_ln2bc)
        v_x2n_j = []
        for j in range(DT):
            P("vector", lambda e, j=j, l0=l0, mbf=mbf: e.tensor_sub(
                out=SQT(3)[:, 0:512], in0=Xj(j, PAD + l0, PAD + l0 + 512),
                in1=mbf[:, 0:512]))
            CB("vector")
            P("vector", lambda e, j=j, k=k, invb=invb: e.tensor_mul(
                out=X2NW(k, j), in0=SQT(3)[:, 0:512], in1=invb[:, 0:512]))
            v_x2n_j.append(M("vector"))
            CB("vector")
        slg = (lambda i: i % 2) if k == 0 else (lambda i: (i + 1) % 2)
        if k == 0:
            pass
        else:
            v_wgu[0] = v_wgu_k1
            W("gpsimd", t_pe, t_pe.n)
            v_wgu[1] = DMA("gpsimd", s_wrf[0],
                           lambda e: e.dma_start(out=WGUPW(1, 0),
                                                 in_=d_wgu[1]))
        v_fuse_last = 0
        v_scr_gate = t_scl.n
        for i in range(IT):
            W("tensor", v_wgu[i])
            bg = banks[i % 2]
            bu = banks[2 + i % 2]
            pe_acquire(bg)
            pe_acquire(bu)
            for j in range(DT):
                if i == 0:
                    W("tensor", t_vec, v_x2n_j[j])
                mm(pb[bg.idx][:, 0:512],
                   WGUPW(k, slg(i))[:, j * 128:(j + 1) * 128],
                   X2NW(k, j), j == 0, j == DT - 1)
            bg.w_pe = t_pe.n
            for j in range(DT):
                mm(pb[bu.idx][:, 0:512],
                   WGUPW(k, slg(i))[:, 2048 + j * 128:2048 + (j + 1) * 128],
                   X2NW(k, j), j == 0, j == DT - 1)
            bu.w_pe = t_pe.n
            W("scalar", t_pe, bg.w_pe)
            if i == 0:
                W("scalar", t_scl, v_scr_gate)
            if i >= 1:
                W("scalar", t_vec, v_fuse_last)
            P("scalar", lambda e, i=i, b=bg.idx: e.activation(
                out=GS, in_=pb[b][:, 0:512], func=AF.Silu,
                scale=SGC[:, i:i + 1]))
            vsc = M("scalar")
            bg.r_trk = (t_scl, vsc)
            W("vector", t_scl, vsc)
            W("vector", t_pe, bu.w_pe)
            P("vector", lambda e, i=i, k=k, b=bu.idx: e.scalar_tensor_tensor(
                out=FUSE(k, i), in0=pb[b][:, 0:512], scalar=SUC[:, i:i + 1],
                in1=GS, op0=ALU.mult, op1=ALU.mult))
            v_fuse_last = M("vector")
            bu.r_trk = (t_vec, v_fuse_last)
            if i + 2 < IT:
                W("gpsimd", t_pe, t_pe.n)
                v_wgu[i + 2] = DMA("gpsimd", s_wrf[slg(i)],
                                   lambda e, i=i, k=k, s=slg(i): e.dma_start(
                                       out=WGUPW(k, s), in_=d_wgu[i + 2]))
            if i == 48 and k == 0:
                W("gpsimd", t_pe, t_pe.n)
                for (tt, qq) in [(0, 0), (0, 1)]:
                    v_wd[(tt, qq)] = DMA(
                        "gpsimd", s_wrd[qq % 2],
                        lambda e, t=tt, q=qq, k=k: e.dma_start(
                            out=WRDW(k, q % 2),
                            in_=d_wd[t, q // 2, :, (q % 2) * 2048:
                                     (q % 2 + 1) * 2048]))
        if k == 1:
            W("gpsimd", t_pe, t_pe.n)
            W("gpsimd", t_vec, v_fuse_last)
            for (tt, qq) in [(0, 0), (0, 1)]:
                v_wd[(tt, qq)] = DMA(
                    "gpsimd", s_wrd[qq % 2],
                    lambda e, t=tt, q=qq, k=k: e.dma_start(
                        out=WRDW(k, q % 2),
                        in_=d_wd[t, q // 2, :, (q % 2) * 2048:
                                 (q % 2 + 1) * 2048]))
        W("tensor", t_vec, v_fuse_last)
        pe_acquire(banks[6 if k == 0 else 7])
        pe_acquire(banks[0 if k == 0 else 1])
        if k == 0:
            W("gpsimd", t_pe, t_pe.n)
            v_wgu_k1 = DMA("gpsimd", s_wrf[1],
                           lambda e: e.dma_start(out=WGUPW(1, 1),
                                                 in_=d_wgu[0]))
        if k == 1:
            act_table("natural_log_exp_and_others")
            ln_state["mb_busy"] = (t_vec, v_fuse_last)
            emit_final(0, v_fuse_last, ba=2, bb=3)
        sqd_prev = {0: 0, 1: 0}
        pend_sum = None
        for t in range(DT):
            bk = banks[4 + t % 2]
            pe_acquire(bk)
            for q in range(4):
                W("tensor", v_wd[(t, q)])
                for kk in range(16):
                    ki = q * 16 + kk
                    mm(pb[bk.idx][:, 0:512],
                       WRDW(k, q % 2)[:, kk * 128:(kk + 1) * 128],
                       FUSE(k, ki), ki == 0, ki == IT - 1)
                pre = (t + (q + 2) // 4, (q + 2) % 4)
                if pre[0] < DT and pre not in v_wd:
                    W("gpsimd", t_pe, t_pe.n)
                    v_wd[pre] = DMA(
                        "gpsimd", s_wrd[pre[1] % 2],
                        lambda e, p=pre, k=k: e.dma_start(
                            out=WRDW(k, p[1] % 2),
                            in_=d_wd[p[0], p[1] // 2, :, (p[1] % 2) * 2048:
                                     (p[1] % 2 + 1) * 2048]))
            bk.w_pe = t_pe.n
            if pend_sum is not None:
                pend_sum()
            sl = t % 2
            W("scalar", t_pe, bk.w_pe)
            W("scalar", t_pe, sqd_prev[sl])
            P("scalar", lambda e, t=t, k=k, b=bk.idx: e.activation(
                out=Y2W(k, t), in_=pb[b][:, 0:512], func=AF.Identity,
                scale=SDC[:, t:t + 1]))
            CB("scalar")
            P("scalar", lambda e, t=t, k=k, sl=sl: e.activation(
                out=SQT(sl)[:, 0:512], in_=Y2W(k, t), func=AF.Square))
            vsc = M("scalar")
            bk.r_trk = (t_scl, vsc)

            def pend_sum(t=t, k=k, sl=sl, vsc=vsc):
                W("tensor", t_scl, vsc)
                mm(pb[6 if k == 0 else 7][0:1, 0:512], ONESB, Y2W(k, t),
                   t == 0, t == DT - 1)
                sqd_prev[sl] = mm(pb[0 if k == 0 else 1][0:1, 0:512], ONESB,
                                  SQT(sl)[:, 0:512], t == 0, t == DT - 1)
        pend_sum()
        banks[6 if k == 0 else 7].w_pe = t_pe.n
        banks[0 if k == 0 else 1].w_pe = t_pe.n
        W("vector", t_pe, t_pe.n)
        P("vector", lambda e, k=k, l0=l0: e.tensor_copy(
            out=SRA[0:1, l0:l0 + 512], in_=pb[6 if k == 0 else 7][0:1, 0:512]))
        P("vector", lambda e, k=k, l0=l0: e.tensor_copy(
            out=SRB[0:1, l0:l0 + 512], in_=pb[0 if k == 0 else 1][0:1, 0:512]))
        vrf = M("vector")
        banks[6 if k == 0 else 7].r_trk = (t_vec, vrf)
        banks[0 if k == 0 else 1].r_trk = (t_vec, vrf)
        v_wd = {}
        v_rows_f[k] = t_vec.n
        if k == 0:
            tap("y2b", Y2AT)
        else:
            emit_final(1, v_rows_f[1])
    v_y2_all = t_scl.n

    W("gpsimd", s_out, s_out.n)
    if s_tap.n:
        W("gpsimd", s_tap, s_tap.n)

    with nc.Block() as block:
        @block.sync
        def _(e):
            for f in plan["sync"]:
                f(e)

        @block.gpsimd
        def _(e):
            for f in plan["gpsimd"]:
                f(e)

        @block.scalar
        def _(e):
            for f in plan["scalar"]:
                f(e)

        @block.vector
        def _(e):
            for f in plan["vector"]:
                f(e)

        @block.tensor
        def _(e):
            for f in plan["tensor"]:
                f(e)

    return nc



def _prep_shared(inp):
    f32 = np.float32
    bf = ml_dtypes.bfloat16

    def tile_kblocks(w, nblk):
        K = w.shape[0]
        out = np.empty((nblk, 128, K), w.dtype)
        for t in range(nblk):
            blk = w[:, t * 128:(t + 1) * 128]
            out[t] = blk.reshape(K // 128, 128, 128).transpose(1, 0, 2).reshape(
                128, K)
        return np.ascontiguousarray(out)

    w_in = np.asarray(inp["L0_attn_proj_in_w"], f32)
    w_out = np.asarray(inp["L0_attn_proj_out_w"], f32)
    sh = {}
    sh["w_in_t"] = tile_kblocks(np.ascontiguousarray(w_in.T), DT)
    sh["w_out_t"] = tile_kblocks(np.ascontiguousarray(w_out.T), DT)
    bwt = np.ascontiguousarray(np.asarray(inp["L0_attn_B_w"], f32).T)
    sh["b_w_t"] = np.ascontiguousarray(
        bwt.reshape(16, 128, 16).transpose(1, 0, 2).reshape(128, 256))
    sh["c_w_t"] = np.ascontiguousarray(np.asarray(inp["L0_attn_C_w"], f32).T)
    a = np.asarray(inp["L0_attn_A"], f32)
    sh["a_t"] = np.ascontiguousarray(
        a.reshape(16, 128, 16).transpose(1, 0, 2).reshape(128, 256))
    for nm, key in [("dcol", "L0_attn_D_param"), ("bincol", "L0_attn_proj_in_b"),
                    ("boutcol", "L0_attn_proj_out_b"), ("sdcol", "L0_ffn_s_down")]:
        sh[nm] = np.ascontiguousarray(np.asarray(inp[key], f32).reshape(16, 128).T)
    for nm, key in [("sgcol", "L0_ffn_s_gate"), ("sucol", "L0_ffn_s_up")]:
        sh[nm] = np.ascontiguousarray(np.asarray(inp[key], f32).reshape(64, 128).T)
    wg_t = tile_kblocks(np.asarray(inp["L0_ffn_w_gate"]), IT)
    wu_t = tile_kblocks(np.asarray(inp["L0_ffn_w_up"]), IT)
    sh["wgu_t"] = np.ascontiguousarray(
        np.concatenate([wg_t, wu_t], axis=2))
    wd = np.asarray(inp["L0_ffn_w_down"])
    wd_t = np.empty((DT, 2, 128, 4096), np.int8)
    for t in range(DT):
        for hf in range(2):
            blk = wd[hf * 4096:(hf + 1) * 4096, t * 128:(t + 1) * 128]
            wd_t[t, hf] = blk.reshape(32, 128, 128).transpose(1, 0, 2).reshape(
                128, 4096)
    sh["wd_t"] = np.ascontiguousarray(wd_t)
    rep = np.zeros((16, 128), bf)
    for c in range(128):
        rep[c % 16, c] = 1.0
    sh["rep16b"] = rep
    sh["ones_b"] = np.ones((128, 1), bf)
    sh["ones_f"] = np.ones((1, 128), f32)
    sh["ood_f"] = np.full((1, 128), 1.0 / D, f32)
    sh["ood_b"] = np.full((1, 128), 1.0 / D, bf)
    sh["ident_f"] = np.eye(128, dtype=f32)
    return sh


def _prep_core(x, b, h):
    f32 = np.float32
    bf = ml_dtypes.bfloat16
    xt = np.ascontiguousarray(x[b].T.astype(f32))
    if h == 0:
        sl = np.concatenate([np.zeros((D, PAD), f32), xt[:, 0:TOUT]], axis=1)
    else:
        sl = xt[:, TOUT - PAD:2 * TOUT]
    xT = np.ascontiguousarray(
        sl.reshape(16, 128, T).transpose(1, 0, 2).reshape(128, DT * T))
    mask = np.ones((16, T), bf)
    if h == 0:
        mask[:, 0:PAD] = 0
    return {"xT": xT, "maskb": mask}


def kernel(**inputs):
    x = np.asarray(inputs["L0_attn_x"], np.float32)
    sh = _prep_shared(inputs)
    in_maps = []
    for c in range(8):
        b, h = divmod(c, 2)
        m = dict(sh)
        m.update(_prep_core(x, b, h))
        in_maps.append(m)
    nc = build_program()
    res = run_bass_kernel_spmd(nc, in_maps, core_ids=list(range(8)))
    out = np.empty((4, 2048, 2048), np.float32)
    for c in range(8):
        b, h = divmod(c, 2)
        o = np.asarray(res.results[c]["out_xT"])
        ot = o.reshape(128, DT, TOUT).transpose(1, 0, 2).reshape(D, TOUT)
        out[b, h * TOUT:(h + 1) * TOUT, :] = ot.T
    return out


# version 06
# speedup vs baseline: 1.2826x; 1.0179x over previous
import contextlib
import numpy as np
import ml_dtypes
import concourse.bass as bass
import concourse.mybir as mybir
from concourse.bass_utils import run_bass_kernel_spmd

F32 = mybir.dt.float32
BF16 = mybir.dt.bfloat16
I8 = mybir.dt.int8
U8 = mybir.dt.uint8
AF = mybir.ActivationFunctionType
ALU = mybir.AluOpType

D = 2048
DT = 16
S = 16
TOUT = 1024
PAD = 64
T = PAD + TOUT
NO = 8
NG = 6
INTER = 8192
IT = 64
WIN3 = [(0, 512), (512, 1024), (1024, T)]
EPS = 1e-5
SBUF_BYTES = 212832

ENGINES = ("sync", "gpsimd", "scalar", "vector", "tensor")


class Trk:
    def __init__(self, name, step=1):
        self.name, self.step, self.n, self.h = name, step, 0, None


class Alloc:
    def __init__(self, nbytes):
        self.cap = nbytes
        self.items = []

    def pin(self, off, size, t0, t1, name):
        size = (size + 63) & ~63
        for (o, s, a0, a1, n) in self.items:
            if not (a1 <= t0 or t1 <= a0) and not (o + s <= off or off + size <= o):
                raise MemoryError(f"pin clash: {name} vs {n}")
        self.items.append((off, size, t0, t1, name))
        return off

    def alloc(self, size, t0, t1, name):
        size = (size + 63) & ~63
        spans = sorted((o, o + s) for (o, s, a0, a1, _n) in self.items
                       if not (a1 <= t0 or t1 <= a0))
        holes, cur = [], 0
        for (s0, s1) in spans:
            if s0 > cur:
                holes.append((cur, s0))
            cur = max(cur, s1)
        if cur < self.cap:
            holes.append((cur, self.cap))
        cand = [h for h in holes if h[1] - h[0] >= size]
        if not cand:
            raise MemoryError(f"SBUF overflow: {name} {size}B t[{t0},{t1}]")
        off = min(cand, key=lambda h: (h[1] - h[0], h[0]))[0]
        self.items.append((off, size, t0, t1, name))
        return off


def build_program(taps=()):
    nc = bass.Bass("TRN2", target_bir_lowering=False, debug=False)

    d_x = nc.dram_tensor("xT", [128, DT * T], F32, kind="ExternalInput").ap()
    d_win = nc.dram_tensor("w_in_t", [DT, 128, 2048], F32, kind="ExternalInput").ap()
    d_wout = nc.dram_tensor("w_out_t", [DT, 128, 2048], F32, kind="ExternalInput").ap()
    d_bwt = nc.dram_tensor("b_w_t", [128, 256], F32, kind="ExternalInput").ap()
    d_cwt = nc.dram_tensor("c_w_t", [16, 2048], F32, kind="ExternalInput").ap()
    d_a = nc.dram_tensor("a_t", [128, 256], F32, kind="ExternalInput").ap()
    d_dcol = nc.dram_tensor("dcol", [128, 16], F32, kind="ExternalInput").ap()
    d_bin = nc.dram_tensor("bincol", [128, 16], F32, kind="ExternalInput").ap()
    d_bout = nc.dram_tensor("boutcol", [128, 16], F32, kind="ExternalInput").ap()
    d_sd = nc.dram_tensor("sdcol", [128, 16], F32, kind="ExternalInput").ap()
    d_sg = nc.dram_tensor("sgcol", [128, 64], F32, kind="ExternalInput").ap()
    d_su = nc.dram_tensor("sucol", [128, 64], F32, kind="ExternalInput").ap()
    d_wgu = nc.dram_tensor("wgu_t", [IT, 128, 4096], I8, kind="ExternalInput").ap()
    d_wd = nc.dram_tensor("wd_t", [DT, 2, 128, 4096], I8, kind="ExternalInput").ap()
    d_mask = nc.dram_tensor("maskb", [16, T], BF16, kind="ExternalInput").ap()
    d_rep16 = nc.dram_tensor("rep16b", [16, 128], BF16, kind="ExternalInput").ap()
    d_ones_b = nc.dram_tensor("ones_b", [128, 1], BF16, kind="ExternalInput").ap()
    d_ones_f = nc.dram_tensor("ones_f", [1, 128], F32, kind="ExternalInput").ap()
    d_oodf = nc.dram_tensor("ood_f", [1, 128], F32, kind="ExternalInput").ap()
    d_oodb = nc.dram_tensor("ood_b", [1, 128], BF16, kind="ExternalInput").ap()
    d_identf = nc.dram_tensor("ident_f", [128, 128], F32, kind="ExternalInput").ap()
    d_out = nc.dram_tensor("out_xT", [128, DT * TOUT], F32, kind="ExternalOutput").ap()
    tap_out = {}
    for tname, tshape, tdt in taps:
        tap_out[tname] = nc.dram_tensor("tap_" + tname, tshape, tdt,
                                        kind="ExternalOutput").ap()

    al = Alloc(SBUF_BYTES)
    o_x = al.pin(0, DT * T * 4, 0, 12, "x")
    o_scr = al.pin(192512, 1024 * 4, 0, 12, "scratch")
    o_sq = al.pin(196608, 4 * 512 * 2, 0, 12, "sqscr")
    o_srown = al.pin(200704, 2 * 1024 * 2, 4, 12, "sumrows")
    o_bcb = al.pin(204800, 2 * 1024, 5, 12, "bcastB")
    o_onesf = al.pin(206848, 128 * 4 + 256, 0, 12, "onesf+oodb")
    o_const = al.pin(207616, 2 * 64 + 2 * 256 + 64, 0, 12, "consts")
    o_fuseA = al.pin(69632, IT * 512 * 2, 6, 8, "fuseA")
    o_wri = al.pin(163840, 2 * 4096, 0, 2, "wring_in")
    o_srow = al.pin(172032, 2 * T * 2, 0, 2, "sumrows_wide")
    o_wro = al.pin(172032, 2 * 4096, 5, 6, "wring_out")
    o_cwt = al.pin(172032, 2048 * 2, 4, 5, "cwtb")
    o_ypre0 = al.pin(139264, DT * TOUT * 2, 4, 6, "ypre")
    o_ypre = [o_ypre0 + t * 2048 for t in range(DT)]
    o_xn1 = al.pin(69632, DT * T * 2, 0, 6, "xn1b")
    o_xpb = al.pin(104448, DT * T * 2, 1, 6, "x_projb/y_attnb")
    o_pkf = al.pin(104448, 2 * 8192, 0, 1, "pkf")
    o_pk = al.pin(139264, DT * NG * 128 * 2, 0, 4, "packs")
    o_fuseB = al.pin(69632, IT * 512 * 2, 8, 11, "fuseB")
    o_xy1 = al.pin(135168, DT * 512 * 2, 6, 10, "x2na+y2a")
    o_x2na = o_xy1
    o_y2a = o_xy1
    o_xy2 = al.pin(151552, DT * 512 * 2, 8, 12, "x2nb+y2b")
    o_x2nb = o_xy2
    o_y2b = o_xy2
    o_wguA = al.pin(151552, 2 * 8192, 6, 8, "wguA")
    o_wguB = al.pin(167936, 8192, 8, 11, "wguB_s0")
    al.pin(176128, 8192, 7, 11, "wguB_s1")
    o_wrdA = al.pin(167936, 2 * 4096, 6, 8, "wrdA")
    o_wrdB = al.pin(184320, 2 * 4096, 9, 12, "wrdB")
    o_gh = al.pin(163840, NG * 576 * 4, 3, 4, "ghat")
    o_dr = al.pin(177664, T * 4, 2, 4, "deltar")
    o_acc1 = al.pin(182016, 520 * 4, 3, 4, "acc1")
    o_shb = al.pin(184128, TOUT * 2, 3, 5, "S_hb")
    o_bca = al.pin(176384, 2048 + 1024, 0, 2, "bcastA")
    o_mask = al.pin(182016, T * 2, 0, 3, "maskb")
    o_db = al.pin(163840, T * 2, 2, 3, "deltab")
    o_sig = al.pin(120832, 2 * 1024, 0, 1, "sig")
    o_mbf1 = al.pin(180224, 2048, 5, 7, "mbf_w0")
    o_mbf2 = al.pin(184320, 2048, 5, 9, "mbf_w1")
    o_bcb2 = al.pin(186368, 1024, 5, 9, "bcastB2")
    o_bwt = al.pin(184192, 256 * 2, 0, 3, "bwtb")
    o_rep16 = al.pin(184704, 256, 0, 3, "rep16")
    o_dcol = al.pin(186688, 64, 0, 5, "dcol")
    o_binc = al.pin(179456, 64, 0, 2, "binc")
    o_identf = al.pin(186176, 128 * 4, 0, 4, "identf")
    o_sq4 = al.pin(208320, 512 * 2, 0, 12, "sqt4")
    import os as _os
    if _os.environ.get("KDEBUG_ALLOC"):
        for (o, s, a0, a1, n) in sorted(al.items):
            print(f"{o:7d} +{s:6d} t[{a0:2d},{a1:2d}] {n}")
        raise SystemExit(0)

    plan = {e: [] for e in ENGINES}

    def Praw(eng, fn):
        plan[eng].append(fn)

    t_pe, t_vec, t_scl = Trk("t_pe"), Trk("t_vec"), Trk("t_scl")
    s_const = Trk("s_const", 16)
    s_x = [Trk(f"s_x{i}", 16) for i in range(3)]
    s_bwt, s_cwt = Trk("s_bwt", 16), Trk("s_cwt", 16)
    s_wri = [Trk(f"s_wri{i}", 16) for i in range(2)]
    s_wro = [Trk(f"s_wro{i}", 16) for i in range(2)]
    s_wrf = [Trk(f"s_wrf{i}", 16) for i in range(4)]
    s_wrd = [Trk(f"s_wrd{i}", 16) for i in range(2)]
    s_out, s_tap = Trk("s_out", 16), Trk("s_tap", 16)
    DMAT = ([s_const] + s_x + [s_bwt, s_cwt] + s_wri + s_wro + s_wrf + s_wrd
            + [s_out, s_tap])
    TRKS = [t_pe, t_vec, t_scl] + DMAT
    ETRK = {"tensor": t_pe, "vector": t_vec, "scalar": t_scl}

    def DMA(queue, trk, build):
        trk.n += trk.step
        plan[queue].append(lambda e, b=build, t=trk: b(e).then_inc(t.h, t.step))
        return (trk, trk.n)

    def W(eng, trk, val=None):
        if val is None:
            trk, val = trk
        if val and val > 0:
            Praw(eng, lambda e, t=trk, v=val: e.wait_ge(t.h, v))

    def P(eng, fn):
        trk = ETRK[eng]
        trk.n += trk.step
        plan[eng].append(lambda e, b=fn, t=trk: b(e).then_inc(t.h, t.step))
        return trk.n

    def E(eng, build, inc=False):
        if inc:
            return P(eng, build)
        Praw(eng, build)
        return None

    def M(eng):
        return ETRK[eng].n

    def CB(eng):
        trk = ETRK[eng]
        Praw(eng, lambda e, t=trk, v=trk.n: e.wait_ge(t.h, v))

    def BAR():
        for eng in ("tensor", "vector", "scalar"):
            for trk in (t_pe, t_vec, t_scl):
                if trk.n > 0:
                    Praw(eng, lambda e, t=trk, v=trk.n: e.wait_ge(t.h, v))

    class Bank:
        def __init__(self, idx):
            self.idx, self.w_pe, self.r_trk = idx, 0, None

    banks = [Bank(i) for i in range(8)]

    def pe_acquire(bk):
        if bk.r_trk is not None:
            W("tensor", bk.r_trk[0], bk.r_trk[1])

    def mm(out_ap, lhsT, rhs, start, stop, inc=True):
        b = (lambda e, o=out_ap, l=lhsT, r=rhs, st=start, sp=stop:
             e.matmul(o, l, r, start=st, stop=sp))
        t_pe.n += 1
        plan["tensor"].append(lambda e, bb=b: bb(e).then_inc(t_pe.h, 1))
        return t_pe.n

    def act_table(set_name):
        from concourse.hw_specs import get_activation_tables
        sid = list(get_activation_tables(nc.m.arch).keys()).index(set_name)

        Praw("scalar", lambda e, sid=sid: e.add_instruction(
            mybir.InstLoadActFuncSet(
                name=nc.get_next_instruction_name(), act_func_set_id=sid,
                ins=[], outs=[])))
        for _ in range(12):
            P("scalar", lambda e: e.copy(out=DLY, in_=DLY))
            CB("scalar")

    es = contextlib.ExitStack()
    nc._kernel_es = es
    ar = es.enter_context(nc.sbuf_tensor("arena", [128, SBUF_BYTES], U8))
    pb = [es.enter_context(nc.psum_tensor(f"pb{i}", [128, 512], F32))
          for i in range(8)]
    for trk in TRKS:
        trk.h = es.enter_context(nc.semaphore(trk.name))

    arena_base = None
    for a in nc.m.functions[0].allocations:
        mls = getattr(a, "memorylocations", None)
        if not mls:
            continue
        if mls[0].name == "arena":
            arena_base = mls[0].addr
    assert arena_base is not None

    def REG(name, off, part, cols, dtype):
        return nc.alloc_sbuf_tensor_at(name, [part, cols], dtype,
                                       offset=arena_base + off).ap()

    XT = REG("XT", o_x, 128, DT * T, F32)
    XN1T = REG("XN1T", o_xn1, 128, DT * T, BF16)
    XPBT = REG("XPBT", o_xpb, 128, DT * T, BF16)
    PKT = REG("PKT", o_pk, 128, DT * NG * 128, BF16)
    GHT = REG("GHT", o_gh, 128, NG * 576, F32)
    DR = REG("DRT", o_dr, 128, T, F32)
    DB = REG("DBT", o_db, 16, T, BF16)
    MASK = REG("MASKT", o_mask, 16, T, BF16)
    ACC1 = REG("ACC1T", o_acc1, 128, 520, F32)
    SHB = REG("SHBT", o_shb, 16, TOUT, BF16)
    SCRT = REG("SCRT", o_scr, 128, 1024, F32)
    SCR = SCRT
    SRT = REG("SRT", o_srow, 1, 2 * T, BF16)
    SRA1 = SRT[:, 0:T]
    SRB1 = SRT[:, T:2 * T]
    SRNT = REG("SRNT", o_srown, 1, 2048, BF16)
    SRA = SRNT[:, 0:1024]
    SRB = SRNT[:, 1024:2048]
    CHS = SCRT[:, 0:512]
    CH2 = SCRT[:, 512:1024]
    GS = SCRT[:, 0:512]
    SQTT = REG("SQTT", o_sq, 128, 2048, BF16)
    SQT4 = REG("SQT4", o_sq4, 128, 512, BF16)

    def SQT(i):
        return SQTT[:, i * 512:(i + 1) * 512]

    DLY = SQTT[:, 0:128]

    MB_F = REG("MBF1", o_mbf1, 128, 512, F32)
    MB_F2 = REG("MBF2", o_mbf2, 128, 512, F32)
    INVBT = REG("INVBT", o_bcb, 128, 1024, BF16)
    INVB = INVBT[:, 0:512]
    MB_B = INVBT[:, 512:1024]
    INVB2 = REG("INVB2", o_bcb2, 128, 512, BF16)
    MB1F = REG("MB1F", o_bca, 128, 512, F32)
    INV1B = REG("INV1B", o_bca + 2048, 128, 512, BF16)
    WRIT = REG("WRIT", o_wri, 128, 4096, BF16)
    WROT = REG("WROT", o_wro, 128, 4096, BF16)
    WGUA = REG("WGUA", o_wguA, 128, 8192, BF16)
    WGUB = REG("WGUB", o_wguB, 128, 8192, BF16)

    def WRI(i):
        return WRIT[:, i * 2048:(i + 1) * 2048]

    def WRO(i):
        return WROT[:, i * 2048:(i + 1) * 2048]

    BWTB = REG("BWTT", o_bwt, 128, 256, BF16)
    CWTB = REG("CWTT", o_cwt, 16, 2048, BF16)
    oc = o_const
    DCOL = REG("DCOL", o_dcol, 128, 16, F32)
    BINC = REG("BINC", o_binc, 128, 16, F32)
    BOUTC = REG("BOUTC", oc, 128, 16, F32); oc += 64
    SDC = REG("SDC", oc, 128, 16, F32); oc += 64
    SGC = REG("SGC", oc, 128, 64, F32); oc += 256
    SUC = REG("SUC", oc, 128, 64, F32); oc += 256
    REP16 = REG("REP16", o_rep16, 16, 128, BF16)
    ONESB = REG("ONESB", oc, 128, 1, BF16); oc += 64
    ONESF = REG("ONESF", o_onesf, 1, 128, F32)
    OODB = REG("OODB", o_onesf + 512, 1, 128, BF16)
    IDENTF = REG("IDENTF", o_identf, 128, 128, F32)
    SIGA = REG("SIGA", o_sig, 128, 256, F32)
    SIG8 = REG("SIG8", o_sig + 1024, 128, 256, F32)
    PKF = REG("PKF", o_pkf, 128, 2048, F32)
    S8R = REG("S8R", o_pkf + 8192, 128, 2048, F32)
    YPRET = [REG(f"YPRE{t}", o_ypre[t], 128, TOUT, BF16) for t in range(DT)]
    FUSETA = REG("FUSETA", o_fuseA, 128, IT * 512, BF16)
    FUSETB = REG("FUSETB", o_fuseB, 128, IT * 512, BF16)
    X2NAT = REG("X2NAT", o_x2na, 128, DT * 512, BF16)
    WRDAT = REG("WRDAT", o_wrdA, 128, 4096, BF16)
    WRDBT = REG("WRDBT", o_wrdB, 128, 4096, BF16)
    X2NBT = REG("X2NBT", o_x2nb, 128, DT * 512, BF16)
    Y2AT = REG("Y2AT", o_y2a, 128, DT * 512, BF16)
    Y2BT = REG("Y2BT", o_y2b, 128, DT * 512, BF16)

    def Xj(j, c0, c1):
        return XT[:, j * T + c0:j * T + c1]

    def XN1j(j, c0, c1):
        return XN1T[:, j * T + c0:j * T + c1]

    def XPBj(j, c0, c1):
        return XPBT[:, j * T + c0:j * T + c1]

    def YATT(j, l0, l1):
        return XPBT[:, j * T + l0:j * T + l1]

    def PACK(j, g):
        return PKT[:, (j * NG + g) * 128:(j * NG + g + 1) * 128]

    def GH(g, c0, c1):
        return GHT[:, g * 576 + c0:g * 576 + c1]

    def YPRE(t, l0, l1):
        return YPRET[t][:, l0:l1]

    def FUSE(w, i):
        t = FUSETA if w == 0 else FUSETB
        return t[:, i * 512:(i + 1) * 512]

    def X2NW(k, j):
        t = X2NAT if k == 0 else X2NBT
        return t[:, j * 512:(j + 1) * 512]

    def Y2W(k, j):
        t = Y2AT if k == 0 else Y2BT
        return t[:, j * 512:(j + 1) * 512]

    def tap(name, view):
        if name not in tap_out:
            return
        W("sync", t_vec, t_vec.n)
        W("sync", t_scl, t_scl.n)
        W("sync", t_pe, t_pe.n)
        DMA("sync", s_tap, lambda e, d=tap_out[name], v=view:
            e.dma_start(out=d, in_=v))
        for eng in ("vector", "scalar", "tensor", "gpsimd"):
            W(eng, s_tap, s_tap.n)

    for (dst, src) in [(SIGA, d_a), (DCOL, d_dcol), (BINC, d_bin),
                       (BOUTC, d_bout), (SDC, d_sd), (SGC, d_sg), (SUC, d_su),
                       (MASK, d_mask), (REP16, d_rep16), (ONESB, d_ones_b),
                       (ONESF, d_ones_f), (OODB, d_oodb),
                       (IDENTF, d_identf)]:
        DMA("sync", s_const, lambda e, d=dst, s=src: e.dma_start(out=d, in_=s))
    v_consts = (s_const, s_const.n)
    d_x3 = d_x.rearrange("p (j c) -> p j c", j=DT)
    x_ar3 = XT.rearrange("p (j c) -> p j c", j=DT)
    v_xw = []
    for wi, (w0, w1) in enumerate(WIN3):
        v_xw.append(DMA("sync", s_x[wi],
                        lambda e, a=x_ar3[:, :, w0:w1], s=d_x3[:, :, w0:w1]:
                        e.dma_start(out=a, in_=s)))
    v_bwt = DMA("gpsimd", s_bwt, lambda e: e.dma_start(out=BWTB, in_=d_bwt))
    v_win = {}
    for t in range(2):
        v_win[t] = DMA("gpsimd", s_wri[t % 2],
                       lambda e, t=t: e.dma_start(out=WRI(t % 2), in_=d_win[t]))

    P("vector", lambda e: e.memset(DLY, 0.0))
    v_dly = M("vector")
    W("scalar", t_vec, v_dly)
    act_table("natural_log_exp_and_others")

    W("vector", v_consts)
    W("scalar", v_consts)
    P("scalar", lambda e: e.activation(out=SCR[:, 0:256], in_=SIGA, func=AF.Exp,
                                       scale=-1.0))
    v_expA = M("scalar")
    W("vector", t_scl, v_expA)
    P("vector", lambda e: e.tensor_scalar_add(out=SCR[:, 0:256],
                                              in0=SCR[:, 0:256], scalar1=1.0))
    CB("vector")
    P("vector", lambda e: e.reciprocal(out=SIGA, in_=SCR[:, 0:256]))
    CB("vector")
    P("vector", lambda e: e.tensor_mul(out=SCR[:, 0:256], in0=SIGA, in1=SIGA))
    CB("vector")
    P("vector", lambda e: e.tensor_mul(out=SCR[:, 256:512], in0=SCR[:, 0:256],
                                       in1=SCR[:, 0:256]))
    CB("vector")
    P("vector", lambda e: e.tensor_mul(out=SIG8, in0=SCR[:, 256:512],
                                       in1=SCR[:, 256:512]))
    CB("vector")
    s8r4 = S8R.rearrange("p (j o s) -> p j o s", j=16, o=8)
    sig8_3 = SIG8.rearrange("p (j s) -> p j s", j=16)
    for o in range(NO):
        P("vector", lambda e, o=o: e.tensor_copy(out=s8r4[:, :, o, :], in_=sig8_3))
    CB("vector")
    pkf4 = PKF.rearrange("p (j o s) -> p j o s", j=16, o=8)
    siga3 = SIGA.rearrange("p (j s) -> p j s", j=16)
    P("vector", lambda e: e.memset(pkf4[:, :, 0, :], 1.0))
    CB("vector")
    for o in range(1, NO):
        P("vector", lambda e, o=o: e.tensor_tensor(
            out=pkf4[:, :, o, :], in0=pkf4[:, :, o - 1, :], in1=siga3,
            op=ALU.mult))
        CB("vector")
    v_pkf0 = M("vector")
    pk5 = PKT.rearrange("p (j g c) -> p j g c", j=DT, g=NG)
    pkf3 = PKF.rearrange("p (j c) -> p j c", j=16)
    W("scalar", t_vec, v_pkf0)
    for g in range(NG):
        P("scalar", lambda e, g=g: e.copy(out=pk5[:, :, g, :], in_=pkf3))
        if g < NG - 1:
            vv = M("scalar")
            W("vector", t_scl, vv)
            P("vector", lambda e: e.tensor_mul(out=PKF, in0=PKF, in1=S8R))
            CB("vector")
            vv2 = M("vector")
            W("scalar", t_vec, vv2)
    M("scalar")
    v_pack_vec = t_vec.n

    ln_state = {"mb_busy": (t_vec, 0)}

    def emit_ln(windows, src_fn, src_is_f32, outs, apply_fn, wait_src):
        for wi, (c0, c1) in enumerate(windows):
            wl = c1 - c0
            pe_acquire(banks[6])
            pe_acquire(banks[7])
            pe_j = {-2: t_pe.n, -1: t_pe.n}
            if wait_src is not None:
                ws = wait_src if isinstance(wait_src, list) else [wait_src]
                for pair in ([ws[wi]] if len(ws) == len(windows) else ws):
                    W("scalar", pair)
            for j in range(DT):
                W("scalar", t_pe, pe_j[j - 2])
                if src_is_f32:
                    sl = (j % 2) * 2
                    P("scalar", lambda e, j=j, sl=sl, c0=c0, c1=c1:
                      e.copy(out=SQT(sl)[:, 0:c1 - c0], in_=src_fn(j, c0, c1)))
                    P("scalar", lambda e, j=j, sl=sl, c0=c0, c1=c1:
                      e.activation(out=SQT(sl + 1)[:, 0:c1 - c0],
                                   in_=src_fn(j, c0, c1), func=AF.Square))
                    rhs_sum, rhs_sq = SQT(sl)[:, 0:wl], SQT(sl + 1)[:, 0:wl]
                else:
                    sl = j % 2
                    P("scalar", lambda e, j=j, sl=sl, c0=c0, c1=c1:
                      e.activation(out=SQT(sl)[:, 0:c1 - c0],
                                   in_=src_fn(j, c0, c1), func=AF.Square))
                    rhs_sum, rhs_sq = src_fn(j, c0, c1), SQT(sl)[:, 0:wl]
                vs = M("scalar")
                W("tensor", t_scl, vs)
                mm(pb[6][0:1, 0:wl], ONESB, rhs_sum, j == 0, j == DT - 1)
                pe_j[j] = mm(pb[7][0:1, 0:wl], ONESB, rhs_sq,
                             j == 0, j == DT - 1, inc=True)
            banks[6].w_pe = banks[7].w_pe = t_pe.n
            W("vector", t_pe, t_pe.n)
            P("vector", lambda e, c0=c0, c1=c1, wl=wl: e.tensor_copy(
                out=SRA1[0:1, c0:c1], in_=pb[6][0:1, 0:wl]))
            P("vector", lambda e, c0=c0, c1=c1, wl=wl: e.tensor_copy(
                out=SRB1[0:1, c0:c1], in_=pb[7][0:1, 0:wl]))
            vv = M("vector")
            banks[6].r_trk = banks[7].r_trk = (t_vec, vv)
            emit_ln_chain([(c0, c1)], [outs[wi]],
                          (None if apply_fn is None else
                           lambda _w, a, b, v, wi=wi: apply_fn(wi, a, b, v)),
                          vv, SRA1, SRB1)

    def emit_ln_chain(windows, outs, apply_fn, v_rows, ra=None, rb=None,
                      ba=6, bb=7):
        ra = SRA if ra is None else ra
        rb = SRB if rb is None else rb
        ood = OODB
        for wi, (c0, c1) in enumerate(windows):
            wl = c1 - c0
            mbf, mbb, invb = outs[wi]
            W("tensor", t_vec, v_rows)
            pe_acquire(banks[ba])
            mm(pb[ba][:, 0:wl], ood, ra[0:1, c0:c1], True, True, inc=True)
            banks[ba].w_pe = t_pe.n
            W("scalar", t_pe, banks[ba].w_pe)
            W("scalar", ln_state["mb_busy"][0], ln_state["mb_busy"][1])
            CB("scalar")
            P("scalar", lambda e, wl=wl, ba=ba: e.copy(out=CHS[:, 0:wl],
                                                       in_=pb[ba][:, 0:wl]))
            if mbf is not None:
                P("scalar", lambda e, d=mbf, wl=wl, ba=ba: e.copy(
                    out=d[:, 0:wl], in_=pb[ba][:, 0:wl]))
            if mbb is not None:
                P("scalar", lambda e, d=mbb, wl=wl, ba=ba: e.copy(
                    out=d[:, 0:wl], in_=pb[ba][:, 0:wl]))
            v_m = M("scalar")
            banks[ba].r_trk = (t_scl, v_m)
            pe_acquire(banks[bb])
            mm(pb[bb][:, 0:wl], ood, rb[0:1, c0:c1], True, True, inc=True)
            banks[bb].w_pe = t_pe.n
            W("vector", t_scl, v_m)
            P("vector", lambda e, wl=wl: e.tensor_mul(
                out=CH2[:, 0:wl], in0=CHS[:, 0:wl], in1=CHS[:, 0:wl]))
            CB("vector")
            W("vector", t_pe, banks[bb].w_pe)
            P("vector", lambda e, wl=wl, bb=bb: e.tensor_sub(
                out=CHS[:, 0:wl], in0=pb[bb][:, 0:wl], in1=CH2[:, 0:wl]))
            CB("vector")
            P("vector", lambda e, wl=wl: e.tensor_scalar_add(
                out=CHS[:, 0:wl], in0=CHS[:, 0:wl], scalar1=EPS))
            vv2 = M("vector")
            banks[bb].r_trk = (t_vec, vv2)
            W("scalar", t_vec, vv2)
            P("scalar", lambda e, wl=wl: e.activation(
                out=CHS[:, 0:wl], in_=CHS[:, 0:wl], func=AF.Ln))
            CB("scalar")
            P("scalar", lambda e, d=invb, wl=wl: e.activation(
                out=d[:, 0:wl], in_=CHS[:, 0:wl], func=AF.Exp, scale=-0.5))
            v_chain = M("scalar")
            if apply_fn is not None:
                apply_fn(wi, c0, c1, v_chain)

    v_xn1_w = {}

    def ln1_apply(wi, c0, c1, v_chain):
        wl = c1 - c0
        W("vector", t_scl, v_chain)
        for j in range(DT):
            P("vector", lambda e, j=j, c0=c0, c1=c1, wl=wl: e.tensor_sub(
                out=SQT4[:, 0:wl], in0=Xj(j, c0, c1), in1=MB1F[:, 0:wl]))
            CB("vector")
            P("vector", lambda e, j=j, c0=c0, c1=c1, wl=wl: e.tensor_mul(
                out=XN1j(j, c0, c1), in0=SQT4[:, 0:wl], in1=INV1B[:, 0:wl]))
            CB("vector")
        v_xn1_w[wi] = M("vector")
        ln_state["mb_busy"] = (t_vec, v_xn1_w[wi])

    emit_ln(WIN3, lambda j, c0, c1: Xj(j, c0, c1), True,
            [(MB1F, None, INV1B)] * 3, ln1_apply, v_xw)
    v_xn1_done = ln_state["mb_busy"][1]
    tap("xn1b", XN1T)

    for t in range(DT):
        W("tensor", v_win[t])
        for wi, (w0, w1) in enumerate(WIN3):
            if t == 0:
                W("tensor", t_vec, v_xn1_w[wi])
            wl = w1 - w0
            bk = banks[(t * 3 + wi) % 4]
            pe_acquire(bk)
            for j in range(DT):
                mm(pb[bk.idx][:, 0:wl], WRI(t % 2)[:, j * 128:(j + 1) * 128],
                   XN1j(j, w0, w1), j == 0, j == DT - 1, inc=(j == DT - 1))
            bk.w_pe = t_pe.n
            W("scalar", t_pe, bk.w_pe)
            if t == 0 and wi == 0:
                W("scalar", t_vec, v_pack_vec)
            P("scalar", lambda e, t=t, w0=w0, w1=w1, wl=wl, b=bk.idx:
              e.activation(out=XPBj(t, w0, w1), in_=pb[b][:, 0:wl],
                           func=AF.Identity, bias=BINC[:, t:t + 1]))
            vv = M("scalar")
            bk.r_trk = (t_scl, vv)
        if t + 2 < DT:
            W("gpsimd", t_pe, t_pe.n)
            v_win[t + 2] = DMA("gpsimd", s_wri[t % 2],
                               lambda e, t=t: e.dma_start(out=WRI(t % 2),
                                                          in_=d_win[t + 2]))
    v_xpb_all = t_scl.n
    tap("x_projb", XPBT)

    BAR()
    W("tensor", v_bwt)
    W("tensor", t_scl, v_xpb_all)
    for wi, (w0, w1) in enumerate(WIN3):
        wl = w1 - w0
        bk = banks[4 + (wi % 2)]
        pe_acquire(bk)
        for j in range(DT):
            mm(pb[bk.idx][0:16, 0:wl], BWTB[:, j * 16:(j + 1) * 16],
               XPBj(j, w0, w1), j == 0, j == DT - 1, inc=(j == DT - 1))
        bk.w_pe = t_pe.n
        W("scalar", t_pe, bk.w_pe)
        P("scalar", lambda e, b=bk.idx, wl=wl: e.activation(
            out=SCR[0:16, 0:wl], in_=pb[b][0:16, 0:wl], func=AF.Exp))
        CB("scalar")
        P("scalar", lambda e, wl=wl: e.activation(
            out=SCR[0:16, 512:512 + wl], in_=SCR[0:16, 0:wl], func=AF.Ln,
            bias=1.0))
        CB("scalar")
        P("scalar", lambda e, w0=w0, w1=w1, wl=wl: e.copy(
            out=DB[:, w0:w1], in_=SCR[0:16, 512:512 + wl]))
        vv = M("scalar")
        bk.r_trk = (t_scl, vv)
    W("vector", t_scl, t_scl.n)
    P("vector", lambda e: e.tensor_mul(out=DB, in0=DB, in1=MASK))
    v_dbm = M("vector")
    W("tensor", t_vec, v_dbm)
    for wi, (w0, w1) in enumerate(WIN3):
        wl = w1 - w0
        bk = banks[4 + (wi % 2)]
        pe_acquire(bk)
        mm(pb[bk.idx][:, 0:wl], REP16, DB[:, w0:w1], True, True, inc=True)
        bk.w_pe = t_pe.n
        W("scalar", t_pe, bk.w_pe)
        P("scalar", lambda e, b=bk.idx, w0=w0, w1=w1, wl=wl: e.copy(
            out=DR[:, w0:w1], in_=pb[b][:, 0:wl]))
        vv = M("scalar")
        bk.r_trk = (t_scl, vv)
    v_dr = t_scl.n
    tap("deltab", DB)

    BAR()
    W("tensor", t_scl, v_dr)
    v_stageB = 0
    for k in range(2):
        gbase = 512 * k
        pieces = [(gbase, gbase + 512), (gbase + 512, min(gbase + 576, T))]
        if k == 1:
            W("tensor", t_vec, v_stageB)
        for g in range(NG):
            for pi, (p0, p1) in enumerate(pieces):
                pl = p1 - p0
                bk = banks[(g * 2 + pi) % 4]
                pe_acquire(bk)
                for j in range(DT):
                    mm(pb[bk.idx][:, 0:pl], PACK(j, g), XPBj(j, p0, p1),
                       j == 0, j == DT - 1, inc=(j == DT - 1))
                bk.w_pe = t_pe.n
                W("vector", t_pe, bk.w_pe)
                P("vector", lambda e, g=g, p0=p0, p1=p1, pl=pl, b=bk.idx,
                  gb=gbase: e.tensor_tensor(
                      out=GH(g, p0 - gb, p1 - gb), in0=pb[b][:, 0:pl],
                      in1=DR[:, p0:p1], op=ALU.mult))
                vv = M("vector")
                bk.r_trk = (t_vec, vv)
        CB("vector")
        P("vector", lambda e: e.tensor_add(
            out=ACC1[:, 0:519], in0=GH(0, 57, 576), in1=GH(1, 49, 568)))
        CB("vector")
        for g in range(2, NG):
            c0 = 57 - 8 * g
            P("vector", lambda e, c0=c0, g=g: e.tensor_add(
                out=ACC1[:, 0:519], in0=ACC1[:, 0:519], in1=GH(g, c0, c0 + 519)))
            CB("vector")
        v_stageA = M("vector")
        bkS = banks[4 + k % 2]
        W("tensor", t_vec, v_stageA)
        pe_acquire(bkS)
        for o in range(NO):
            mm(pb[bkS.idx][0:16, 0:512], IDENTF[:, 16 * o:16 * o + 16],
               ACC1[:, 7 - o:519 - o], o == 0, o == NO - 1, inc=(o == NO - 1))
        bkS.w_pe = t_pe.n
        W("vector", t_pe, bkS.w_pe)
        P("vector", lambda e, k=k, b=bkS.idx: e.tensor_copy(
            out=SHB[:, 512 * k:512 * (k + 1)], in_=pb[b][0:16, 0:512]))
        v_stageB = M("vector")
        bkS.r_trk = (t_vec, v_stageB)
    CB("vector")
    v_shb = t_vec.n
    tap("S_h", SHB)

    BAR()
    W("gpsimd", t_pe, t_pe.n)
    v_cwt = DMA("gpsimd", s_cwt, lambda e: e.dma_start(out=CWTB, in_=d_cwt))
    W("tensor", v_cwt)
    W("tensor", t_vec, v_shb)
    v_dscr_read = 0
    for t in range(DT):
        for wi in range(2):
            l0 = wi * 512
            bk = banks[(t * 2 + wi) % 4]
            pe_acquire(bk)
            mm(pb[bk.idx][:, 0:512], CWTB[:, t * 128:(t + 1) * 128],
               SHB[:, l0:l0 + 512], True, True, inc=True)
            bk.w_pe = t_pe.n
            W("scalar", t_vec, v_dscr_read)
            P("scalar", lambda e, t=t, l0=l0: e.activation(
                out=SCR[:, 0:512], in_=XPBj(t, PAD + l0, PAD + l0 + 512),
                func=AF.Copy, scale=DCOL[:, t:t + 1]))
            vsc = M("scalar")
            W("vector", t_pe, bk.w_pe)
            W("vector", t_scl, vsc)
            P("vector", lambda e, t=t, l0=l0, b=bk.idx: e.tensor_add(
                out=YPRE(t, l0, l0 + 512), in0=pb[b][:, 0:512],
                in1=SCR[:, 0:512]))
            vv = M("vector")
            bk.r_trk = (t_vec, vv)
            v_dscr_read = vv
    v_ypre = t_vec.n
    tap("ypre0", YPRET[0])

    BAR()
    WL2 = [(0, 512), (512, 1024)]
    v_wout = {}
    W("gpsimd", t_pe, t_pe.n)
    for t in range(2):
        v_wout[t] = DMA("gpsimd", s_wro[t % 2],
                        lambda e, t=t: e.dma_start(out=WRO(t % 2),
                                                   in_=d_wout[t]))
    W("tensor", t_vec, v_ypre)
    for b_ in (4, 5, 6, 7):
        pe_acquire(banks[b_])
    v_yatt_prev = {0: 0, 1: 0}
    sq_prev = {0: 0, 1: 0}
    pend_sum = None
    for t in range(DT):
        W("tensor", v_wout[t])
        for wi in range(2):
            l0 = wi * 512
            bk = banks[(t * 2 + wi) % 4]
            pe_acquire(bk)
            for j in range(DT):
                mm(pb[bk.idx][:, 0:512], WRO(t % 2)[:, j * 128:(j + 1) * 128],
                   YPRE(j, l0, l0 + 512), j == 0, j == DT - 1)
            bk.w_pe = t_pe.n
            if pend_sum is not None:
                pend_sum()
            sl = (t * 2 + wi) % 2
            W("scalar", t_pe, bk.w_pe)
            W("scalar", t_vec, v_yatt_prev[sl])
            P("scalar", lambda e, t=t, b=bk.idx, sl=sl: e.activation(
                out=SQT(sl)[:, 0:512], in_=pb[b][:, 0:512], func=AF.Identity,
                bias=BOUTC[:, t:t + 1]))
            vsc = M("scalar")
            bk.r_trk = (t_scl, vsc)
            W("vector", t_scl, vsc)
            P("vector", lambda e, t=t, l0=l0, sl=sl: e.tensor_add(
                out=YATT(t, l0, l0 + 512), in0=SQT(sl)[:, 0:512],
                in1=XN1j(t, PAD + l0, PAD + l0 + 512)))
            vv = M("vector")
            v_yatt_prev[sl] = vv
            W("scalar", t_vec, vv)
            W("scalar", t_pe, sq_prev[sl])
            P("scalar", lambda e, t=t, l0=l0, sl=sl: e.activation(
                out=SQT(2 + sl)[:, 0:512], in_=YATT(t, l0, l0 + 512),
                func=AF.Square))
            vsq = M("scalar")

            def pend_sum(t=t, wi=wi, l0=l0, sl=sl, vv=vv, vsq=vsq):
                W("tensor", t_vec, vv)
                mm(pb[6 if wi == 0 else 7][0:1, 0:512], ONESB,
                   YATT(t, l0, l0 + 512), t == 0, t == DT - 1)
                W("tensor", t_scl, vsq)
                sq_prev[sl] = mm(pb[4 if wi == 0 else 5][0:1, 0:512], ONESB,
                                 SQT(2 + sl)[:, 0:512], t == 0, t == DT - 1)
        if t + 2 < DT:
            W("gpsimd", t_pe, t_pe.n)
            v_wout[t + 2] = DMA("gpsimd", s_wro[t % 2],
                                lambda e, t=t: e.dma_start(out=WRO(t % 2),
                                                           in_=d_wout[t + 2]))
    pend_sum()
    for b_ in (4, 5, 6, 7):
        banks[b_].w_pe = t_pe.n
    v_wgu = {}
    W("gpsimd", t_pe, t_pe.n)
    for i in range(2):
        v_wgu[i] = DMA("gpsimd", s_wrf[i % 2],
                       lambda e, i=i: e.dma_start(out=WGUPW(0, i % 2),
                                                  in_=d_wgu[i]))
    W("vector", t_pe, t_pe.n)
    P("vector", lambda e: e.tensor_copy(out=SRA[0:1, 0:512], in_=pb[6][0:1, 0:512]))
    P("vector", lambda e: e.tensor_copy(out=SRB[0:1, 0:512], in_=pb[4][0:1, 0:512]))
    P("vector", lambda e: e.tensor_copy(out=SRA[0:1, 512:1024], in_=pb[7][0:1, 0:512]))
    P("vector", lambda e: e.tensor_copy(out=SRB[0:1, 512:1024], in_=pb[5][0:1, 0:512]))
    v_rows1 = M("vector")
    for b_ in (4, 5, 6, 7):
        banks[b_].r_trk = (t_vec, v_rows1)
    tap("yatt", XPBT)

    ln2_rows_done = []

    def outer1_apply(wi, c0, c1, v_chain):
        wl = c1 - c0
        W("vector", t_scl, v_chain)
        pe_acquire(banks[4])
        pe_acquire(banks[5])
        pe_j = {-1: t_pe.n}
        for j in range(DT):
            P("vector", lambda e, j=j, c0=c0, wl=wl: e.tensor_sub(
                out=SQT(3)[:, 0:wl], in0=YATT(j, c0, c0 + wl),
                in1=MB_B[:, 0:wl]))
            CB("vector")
            P("vector", lambda e, j=j, c0=c0, wl=wl: e.tensor_mul(
                out=CH2[:, 0:wl], in0=SQT(3)[:, 0:wl], in1=INVB[:, 0:wl]))
            CB("vector")
            P("vector", lambda e, j=j, c0=c0, wl=wl: e.tensor_add(
                out=Xj(j, PAD + c0, PAD + c0 + wl),
                in0=Xj(j, PAD + c0, PAD + c0 + wl), in1=CH2[:, 0:wl]))
            vxj = M("vector")
            CB("vector")
            W("scalar", t_vec, vxj)
            W("scalar", t_pe, pe_j[j - 1])
            P("scalar", lambda e, j=j, c0=c0, wl=wl: e.copy(
                out=SQT(0)[:, 0:wl], in_=Xj(j, PAD + c0, PAD + c0 + wl)))
            P("scalar", lambda e, j=j, c0=c0, wl=wl: e.activation(
                out=SQT(1)[:, 0:wl], in_=Xj(j, PAD + c0, PAD + c0 + wl),
                func=AF.Square))
            vs2 = M("scalar")
            W("tensor", t_scl, vs2)
            mm(pb[4][0:1, 0:wl], ONESB, SQT(0)[:, 0:wl], j == 0, j == DT - 1)
            pe_j[j] = mm(pb[5][0:1, 0:wl], ONESB, SQT(1)[:, 0:wl],
                         j == 0, j == DT - 1)
        banks[4].w_pe = banks[5].w_pe = t_pe.n
        ln_state["mb_busy"] = (t_vec, t_vec.n)
        W("vector", t_pe, t_pe.n)
        P("vector", lambda e, c0=c0, c1=c1, wl=wl: e.tensor_copy(
            out=SRA[0:1, c0:c1], in_=pb[4][0:1, 0:wl]))
        P("vector", lambda e, c0=c0, c1=c1, wl=wl: e.tensor_copy(
            out=SRB[0:1, c0:c1], in_=pb[5][0:1, 0:wl]))
        vr = M("vector")
        banks[4].r_trk = banks[5].r_trk = (t_vec, vr)
        ln2_rows_done.append(vr)

    emit_ln_chain(WL2, [(None, MB_B, INVB)] * 2, outer1_apply, v_rows1)
    tap("x2", XT)

    emit_ln_chain(WL2, [(MB_F, None, INVB), (MB_F2, None, INVB2)], None,
                  ln2_rows_done[-1])
    v_ln2bc = t_scl.n

    act_table("silu_and_others")

    d_out3 = d_out.rearrange("p (j c) -> p j c", j=DT)
    v_out_parts = []
    v_rows_f = {}
    v_final_apply = {}

    def final_apply(wi, c0, c1, v_chain):
        wl = c1 - c0
        W("vector", t_scl, v_chain)
        for j in range(DT):
            P("vector", lambda e, j=j, wi=wi, wl=wl: e.tensor_sub(
                out=SQT(2)[:, 0:wl], in0=Y2W(wi, j), in1=MB_B[:, 0:wl]))
            CB("vector")
            P("vector", lambda e, j=j, wl=wl: e.tensor_mul(
                out=CH2[:, 0:wl], in0=SQT(2)[:, 0:wl], in1=INVB[:, 0:wl]))
            CB("vector")
            P("vector", lambda e, j=j, c0=c0, wl=wl: e.tensor_add(
                out=Xj(j, PAD + c0, PAD + c0 + wl),
                in0=Xj(j, PAD + c0, PAD + c0 + wl), in1=CH2[:, 0:wl]))
            CB("vector")
        v_apply = M("vector")
        v_final_apply[wi] = v_apply
        ln_state["mb_busy"] = (t_vec, v_apply)
        W("sync", t_vec, v_apply)
        v_out_parts.append(DMA("sync", s_out, lambda e, c0=c0, c1=c1:
                               e.dma_start(out=d_out3[:, :, c0:c1],
                                           in_=x_ar3[:, :, PAD + c0:PAD + c1])))

    def emit_final(wi, v_rows, ba=6, bb=7):
        c0 = 512 * wi
        emit_ln_chain([(c0, c0 + 512)], [(None, MB_B, INVB)],
                      lambda _w, a, b, v: final_apply(wi, a, b, v), v_rows,
                      ba=ba, bb=bb)

    v_wd = {}

    def WGUPW(w, slot):
        t = WGUA if w == 0 else WGUB
        return t[:, slot * 4096:(slot + 1) * 4096]

    def WRDW(w, slot):
        t = WRDAT if w == 0 else WRDBT
        return t[:, slot * 2048:(slot + 1) * 2048]

    for k in range(2):
        l0 = 512 * k
        mbf = MB_F if k == 0 else MB_F2
        invb = INVB if k == 0 else INVB2
        W("vector", t_scl, v_ln2bc)
        v_x2n_j = []
        for j in range(DT):
            P("vector", lambda e, j=j, l0=l0, mbf=mbf: e.tensor_sub(
                out=SQT(3)[:, 0:512], in0=Xj(j, PAD + l0, PAD + l0 + 512),
                in1=mbf[:, 0:512]))
            CB("vector")
            P("vector", lambda e, j=j, k=k, invb=invb: e.tensor_mul(
                out=X2NW(k, j), in0=SQT(3)[:, 0:512], in1=invb[:, 0:512]))
            v_x2n_j.append(M("vector"))
            CB("vector")
        slg = (lambda i: i % 2) if k == 0 else (lambda i: (i + 1) % 2)
        if k == 0:
            pass
        else:
            v_wgu[0] = v_wgu_k1
            W("gpsimd", t_pe, t_pe.n)
            v_wgu[1] = DMA("gpsimd", s_wrf[0],
                           lambda e: e.dma_start(out=WGUPW(1, 0),
                                                 in_=d_wgu[1]))
        v_fuse_last = 0
        v_scr_gate = t_scl.n
        for i in range(IT):
            W("tensor", v_wgu[i])
            bg = banks[i % 2]
            bu = banks[2 + i % 2]
            pe_acquire(bg)
            pe_acquire(bu)
            for j in range(DT):
                if i == 0:
                    W("tensor", t_vec, v_x2n_j[j])
                mm(pb[bg.idx][:, 0:512],
                   WGUPW(k, slg(i))[:, j * 128:(j + 1) * 128],
                   X2NW(k, j), j == 0, j == DT - 1)
            bg.w_pe = t_pe.n
            for j in range(DT):
                mm(pb[bu.idx][:, 0:512],
                   WGUPW(k, slg(i))[:, 2048 + j * 128:2048 + (j + 1) * 128],
                   X2NW(k, j), j == 0, j == DT - 1)
            bu.w_pe = t_pe.n
            W("scalar", t_pe, bg.w_pe)
            if i == 0:
                W("scalar", t_scl, v_scr_gate)
            if i >= 1:
                W("scalar", t_vec, v_fuse_last)
            P("scalar", lambda e, i=i, b=bg.idx: e.activation(
                out=GS, in_=pb[b][:, 0:512], func=AF.Silu,
                scale=SGC[:, i:i + 1]))
            vsc = M("scalar")
            bg.r_trk = (t_scl, vsc)
            W("vector", t_scl, vsc)
            W("vector", t_pe, bu.w_pe)
            P("vector", lambda e, i=i, k=k, b=bu.idx: e.scalar_tensor_tensor(
                out=FUSE(k, i), in0=pb[b][:, 0:512], scalar=SUC[:, i:i + 1],
                in1=GS, op0=ALU.mult, op1=ALU.mult))
            v_fuse_last = M("vector")
            bu.r_trk = (t_vec, v_fuse_last)
            if i + 2 < IT:
                W("gpsimd", t_pe, t_pe.n)
                v_wgu[i + 2] = DMA("gpsimd", s_wrf[slg(i)],
                                   lambda e, i=i, k=k, s=slg(i): e.dma_start(
                                       out=WGUPW(k, s), in_=d_wgu[i + 2]))
            if i == 48 and k == 0:
                W("gpsimd", t_pe, t_pe.n)
                for (tt, qq) in [(0, 0), (0, 1)]:
                    v_wd[(tt, qq)] = DMA(
                        "gpsimd", s_wrd[qq % 2],
                        lambda e, t=tt, q=qq, k=k: e.dma_start(
                            out=WRDW(k, q % 2),
                            in_=d_wd[t, q // 2, :, (q % 2) * 2048:
                                     (q % 2 + 1) * 2048]))
        if k == 1:
            W("gpsimd", t_pe, t_pe.n)
            W("gpsimd", t_vec, v_fuse_last)
            for (tt, qq) in [(0, 0), (0, 1)]:
                v_wd[(tt, qq)] = DMA(
                    "gpsimd", s_wrd[qq % 2],
                    lambda e, t=tt, q=qq, k=k: e.dma_start(
                        out=WRDW(k, q % 2),
                        in_=d_wd[t, q // 2, :, (q % 2) * 2048:
                                 (q % 2 + 1) * 2048]))
        W("tensor", t_vec, v_fuse_last)
        pe_acquire(banks[6 if k == 0 else 7])
        pe_acquire(banks[0 if k == 0 else 1])
        if k == 0:
            W("gpsimd", t_pe, t_pe.n)
            v_wgu_k1 = DMA("gpsimd", s_wrf[1],
                           lambda e: e.dma_start(out=WGUPW(1, 1),
                                                 in_=d_wgu[0]))
        if k == 1:
            act_table("natural_log_exp_and_others")
            ln_state["mb_busy"] = (t_vec, v_fuse_last)
            emit_final(0, v_fuse_last, ba=2, bb=3)
        sqd_prev = {0: 0, 1: 0}
        pend_sum = None
        for t in range(DT):
            bk = banks[4 + t % 2]
            pe_acquire(bk)
            for q in range(4):
                W("tensor", v_wd[(t, q)])
                for kk in range(16):
                    ki = q * 16 + kk
                    mm(pb[bk.idx][:, 0:512],
                       WRDW(k, q % 2)[:, kk * 128:(kk + 1) * 128],
                       FUSE(k, ki), ki == 0, ki == IT - 1)
                pre = (t + (q + 2) // 4, (q + 2) % 4)
                if pre[0] < DT and pre not in v_wd:
                    W("gpsimd", t_pe, t_pe.n)
                    v_wd[pre] = DMA(
                        "gpsimd", s_wrd[pre[1] % 2],
                        lambda e, p=pre, k=k: e.dma_start(
                            out=WRDW(k, p[1] % 2),
                            in_=d_wd[p[0], p[1] // 2, :, (p[1] % 2) * 2048:
                                     (p[1] % 2 + 1) * 2048]))
            bk.w_pe = t_pe.n
            if pend_sum is not None:
                pend_sum()
            sl = t % 2
            W("scalar", t_pe, bk.w_pe)
            W("scalar", t_pe, sqd_prev[sl])
            P("scalar", lambda e, t=t, k=k, b=bk.idx: e.activation(
                out=Y2W(k, t), in_=pb[b][:, 0:512], func=AF.Identity,
                scale=SDC[:, t:t + 1]))
            CB("scalar")
            P("scalar", lambda e, t=t, k=k, sl=sl: e.activation(
                out=SQT(sl)[:, 0:512], in_=Y2W(k, t), func=AF.Square))
            vsc = M("scalar")
            bk.r_trk = (t_scl, vsc)

            def pend_sum(t=t, k=k, sl=sl, vsc=vsc):
                W("tensor", t_scl, vsc)
                mm(pb[6 if k == 0 else 7][0:1, 0:512], ONESB, Y2W(k, t),
                   t == 0, t == DT - 1)
                sqd_prev[sl] = mm(pb[0 if k == 0 else 1][0:1, 0:512], ONESB,
                                  SQT(sl)[:, 0:512], t == 0, t == DT - 1)
        pend_sum()
        banks[6 if k == 0 else 7].w_pe = t_pe.n
        banks[0 if k == 0 else 1].w_pe = t_pe.n
        W("vector", t_pe, t_pe.n)
        P("vector", lambda e, k=k, l0=l0: e.tensor_copy(
            out=SRA[0:1, l0:l0 + 512], in_=pb[6 if k == 0 else 7][0:1, 0:512]))
        P("vector", lambda e, k=k, l0=l0: e.tensor_copy(
            out=SRB[0:1, l0:l0 + 512], in_=pb[0 if k == 0 else 1][0:1, 0:512]))
        vrf = M("vector")
        banks[6 if k == 0 else 7].r_trk = (t_vec, vrf)
        banks[0 if k == 0 else 1].r_trk = (t_vec, vrf)
        v_wd = {}
        v_rows_f[k] = t_vec.n
        if k == 0:
            tap("y2b", Y2AT)
        else:
            emit_final(1, v_rows_f[1])
    v_y2_all = t_scl.n

    W("gpsimd", s_out, s_out.n)
    if s_tap.n:
        W("gpsimd", s_tap, s_tap.n)

    with nc.Block() as block:
        @block.sync
        def _(e):
            for f in plan["sync"]:
                f(e)

        @block.gpsimd
        def _(e):
            for f in plan["gpsimd"]:
                f(e)

        @block.scalar
        def _(e):
            for f in plan["scalar"]:
                f(e)

        @block.vector
        def _(e):
            for f in plan["vector"]:
                f(e)

        @block.tensor
        def _(e):
            for f in plan["tensor"]:
                f(e)

    return nc



def _prep_shared(inp):
    f32 = np.float32
    bf = ml_dtypes.bfloat16

    def tile_kblocks(w, nblk):
        K = w.shape[0]
        out = np.empty((nblk, 128, K), w.dtype)
        for t in range(nblk):
            blk = w[:, t * 128:(t + 1) * 128]
            out[t] = blk.reshape(K // 128, 128, 128).transpose(1, 0, 2).reshape(
                128, K)
        return np.ascontiguousarray(out)

    w_in = np.asarray(inp["L0_attn_proj_in_w"], f32)
    w_out = np.asarray(inp["L0_attn_proj_out_w"], f32)
    sh = {}
    sh["w_in_t"] = tile_kblocks(np.ascontiguousarray(w_in.T), DT)
    sh["w_out_t"] = tile_kblocks(np.ascontiguousarray(w_out.T), DT)
    bwt = np.ascontiguousarray(np.asarray(inp["L0_attn_B_w"], f32).T)
    sh["b_w_t"] = np.ascontiguousarray(
        bwt.reshape(16, 128, 16).transpose(1, 0, 2).reshape(128, 256))
    sh["c_w_t"] = np.ascontiguousarray(np.asarray(inp["L0_attn_C_w"], f32).T)
    a = np.asarray(inp["L0_attn_A"], f32)
    sh["a_t"] = np.ascontiguousarray(
        a.reshape(16, 128, 16).transpose(1, 0, 2).reshape(128, 256))
    for nm, key in [("dcol", "L0_attn_D_param"), ("bincol", "L0_attn_proj_in_b"),
                    ("boutcol", "L0_attn_proj_out_b"), ("sdcol", "L0_ffn_s_down")]:
        sh[nm] = np.ascontiguousarray(np.asarray(inp[key], f32).reshape(16, 128).T)
    for nm, key in [("sgcol", "L0_ffn_s_gate"), ("sucol", "L0_ffn_s_up")]:
        sh[nm] = np.ascontiguousarray(np.asarray(inp[key], f32).reshape(64, 128).T)
    wg_t = tile_kblocks(np.asarray(inp["L0_ffn_w_gate"]), IT)
    wu_t = tile_kblocks(np.asarray(inp["L0_ffn_w_up"]), IT)
    sh["wgu_t"] = np.ascontiguousarray(
        np.concatenate([wg_t, wu_t], axis=2))
    wd = np.asarray(inp["L0_ffn_w_down"])
    wd_t = np.empty((DT, 2, 128, 4096), np.int8)
    for t in range(DT):
        for hf in range(2):
            blk = wd[hf * 4096:(hf + 1) * 4096, t * 128:(t + 1) * 128]
            wd_t[t, hf] = blk.reshape(32, 128, 128).transpose(1, 0, 2).reshape(
                128, 4096)
    sh["wd_t"] = np.ascontiguousarray(wd_t)
    rep = np.zeros((16, 128), bf)
    for c in range(128):
        rep[c % 16, c] = 1.0
    sh["rep16b"] = rep
    sh["ones_b"] = np.ones((128, 1), bf)
    sh["ones_f"] = np.ones((1, 128), f32)
    sh["ood_f"] = np.full((1, 128), 1.0 / D, f32)
    sh["ood_b"] = np.full((1, 128), 1.0 / D, bf)
    sh["ident_f"] = np.eye(128, dtype=f32)
    return sh


def _prep_core(x, b, h):
    f32 = np.float32
    bf = ml_dtypes.bfloat16
    xt = np.ascontiguousarray(x[b].T.astype(f32))
    if h == 0:
        sl = np.concatenate([np.zeros((D, PAD), f32), xt[:, 0:TOUT]], axis=1)
    else:
        sl = xt[:, TOUT - PAD:2 * TOUT]
    xT = np.ascontiguousarray(
        sl.reshape(16, 128, T).transpose(1, 0, 2).reshape(128, DT * T))
    mask = np.ones((16, T), bf)
    if h == 0:
        mask[:, 0:PAD] = 0
    return {"xT": xT, "maskb": mask}


def kernel(**inputs):
    x = np.asarray(inputs["L0_attn_x"], np.float32)
    sh = _prep_shared(inputs)
    in_maps = []
    for c in range(8):
        b, h = divmod(c, 2)
        m = dict(sh)
        m.update(_prep_core(x, b, h))
        in_maps.append(m)
    nc = build_program()
    res = run_bass_kernel_spmd(nc, in_maps, core_ids=list(range(8)))
    out = np.empty((4, 2048, 2048), np.float32)
    for c in range(8):
        b, h = divmod(c, 2)
        o = np.asarray(res.results[c]["out_xT"])
        ot = o.reshape(128, DT, TOUT).transpose(1, 0, 2).reshape(D, TOUT)
        out[b, h * TOUT:(h + 1) * TOUT, :] = ot.T
    return out


# version 07
# speedup vs baseline: 1.2868x; 1.0032x over previous
import contextlib
import numpy as np
import ml_dtypes
import concourse.bass as bass
import concourse.mybir as mybir
from concourse.bass_utils import run_bass_kernel_spmd

F32 = mybir.dt.float32
BF16 = mybir.dt.bfloat16
I8 = mybir.dt.int8
U8 = mybir.dt.uint8
AF = mybir.ActivationFunctionType
ALU = mybir.AluOpType

D = 2048
DT = 16
S = 16
TOUT = 1024
PAD = 64
T = PAD + TOUT
NO = 8
NG = 6
INTER = 8192
IT = 64
WIN3 = [(0, 512), (512, 1024), (1024, T)]
EPS = 1e-5
SBUF_BYTES = 212832

ENGINES = ("sync", "gpsimd", "scalar", "vector", "tensor")


class Trk:
    def __init__(self, name, step=1):
        self.name, self.step, self.n, self.h = name, step, 0, None


class Alloc:
    def __init__(self, nbytes):
        self.cap = nbytes
        self.items = []

    def pin(self, off, size, t0, t1, name):
        size = (size + 63) & ~63
        for (o, s, a0, a1, n) in self.items:
            if not (a1 <= t0 or t1 <= a0) and not (o + s <= off or off + size <= o):
                raise MemoryError(f"pin clash: {name} vs {n}")
        self.items.append((off, size, t0, t1, name))
        return off

    def alloc(self, size, t0, t1, name):
        size = (size + 63) & ~63
        spans = sorted((o, o + s) for (o, s, a0, a1, _n) in self.items
                       if not (a1 <= t0 or t1 <= a0))
        holes, cur = [], 0
        for (s0, s1) in spans:
            if s0 > cur:
                holes.append((cur, s0))
            cur = max(cur, s1)
        if cur < self.cap:
            holes.append((cur, self.cap))
        cand = [h for h in holes if h[1] - h[0] >= size]
        if not cand:
            raise MemoryError(f"SBUF overflow: {name} {size}B t[{t0},{t1}]")
        off = min(cand, key=lambda h: (h[1] - h[0], h[0]))[0]
        self.items.append((off, size, t0, t1, name))
        return off


def build_program(taps=()):
    nc = bass.Bass("TRN2", target_bir_lowering=False, debug=False)

    d_x = nc.dram_tensor("xT", [128, DT * T], F32, kind="ExternalInput").ap()
    d_win = nc.dram_tensor("w_in_t", [DT, 128, 2048], F32, kind="ExternalInput").ap()
    d_wout = nc.dram_tensor("w_out_t", [DT, 128, 2048], F32, kind="ExternalInput").ap()
    d_bwt = nc.dram_tensor("b_w_t", [128, 256], F32, kind="ExternalInput").ap()
    d_cwt = nc.dram_tensor("c_w_t", [16, 2048], F32, kind="ExternalInput").ap()
    d_a = nc.dram_tensor("a_t", [128, 256], F32, kind="ExternalInput").ap()
    d_dcol = nc.dram_tensor("dcol", [128, 16], F32, kind="ExternalInput").ap()
    d_bin = nc.dram_tensor("bincol", [128, 16], F32, kind="ExternalInput").ap()
    d_bout = nc.dram_tensor("boutcol", [128, 16], F32, kind="ExternalInput").ap()
    d_sd = nc.dram_tensor("sdcol", [128, 16], F32, kind="ExternalInput").ap()
    d_sg = nc.dram_tensor("sgcol", [128, 64], F32, kind="ExternalInput").ap()
    d_su = nc.dram_tensor("sucol", [128, 64], F32, kind="ExternalInput").ap()
    d_wgu = nc.dram_tensor("wgu_t", [IT, 128, 4096], I8, kind="ExternalInput").ap()
    d_wd = nc.dram_tensor("wd_t", [DT, 2, 128, 4096], I8, kind="ExternalInput").ap()
    d_mask = nc.dram_tensor("maskb", [16, T], BF16, kind="ExternalInput").ap()
    d_rep16 = nc.dram_tensor("rep16b", [16, 128], BF16, kind="ExternalInput").ap()
    d_ones_b = nc.dram_tensor("ones_b", [128, 1], BF16, kind="ExternalInput").ap()
    d_ones_f = nc.dram_tensor("ones_f", [1, 128], F32, kind="ExternalInput").ap()
    d_oodf = nc.dram_tensor("ood_f", [1, 128], F32, kind="ExternalInput").ap()
    d_oodb = nc.dram_tensor("ood_b", [1, 128], BF16, kind="ExternalInput").ap()
    d_identf = nc.dram_tensor("ident_f", [128, 128], F32, kind="ExternalInput").ap()
    d_out = nc.dram_tensor("out_xT", [128, DT * TOUT], F32, kind="ExternalOutput").ap()
    tap_out = {}
    for tname, tshape, tdt in taps:
        tap_out[tname] = nc.dram_tensor("tap_" + tname, tshape, tdt,
                                        kind="ExternalOutput").ap()

    al = Alloc(SBUF_BYTES)
    o_x = al.pin(0, DT * T * 4, 0, 12, "x")
    o_scr = al.pin(192512, 1024 * 4, 0, 12, "scratch")
    o_sq = al.pin(196608, 4 * 512 * 2, 0, 12, "sqscr")
    o_srown = al.pin(200704, 2 * 1024 * 2, 4, 12, "sumrows")
    o_bcb = al.pin(204800, 2 * 1024, 5, 12, "bcastB")
    o_onesf = al.pin(206848, 128 * 4 + 256, 0, 12, "onesf+oodb")
    o_const = al.pin(207616, 2 * 64 + 2 * 256 + 64, 0, 12, "consts")
    o_fuseA = al.pin(69632, IT * 512 * 2, 6, 8, "fuseA")
    o_wri = al.pin(163840, 2 * 4096, 0, 2, "wring_in")
    o_srow = al.pin(172032, 2 * T * 2, 0, 2, "sumrows_wide")
    o_wro = al.pin(172032, 2 * 4096, 5, 6, "wring_out")
    o_cwt = al.pin(172032, 2048 * 2, 4, 5, "cwtb")
    o_ypre0 = al.pin(139264, DT * TOUT * 2, 4, 6, "ypre")
    o_ypre = [o_ypre0 + t * 2048 for t in range(DT)]
    o_xn1 = al.pin(69632, DT * T * 2, 0, 6, "xn1b")
    o_xpb = al.pin(104448, DT * T * 2, 1, 6, "x_projb/y_attnb")
    o_pkf = al.pin(104448, 2 * 8192, 0, 1, "pkf")
    o_pk = al.pin(139264, DT * NG * 128 * 2, 0, 4, "packs")
    o_fuseB = al.pin(69632, IT * 512 * 2, 8, 11, "fuseB")
    o_xy1 = al.pin(135168, DT * 512 * 2, 6, 10, "x2na+y2a")
    o_x2na = o_xy1
    o_y2a = o_xy1
    o_xy2 = al.pin(151552, DT * 512 * 2, 8, 12, "x2nb+y2b")
    o_x2nb = o_xy2
    o_y2b = o_xy2
    o_wguA = al.pin(151552, 2 * 8192, 6, 8, "wguA")
    o_wguB = al.pin(167936, 8192, 8, 11, "wguB_s0")
    al.pin(176128, 8192, 7, 11, "wguB_s1")
    o_wrdA = al.pin(167936, 2 * 4096, 6, 8, "wrdA")
    o_wrdB = al.pin(184320, 2 * 4096, 9, 12, "wrdB")
    o_gh = al.pin(163840, NG * 576 * 4, 3, 4, "ghat")
    o_dr = al.pin(177664, T * 4, 2, 4, "deltar")
    o_acc1 = al.pin(182016, 520 * 4, 3, 4, "acc1")
    o_shb = al.pin(184128, TOUT * 2, 3, 5, "S_hb")
    o_bca = al.pin(176384, 2048 + 1024, 0, 2, "bcastA")
    o_mask = al.pin(182016, T * 2, 0, 3, "maskb")
    o_db = al.pin(163840, T * 2, 2, 3, "deltab")
    o_sig = al.pin(120832, 2 * 1024, 0, 1, "sig")
    o_mbf1 = al.pin(180224, 2048, 5, 7, "mbf_w0")
    o_mbf2 = al.pin(184320, 2048, 5, 9, "mbf_w1")
    o_bcb2 = al.pin(186368, 1024, 5, 9, "bcastB2")
    o_bwt = al.pin(184192, 256 * 2, 0, 3, "bwtb")
    o_rep16 = al.pin(184704, 256, 0, 3, "rep16")
    o_dcol = al.pin(186688, 64, 0, 5, "dcol")
    o_binc = al.pin(179456, 64, 0, 2, "binc")
    o_identf = al.pin(186176, 128 * 4, 0, 4, "identf")
    o_sq4 = al.pin(208320, 512 * 2, 0, 12, "sqt4")
    import os as _os
    if _os.environ.get("KDEBUG_ALLOC"):
        for (o, s, a0, a1, n) in sorted(al.items):
            print(f"{o:7d} +{s:6d} t[{a0:2d},{a1:2d}] {n}")
        raise SystemExit(0)

    plan = {e: [] for e in ENGINES}

    def Praw(eng, fn):
        plan[eng].append(fn)

    t_pe, t_vec, t_scl = Trk("t_pe"), Trk("t_vec"), Trk("t_scl")
    s_const = Trk("s_const", 16)
    s_x = [Trk(f"s_x{i}", 16) for i in range(3)]
    s_bwt, s_cwt = Trk("s_bwt", 16), Trk("s_cwt", 16)
    s_wri = [Trk(f"s_wri{i}", 16) for i in range(2)]
    s_wro = [Trk(f"s_wro{i}", 16) for i in range(2)]
    s_wrf = [Trk(f"s_wrf{i}", 16) for i in range(4)]
    s_wrd = [Trk(f"s_wrd{i}", 16) for i in range(2)]
    s_out, s_tap = Trk("s_out", 16), Trk("s_tap", 16)
    DMAT = ([s_const] + s_x + [s_bwt, s_cwt] + s_wri + s_wro + s_wrf + s_wrd
            + [s_out, s_tap])
    TRKS = [t_pe, t_vec, t_scl] + DMAT
    ETRK = {"tensor": t_pe, "vector": t_vec, "scalar": t_scl}

    def DMA(queue, trk, build):
        trk.n += trk.step
        plan[queue].append(lambda e, b=build, t=trk: b(e).then_inc(t.h, t.step))
        return (trk, trk.n)

    def W(eng, trk, val=None):
        if val is None:
            trk, val = trk
        if val and val > 0:
            Praw(eng, lambda e, t=trk, v=val: e.wait_ge(t.h, v))

    def P(eng, fn):
        trk = ETRK[eng]
        trk.n += trk.step
        plan[eng].append(lambda e, b=fn, t=trk: b(e).then_inc(t.h, t.step))
        return trk.n

    def E(eng, build, inc=False):
        if inc:
            return P(eng, build)
        Praw(eng, build)
        return None

    def M(eng):
        return ETRK[eng].n

    def CB(eng):
        trk = ETRK[eng]
        Praw(eng, lambda e, t=trk, v=trk.n: e.wait_ge(t.h, v))

    def BAR():
        for eng in ("tensor", "vector", "scalar"):
            for trk in (t_pe, t_vec, t_scl):
                if trk.n > 0:
                    Praw(eng, lambda e, t=trk, v=trk.n: e.wait_ge(t.h, v))

    class Bank:
        def __init__(self, idx):
            self.idx, self.w_pe, self.r_trk = idx, 0, None

    banks = [Bank(i) for i in range(8)]

    def pe_acquire(bk):
        if bk.r_trk is not None:
            W("tensor", bk.r_trk[0], bk.r_trk[1])

    def mm(out_ap, lhsT, rhs, start, stop, inc=True):
        b = (lambda e, o=out_ap, l=lhsT, r=rhs, st=start, sp=stop:
             e.matmul(o, l, r, start=st, stop=sp))
        t_pe.n += 1
        plan["tensor"].append(lambda e, bb=b: bb(e).then_inc(t_pe.h, 1))
        return t_pe.n

    def act_table(set_name):
        from concourse.hw_specs import get_activation_tables
        sid = list(get_activation_tables(nc.m.arch).keys()).index(set_name)

        Praw("scalar", lambda e, sid=sid: e.add_instruction(
            mybir.InstLoadActFuncSet(
                name=nc.get_next_instruction_name(), act_func_set_id=sid,
                ins=[], outs=[])))
        for _ in range(12):
            P("scalar", lambda e: e.copy(out=DLY, in_=DLY))
            CB("scalar")

    es = contextlib.ExitStack()
    nc._kernel_es = es
    ar = es.enter_context(nc.sbuf_tensor("arena", [128, SBUF_BYTES], U8))
    pb = [es.enter_context(nc.psum_tensor(f"pb{i}", [128, 512], F32))
          for i in range(8)]
    for trk in TRKS:
        trk.h = es.enter_context(nc.semaphore(trk.name))

    arena_base = None
    for a in nc.m.functions[0].allocations:
        mls = getattr(a, "memorylocations", None)
        if not mls:
            continue
        if mls[0].name == "arena":
            arena_base = mls[0].addr
    assert arena_base is not None

    def REG(name, off, part, cols, dtype):
        return nc.alloc_sbuf_tensor_at(name, [part, cols], dtype,
                                       offset=arena_base + off).ap()

    XT = REG("XT", o_x, 128, DT * T, F32)
    XN1T = REG("XN1T", o_xn1, 128, DT * T, BF16)
    XPBT = REG("XPBT", o_xpb, 128, DT * T, BF16)
    PKT = REG("PKT", o_pk, 128, DT * NG * 128, BF16)
    GHT = REG("GHT", o_gh, 128, NG * 576, F32)
    DR = REG("DRT", o_dr, 128, T, F32)
    DB = REG("DBT", o_db, 16, T, BF16)
    MASK = REG("MASKT", o_mask, 16, T, BF16)
    ACC1 = REG("ACC1T", o_acc1, 128, 520, F32)
    SHB = REG("SHBT", o_shb, 16, TOUT, BF16)
    SCRT = REG("SCRT", o_scr, 128, 1024, F32)
    SCR = SCRT
    SRT = REG("SRT", o_srow, 1, 2 * T, BF16)
    SRA1 = SRT[:, 0:T]
    SRB1 = SRT[:, T:2 * T]
    SRNT = REG("SRNT", o_srown, 1, 2048, BF16)
    SRA = SRNT[:, 0:1024]
    SRB = SRNT[:, 1024:2048]
    CHS = SCRT[:, 0:512]
    CH2 = SCRT[:, 512:1024]
    GS = SCRT[:, 0:512]
    SQTT = REG("SQTT", o_sq, 128, 2048, BF16)
    SQT4 = REG("SQT4", o_sq4, 128, 512, BF16)

    def SQT(i):
        return SQTT[:, i * 512:(i + 1) * 512]

    DLY = SQTT[:, 0:128]

    MB_F = REG("MBF1", o_mbf1, 128, 512, F32)
    MB_F2 = REG("MBF2", o_mbf2, 128, 512, F32)
    INVBT = REG("INVBT", o_bcb, 128, 1024, BF16)
    INVB = INVBT[:, 0:512]
    MB_B = INVBT[:, 512:1024]
    INVB2 = REG("INVB2", o_bcb2, 128, 512, BF16)
    MB1F = REG("MB1F", o_bca, 128, 512, F32)
    INV1B = REG("INV1B", o_bca + 2048, 128, 512, BF16)
    WRIT = REG("WRIT", o_wri, 128, 4096, BF16)
    WROT = REG("WROT", o_wro, 128, 4096, BF16)
    WGUA = REG("WGUA", o_wguA, 128, 8192, BF16)
    WGUB = REG("WGUB", o_wguB, 128, 8192, BF16)

    def WRI(i):
        return WRIT[:, i * 2048:(i + 1) * 2048]

    def WRO(i):
        return WROT[:, i * 2048:(i + 1) * 2048]

    BWTB = REG("BWTT", o_bwt, 128, 256, BF16)
    CWTB = REG("CWTT", o_cwt, 16, 2048, BF16)
    oc = o_const
    DCOL = REG("DCOL", o_dcol, 128, 16, F32)
    BINC = REG("BINC", o_binc, 128, 16, F32)
    BOUTC = REG("BOUTC", oc, 128, 16, F32); oc += 64
    SDC = REG("SDC", oc, 128, 16, F32); oc += 64
    SGC = REG("SGC", oc, 128, 64, F32); oc += 256
    SUC = REG("SUC", oc, 128, 64, F32); oc += 256
    REP16 = REG("REP16", o_rep16, 16, 128, BF16)
    ONESB = REG("ONESB", oc, 128, 1, BF16); oc += 64
    ONESF = REG("ONESF", o_onesf, 1, 128, F32)
    OODB = REG("OODB", o_onesf + 512, 1, 128, BF16)
    IDENTF = REG("IDENTF", o_identf, 128, 128, F32)
    SIGA = REG("SIGA", o_sig, 128, 256, F32)
    SIG8 = REG("SIG8", o_sig + 1024, 128, 256, F32)
    PKF = REG("PKF", o_pkf, 128, 2048, F32)
    S8R = REG("S8R", o_pkf + 8192, 128, 2048, F32)
    YPRET = [REG(f"YPRE{t}", o_ypre[t], 128, TOUT, BF16) for t in range(DT)]
    FUSETA = REG("FUSETA", o_fuseA, 128, IT * 512, BF16)
    FUSETB = REG("FUSETB", o_fuseB, 128, IT * 512, BF16)
    X2NAT = REG("X2NAT", o_x2na, 128, DT * 512, BF16)
    WRDAT = REG("WRDAT", o_wrdA, 128, 4096, BF16)
    WRDBT = REG("WRDBT", o_wrdB, 128, 4096, BF16)
    X2NBT = REG("X2NBT", o_x2nb, 128, DT * 512, BF16)
    Y2AT = REG("Y2AT", o_y2a, 128, DT * 512, BF16)
    Y2BT = REG("Y2BT", o_y2b, 128, DT * 512, BF16)

    def Xj(j, c0, c1):
        return XT[:, j * T + c0:j * T + c1]

    def XN1j(j, c0, c1):
        return XN1T[:, j * T + c0:j * T + c1]

    def XPBj(j, c0, c1):
        return XPBT[:, j * T + c0:j * T + c1]

    def YATT(j, l0, l1):
        return XPBT[:, j * T + l0:j * T + l1]

    def PACK(j, g):
        return PKT[:, (j * NG + g) * 128:(j * NG + g + 1) * 128]

    def GH(g, c0, c1):
        return GHT[:, g * 576 + c0:g * 576 + c1]

    def YPRE(t, l0, l1):
        return YPRET[t][:, l0:l1]

    def FUSE(w, i):
        t = FUSETA if w == 0 else FUSETB
        return t[:, i * 512:(i + 1) * 512]

    def X2NW(k, j):
        t = X2NAT if k == 0 else X2NBT
        return t[:, j * 512:(j + 1) * 512]

    def Y2W(k, j):
        t = Y2AT if k == 0 else Y2BT
        return t[:, j * 512:(j + 1) * 512]

    def tap(name, view):
        if name not in tap_out:
            return
        W("sync", t_vec, t_vec.n)
        W("sync", t_scl, t_scl.n)
        W("sync", t_pe, t_pe.n)
        DMA("sync", s_tap, lambda e, d=tap_out[name], v=view:
            e.dma_start(out=d, in_=v))
        for eng in ("vector", "scalar", "tensor", "gpsimd"):
            W(eng, s_tap, s_tap.n)

    for (dst, src) in [(SIGA, d_a), (DCOL, d_dcol), (BINC, d_bin),
                       (BOUTC, d_bout), (SDC, d_sd), (SGC, d_sg), (SUC, d_su),
                       (MASK, d_mask), (REP16, d_rep16), (ONESB, d_ones_b),
                       (ONESF, d_ones_f), (OODB, d_oodb),
                       (IDENTF, d_identf)]:
        DMA("sync", s_const, lambda e, d=dst, s=src: e.dma_start(out=d, in_=s))
    v_consts = (s_const, s_const.n)
    d_x3 = d_x.rearrange("p (j c) -> p j c", j=DT)
    x_ar3 = XT.rearrange("p (j c) -> p j c", j=DT)
    LN1_ORDER = [2, 0, 1]
    v_xw = {}
    for wi in LN1_ORDER:
        w0, w1 = WIN3[wi]
        v_xw[wi] = DMA("sync", s_x[wi],
                       lambda e, a=x_ar3[:, :, w0:w1], s=d_x3[:, :, w0:w1]:
                       e.dma_start(out=a, in_=s))
    v_bwt = DMA("gpsimd", s_bwt, lambda e: e.dma_start(out=BWTB, in_=d_bwt))
    v_win = {}
    for t in range(2):
        v_win[t] = DMA("gpsimd", s_wri[t % 2],
                       lambda e, t=t: e.dma_start(out=WRI(t % 2), in_=d_win[t]))

    P("vector", lambda e: e.memset(DLY, 0.0))
    v_dly = M("vector")
    W("scalar", t_vec, v_dly)
    act_table("natural_log_exp_and_others")

    W("vector", v_consts)
    W("scalar", v_consts)
    P("scalar", lambda e: e.activation(out=SCR[:, 0:256], in_=SIGA, func=AF.Exp,
                                       scale=-1.0))
    v_expA = M("scalar")
    W("vector", t_scl, v_expA)
    P("vector", lambda e: e.tensor_scalar_add(out=SCR[:, 0:256],
                                              in0=SCR[:, 0:256], scalar1=1.0))
    CB("vector")
    P("vector", lambda e: e.reciprocal(out=SIGA, in_=SCR[:, 0:256]))
    CB("vector")
    P("vector", lambda e: e.tensor_mul(out=SCR[:, 0:256], in0=SIGA, in1=SIGA))
    CB("vector")
    P("vector", lambda e: e.tensor_mul(out=SCR[:, 256:512], in0=SCR[:, 0:256],
                                       in1=SCR[:, 0:256]))
    CB("vector")
    P("vector", lambda e: e.tensor_mul(out=SIG8, in0=SCR[:, 256:512],
                                       in1=SCR[:, 256:512]))
    CB("vector")
    s8r4 = S8R.rearrange("p (j o s) -> p j o s", j=16, o=8)
    sig8_3 = SIG8.rearrange("p (j s) -> p j s", j=16)
    for o in range(NO):
        P("vector", lambda e, o=o: e.tensor_copy(out=s8r4[:, :, o, :], in_=sig8_3))
    CB("vector")
    pkf4 = PKF.rearrange("p (j o s) -> p j o s", j=16, o=8)
    siga3 = SIGA.rearrange("p (j s) -> p j s", j=16)
    P("vector", lambda e: e.memset(pkf4[:, :, 0, :], 1.0))
    CB("vector")
    for o in range(1, NO):
        P("vector", lambda e, o=o: e.tensor_tensor(
            out=pkf4[:, :, o, :], in0=pkf4[:, :, o - 1, :], in1=siga3,
            op=ALU.mult))
        CB("vector")
    v_pkf0 = M("vector")
    pk5 = PKT.rearrange("p (j g c) -> p j g c", j=DT, g=NG)
    pkf3 = PKF.rearrange("p (j c) -> p j c", j=16)
    W("scalar", t_vec, v_pkf0)
    for g in range(NG):
        P("scalar", lambda e, g=g: e.copy(out=pk5[:, :, g, :], in_=pkf3))
        if g < NG - 1:
            vv = M("scalar")
            W("vector", t_scl, vv)
            P("vector", lambda e: e.tensor_mul(out=PKF, in0=PKF, in1=S8R))
            CB("vector")
            vv2 = M("vector")
            W("scalar", t_vec, vv2)
    M("scalar")
    v_pack_vec = t_vec.n

    ln_state = {"mb_busy": (t_vec, 0)}

    def emit_ln(windows, src_fn, src_is_f32, outs, apply_fn, wait_src):
        for wi, (c0, c1) in enumerate(windows):
            wl = c1 - c0
            pe_acquire(banks[6])
            pe_acquire(banks[7])
            pe_j = {-2: t_pe.n, -1: t_pe.n}
            if wait_src is not None:
                ws = wait_src if isinstance(wait_src, list) else [wait_src]
                for pair in ([ws[wi]] if len(ws) == len(windows) else ws):
                    W("scalar", pair)
            for j in range(DT):
                W("scalar", t_pe, pe_j[j - 2])
                if src_is_f32:
                    sl = (j % 2) * 2
                    P("scalar", lambda e, j=j, sl=sl, c0=c0, c1=c1:
                      e.copy(out=SQT(sl)[:, 0:c1 - c0], in_=src_fn(j, c0, c1)))
                    P("scalar", lambda e, j=j, sl=sl, c0=c0, c1=c1:
                      e.activation(out=SQT(sl + 1)[:, 0:c1 - c0],
                                   in_=src_fn(j, c0, c1), func=AF.Square))
                    rhs_sum, rhs_sq = SQT(sl)[:, 0:wl], SQT(sl + 1)[:, 0:wl]
                else:
                    sl = j % 2
                    P("scalar", lambda e, j=j, sl=sl, c0=c0, c1=c1:
                      e.activation(out=SQT(sl)[:, 0:c1 - c0],
                                   in_=src_fn(j, c0, c1), func=AF.Square))
                    rhs_sum, rhs_sq = src_fn(j, c0, c1), SQT(sl)[:, 0:wl]
                vs = M("scalar")
                W("tensor", t_scl, vs)
                mm(pb[6][0:1, 0:wl], ONESB, rhs_sum, j == 0, j == DT - 1)
                pe_j[j] = mm(pb[7][0:1, 0:wl], ONESB, rhs_sq,
                             j == 0, j == DT - 1, inc=True)
            banks[6].w_pe = banks[7].w_pe = t_pe.n
            W("vector", t_pe, t_pe.n)
            P("vector", lambda e, c0=c0, c1=c1, wl=wl: e.tensor_copy(
                out=SRA1[0:1, c0:c1], in_=pb[6][0:1, 0:wl]))
            P("vector", lambda e, c0=c0, c1=c1, wl=wl: e.tensor_copy(
                out=SRB1[0:1, c0:c1], in_=pb[7][0:1, 0:wl]))
            vv = M("vector")
            banks[6].r_trk = banks[7].r_trk = (t_vec, vv)
            emit_ln_chain([(c0, c1)], [outs[wi]],
                          (None if apply_fn is None else
                           lambda _w, a, b, v, wi=wi: apply_fn(wi, a, b, v)),
                          vv, SRA1, SRB1)

    def emit_ln_chain(windows, outs, apply_fn, v_rows, ra=None, rb=None,
                      ba=6, bb=7):
        ra = SRA if ra is None else ra
        rb = SRB if rb is None else rb
        ood = OODB
        for wi, (c0, c1) in enumerate(windows):
            wl = c1 - c0
            mbf, mbb, invb = outs[wi]
            W("tensor", t_vec, v_rows)
            pe_acquire(banks[ba])
            mm(pb[ba][:, 0:wl], ood, ra[0:1, c0:c1], True, True, inc=True)
            banks[ba].w_pe = t_pe.n
            W("scalar", t_pe, banks[ba].w_pe)
            W("scalar", ln_state["mb_busy"][0], ln_state["mb_busy"][1])
            CB("scalar")
            P("scalar", lambda e, wl=wl, ba=ba: e.copy(out=CHS[:, 0:wl],
                                                       in_=pb[ba][:, 0:wl]))
            if mbf is not None:
                P("scalar", lambda e, d=mbf, wl=wl, ba=ba: e.copy(
                    out=d[:, 0:wl], in_=pb[ba][:, 0:wl]))
            if mbb is not None:
                P("scalar", lambda e, d=mbb, wl=wl, ba=ba: e.copy(
                    out=d[:, 0:wl], in_=pb[ba][:, 0:wl]))
            v_m = M("scalar")
            banks[ba].r_trk = (t_scl, v_m)
            pe_acquire(banks[bb])
            mm(pb[bb][:, 0:wl], ood, rb[0:1, c0:c1], True, True, inc=True)
            banks[bb].w_pe = t_pe.n
            W("vector", t_scl, v_m)
            P("vector", lambda e, wl=wl: e.tensor_mul(
                out=CH2[:, 0:wl], in0=CHS[:, 0:wl], in1=CHS[:, 0:wl]))
            CB("vector")
            W("vector", t_pe, banks[bb].w_pe)
            P("vector", lambda e, wl=wl, bb=bb: e.tensor_sub(
                out=CHS[:, 0:wl], in0=pb[bb][:, 0:wl], in1=CH2[:, 0:wl]))
            CB("vector")
            P("vector", lambda e, wl=wl: e.tensor_scalar_add(
                out=CHS[:, 0:wl], in0=CHS[:, 0:wl], scalar1=EPS))
            vv2 = M("vector")
            banks[bb].r_trk = (t_vec, vv2)
            W("scalar", t_vec, vv2)
            P("scalar", lambda e, wl=wl: e.activation(
                out=CHS[:, 0:wl], in_=CHS[:, 0:wl], func=AF.Ln))
            CB("scalar")
            P("scalar", lambda e, d=invb, wl=wl: e.activation(
                out=d[:, 0:wl], in_=CHS[:, 0:wl], func=AF.Exp, scale=-0.5))
            v_chain = M("scalar")
            if apply_fn is not None:
                apply_fn(wi, c0, c1, v_chain)

    v_xn1_w = {}

    def ln1_apply(wi, c0, c1, v_chain):
        wl = c1 - c0
        W("vector", t_scl, v_chain)
        for j in range(DT):
            P("vector", lambda e, j=j, c0=c0, c1=c1, wl=wl: e.tensor_sub(
                out=SQT4[:, 0:wl], in0=Xj(j, c0, c1), in1=MB1F[:, 0:wl]))
            CB("vector")
            P("vector", lambda e, j=j, c0=c0, c1=c1, wl=wl: e.tensor_mul(
                out=XN1j(j, c0, c1), in0=SQT4[:, 0:wl], in1=INV1B[:, 0:wl]))
            CB("vector")
        v_xn1_w[c0] = M("vector")
        ln_state["mb_busy"] = (t_vec, v_xn1_w[c0])

    emit_ln([WIN3[i] for i in LN1_ORDER], lambda j, c0, c1: Xj(j, c0, c1),
            True, [(MB1F, None, INV1B)] * 3, ln1_apply,
            [v_xw[i] for i in LN1_ORDER])
    v_xn1_done = ln_state["mb_busy"][1]
    tap("xn1b", XN1T)

    for t in range(DT):
        W("tensor", v_win[t])
        for wi, (w0, w1) in enumerate(WIN3):
            if t == 0:
                W("tensor", t_vec, v_xn1_w[w0])
            wl = w1 - w0
            bk = banks[(t * 3 + wi) % 4]
            pe_acquire(bk)
            for j in range(DT):
                mm(pb[bk.idx][:, 0:wl], WRI(t % 2)[:, j * 128:(j + 1) * 128],
                   XN1j(j, w0, w1), j == 0, j == DT - 1, inc=(j == DT - 1))
            bk.w_pe = t_pe.n
            W("scalar", t_pe, bk.w_pe)
            if t == 0 and wi == 0:
                W("scalar", t_vec, v_pack_vec)
            P("scalar", lambda e, t=t, w0=w0, w1=w1, wl=wl, b=bk.idx:
              e.activation(out=XPBj(t, w0, w1), in_=pb[b][:, 0:wl],
                           func=AF.Identity, bias=BINC[:, t:t + 1]))
            vv = M("scalar")
            bk.r_trk = (t_scl, vv)
        if t + 2 < DT:
            W("gpsimd", t_pe, t_pe.n)
            v_win[t + 2] = DMA("gpsimd", s_wri[t % 2],
                               lambda e, t=t: e.dma_start(out=WRI(t % 2),
                                                          in_=d_win[t + 2]))
    v_xpb_all = t_scl.n
    tap("x_projb", XPBT)

    BAR()
    W("tensor", v_bwt)
    W("tensor", t_scl, v_xpb_all)
    for wi, (w0, w1) in enumerate(WIN3):
        wl = w1 - w0
        bk = banks[4 + (wi % 2)]
        pe_acquire(bk)
        for j in range(DT):
            mm(pb[bk.idx][0:16, 0:wl], BWTB[:, j * 16:(j + 1) * 16],
               XPBj(j, w0, w1), j == 0, j == DT - 1, inc=(j == DT - 1))
        bk.w_pe = t_pe.n
        W("scalar", t_pe, bk.w_pe)
        P("scalar", lambda e, b=bk.idx, wl=wl: e.activation(
            out=SCR[0:16, 0:wl], in_=pb[b][0:16, 0:wl], func=AF.Exp))
        CB("scalar")
        P("scalar", lambda e, wl=wl: e.activation(
            out=SCR[0:16, 512:512 + wl], in_=SCR[0:16, 0:wl], func=AF.Ln,
            bias=1.0))
        CB("scalar")
        P("scalar", lambda e, w0=w0, w1=w1, wl=wl: e.copy(
            out=DB[:, w0:w1], in_=SCR[0:16, 512:512 + wl]))
        vv = M("scalar")
        bk.r_trk = (t_scl, vv)
    W("vector", t_scl, t_scl.n)
    P("vector", lambda e: e.tensor_mul(out=DB, in0=DB, in1=MASK))
    v_dbm = M("vector")
    W("tensor", t_vec, v_dbm)
    for wi, (w0, w1) in enumerate(WIN3):
        wl = w1 - w0
        bk = banks[4 + (wi % 2)]
        pe_acquire(bk)
        mm(pb[bk.idx][:, 0:wl], REP16, DB[:, w0:w1], True, True, inc=True)
        bk.w_pe = t_pe.n
        W("scalar", t_pe, bk.w_pe)
        P("scalar", lambda e, b=bk.idx, w0=w0, w1=w1, wl=wl: e.copy(
            out=DR[:, w0:w1], in_=pb[b][:, 0:wl]))
        vv = M("scalar")
        bk.r_trk = (t_scl, vv)
    v_dr = t_scl.n
    tap("deltab", DB)

    BAR()
    W("tensor", t_scl, v_dr)
    v_stageA_prev = 0
    for k in range(2):
        gbase = 512 * k
        pieces = [(gbase, gbase + 512), (gbase + 512, min(gbase + 576, T))]
        if k == 1:
            W("tensor", t_vec, v_stageA_prev)
        for g in range(NG):
            for pi, (p0, p1) in enumerate(pieces):
                pl = p1 - p0
                bk = banks[(g * 2 + pi) % 4]
                pe_acquire(bk)
                for j in range(DT):
                    mm(pb[bk.idx][:, 0:pl], PACK(j, g), XPBj(j, p0, p1),
                       j == 0, j == DT - 1, inc=(j == DT - 1))
                bk.w_pe = t_pe.n
                W("vector", t_pe, bk.w_pe)
                P("vector", lambda e, g=g, p0=p0, p1=p1, pl=pl, b=bk.idx,
                  gb=gbase: e.tensor_tensor(
                      out=GH(g, p0 - gb, p1 - gb), in0=pb[b][:, 0:pl],
                      in1=DR[:, p0:p1], op=ALU.mult))
                vv = M("vector")
                bk.r_trk = (t_vec, vv)
            CB("vector")
            if g == 0:
                P("vector", lambda e: e.tensor_copy(
                    out=ACC1[:, 0:519], in_=GH(0, 57, 576)))
            else:
                c0 = 57 - 8 * g
                P("vector", lambda e, c0=c0, g=g: e.tensor_add(
                    out=ACC1[:, 0:519], in0=ACC1[:, 0:519],
                    in1=GH(g, c0, c0 + 519)))
            CB("vector")
        v_stageA = M("vector")
        v_stageA_prev = v_stageA
        if k == 1:
            W("gpsimd", t_vec, v_stageA)
            v_cwt = DMA("gpsimd", s_cwt,
                        lambda e: e.dma_start(out=CWTB, in_=d_cwt))
        bkS = banks[4 + k % 2]
        W("tensor", t_vec, v_stageA)
        pe_acquire(bkS)
        for o in range(NO):
            mm(pb[bkS.idx][0:16, 0:512], IDENTF[:, 16 * o:16 * o + 16],
               ACC1[:, 7 - o:519 - o], o == 0, o == NO - 1, inc=(o == NO - 1))
        bkS.w_pe = t_pe.n
        W("vector", t_pe, bkS.w_pe)
        P("vector", lambda e, k=k, b=bkS.idx: e.tensor_copy(
            out=SHB[:, 512 * k:512 * (k + 1)], in_=pb[b][0:16, 0:512]))
        v_stageB = M("vector")
        bkS.r_trk = (t_vec, v_stageB)
    CB("vector")
    v_shb = t_vec.n
    tap("S_h", SHB)

    BAR()
    W("tensor", v_cwt)
    W("tensor", t_vec, v_shb)
    v_dscr_read = 0
    for t in range(DT):
        for wi in range(2):
            l0 = wi * 512
            bk = banks[(t * 2 + wi) % 4]
            pe_acquire(bk)
            mm(pb[bk.idx][:, 0:512], CWTB[:, t * 128:(t + 1) * 128],
               SHB[:, l0:l0 + 512], True, True, inc=True)
            bk.w_pe = t_pe.n
            W("scalar", t_vec, v_dscr_read)
            P("scalar", lambda e, t=t, l0=l0: e.activation(
                out=SCR[:, 0:512], in_=XPBj(t, PAD + l0, PAD + l0 + 512),
                func=AF.Copy, scale=DCOL[:, t:t + 1]))
            vsc = M("scalar")
            W("vector", t_pe, bk.w_pe)
            W("vector", t_scl, vsc)
            P("vector", lambda e, t=t, l0=l0, b=bk.idx: e.tensor_add(
                out=YPRE(t, l0, l0 + 512), in0=pb[b][:, 0:512],
                in1=SCR[:, 0:512]))
            vv = M("vector")
            bk.r_trk = (t_vec, vv)
            v_dscr_read = vv
    v_ypre = t_vec.n
    tap("ypre0", YPRET[0])

    BAR()
    WL2 = [(0, 512), (512, 1024)]
    v_wout = {}
    W("gpsimd", t_pe, t_pe.n)
    for t in range(2):
        v_wout[t] = DMA("gpsimd", s_wro[t % 2],
                        lambda e, t=t: e.dma_start(out=WRO(t % 2),
                                                   in_=d_wout[t]))
    W("tensor", t_vec, v_ypre)
    for b_ in (4, 5, 6, 7):
        pe_acquire(banks[b_])
    v_yatt_prev = {0: 0, 1: 0}
    sq_prev = {0: 0, 1: 0}
    pend_sum = None
    for t in range(DT):
        W("tensor", v_wout[t])
        for wi in range(2):
            l0 = wi * 512
            bk = banks[(t * 2 + wi) % 4]
            pe_acquire(bk)
            for j in range(DT):
                mm(pb[bk.idx][:, 0:512], WRO(t % 2)[:, j * 128:(j + 1) * 128],
                   YPRE(j, l0, l0 + 512), j == 0, j == DT - 1)
            bk.w_pe = t_pe.n
            if pend_sum is not None:
                pend_sum()
            sl = (t * 2 + wi) % 2
            W("scalar", t_pe, bk.w_pe)
            W("scalar", t_vec, v_yatt_prev[sl])
            P("scalar", lambda e, t=t, b=bk.idx, sl=sl: e.activation(
                out=SQT(sl)[:, 0:512], in_=pb[b][:, 0:512], func=AF.Identity,
                bias=BOUTC[:, t:t + 1]))
            vsc = M("scalar")
            bk.r_trk = (t_scl, vsc)
            W("vector", t_scl, vsc)
            P("vector", lambda e, t=t, l0=l0, sl=sl: e.tensor_add(
                out=YATT(t, l0, l0 + 512), in0=SQT(sl)[:, 0:512],
                in1=XN1j(t, PAD + l0, PAD + l0 + 512)))
            vv = M("vector")
            v_yatt_prev[sl] = vv
            W("scalar", t_vec, vv)
            W("scalar", t_pe, sq_prev[sl])
            P("scalar", lambda e, t=t, l0=l0, sl=sl: e.activation(
                out=SQT(2 + sl)[:, 0:512], in_=YATT(t, l0, l0 + 512),
                func=AF.Square))
            vsq = M("scalar")

            def pend_sum(t=t, wi=wi, l0=l0, sl=sl, vv=vv, vsq=vsq):
                W("tensor", t_vec, vv)
                mm(pb[6 if wi == 0 else 7][0:1, 0:512], ONESB,
                   YATT(t, l0, l0 + 512), t == 0, t == DT - 1)
                W("tensor", t_scl, vsq)
                sq_prev[sl] = mm(pb[4 if wi == 0 else 5][0:1, 0:512], ONESB,
                                 SQT(2 + sl)[:, 0:512], t == 0, t == DT - 1)
        if t + 2 < DT:
            W("gpsimd", t_pe, t_pe.n)
            v_wout[t + 2] = DMA("gpsimd", s_wro[t % 2],
                                lambda e, t=t: e.dma_start(out=WRO(t % 2),
                                                           in_=d_wout[t + 2]))
    pend_sum()
    for b_ in (4, 5, 6, 7):
        banks[b_].w_pe = t_pe.n
    v_wgu = {}
    W("gpsimd", t_pe, t_pe.n)
    for i in range(2):
        v_wgu[i] = DMA("gpsimd", s_wrf[i % 2],
                       lambda e, i=i: e.dma_start(out=WGUPW(0, i % 2),
                                                  in_=d_wgu[i]))
    W("vector", t_pe, t_pe.n)
    P("vector", lambda e: e.tensor_copy(out=SRA[0:1, 0:512], in_=pb[6][0:1, 0:512]))
    P("vector", lambda e: e.tensor_copy(out=SRB[0:1, 0:512], in_=pb[4][0:1, 0:512]))
    P("vector", lambda e: e.tensor_copy(out=SRA[0:1, 512:1024], in_=pb[7][0:1, 0:512]))
    P("vector", lambda e: e.tensor_copy(out=SRB[0:1, 512:1024], in_=pb[5][0:1, 0:512]))
    v_rows1 = M("vector")
    for b_ in (4, 5, 6, 7):
        banks[b_].r_trk = (t_vec, v_rows1)
    tap("yatt", XPBT)

    ln2_rows_done = []

    def outer1_apply(wi, c0, c1, v_chain):
        wl = c1 - c0
        W("vector", t_scl, v_chain)
        pe_acquire(banks[4])
        pe_acquire(banks[5])
        pe_j = {-1: t_pe.n}
        for j in range(DT):
            P("vector", lambda e, j=j, c0=c0, wl=wl: e.tensor_sub(
                out=SQT(3)[:, 0:wl], in0=YATT(j, c0, c0 + wl),
                in1=MB_B[:, 0:wl]))
            CB("vector")
            P("vector", lambda e, j=j, c0=c0, wl=wl: e.tensor_mul(
                out=CH2[:, 0:wl], in0=SQT(3)[:, 0:wl], in1=INVB[:, 0:wl]))
            CB("vector")
            P("vector", lambda e, j=j, c0=c0, wl=wl: e.tensor_add(
                out=Xj(j, PAD + c0, PAD + c0 + wl),
                in0=Xj(j, PAD + c0, PAD + c0 + wl), in1=CH2[:, 0:wl]))
            vxj = M("vector")
            CB("vector")
            W("scalar", t_vec, vxj)
            W("scalar", t_pe, pe_j[j - 1])
            P("scalar", lambda e, j=j, c0=c0, wl=wl: e.copy(
                out=SQT(0)[:, 0:wl], in_=Xj(j, PAD + c0, PAD + c0 + wl)))
            P("scalar", lambda e, j=j, c0=c0, wl=wl: e.activation(
                out=SQT(1)[:, 0:wl], in_=Xj(j, PAD + c0, PAD + c0 + wl),
                func=AF.Square))
            vs2 = M("scalar")
            W("tensor", t_scl, vs2)
            mm(pb[4][0:1, 0:wl], ONESB, SQT(0)[:, 0:wl], j == 0, j == DT - 1)
            pe_j[j] = mm(pb[5][0:1, 0:wl], ONESB, SQT(1)[:, 0:wl],
                         j == 0, j == DT - 1)
        banks[4].w_pe = banks[5].w_pe = t_pe.n
        ln_state["mb_busy"] = (t_vec, t_vec.n)
        W("vector", t_pe, t_pe.n)
        P("vector", lambda e, c0=c0, c1=c1, wl=wl: e.tensor_copy(
            out=SRA[0:1, c0:c1], in_=pb[4][0:1, 0:wl]))
        P("vector", lambda e, c0=c0, c1=c1, wl=wl: e.tensor_copy(
            out=SRB[0:1, c0:c1], in_=pb[5][0:1, 0:wl]))
        vr = M("vector")
        banks[4].r_trk = banks[5].r_trk = (t_vec, vr)
        ln2_rows_done.append(vr)

    emit_ln_chain(WL2, [(None, MB_B, INVB)] * 2, outer1_apply, v_rows1)
    tap("x2", XT)

    emit_ln_chain(WL2, [(MB_F, None, INVB), (MB_F2, None, INVB2)], None,
                  ln2_rows_done[-1])
    v_ln2bc = t_scl.n

    act_table("silu_and_others")

    d_out3 = d_out.rearrange("p (j c) -> p j c", j=DT)
    v_out_parts = []
    v_rows_f = {}
    v_final_apply = {}

    def final_apply(wi, c0, c1, v_chain):
        wl = c1 - c0
        W("vector", t_scl, v_chain)
        for j in range(DT):
            P("vector", lambda e, j=j, wi=wi, wl=wl: e.tensor_sub(
                out=SQT(2)[:, 0:wl], in0=Y2W(wi, j), in1=MB_B[:, 0:wl]))
            CB("vector")
            P("vector", lambda e, j=j, wl=wl: e.tensor_mul(
                out=CH2[:, 0:wl], in0=SQT(2)[:, 0:wl], in1=INVB[:, 0:wl]))
            CB("vector")
            P("vector", lambda e, j=j, c0=c0, wl=wl: e.tensor_add(
                out=Xj(j, PAD + c0, PAD + c0 + wl),
                in0=Xj(j, PAD + c0, PAD + c0 + wl), in1=CH2[:, 0:wl]))
            CB("vector")
        v_apply = M("vector")
        v_final_apply[wi] = v_apply
        ln_state["mb_busy"] = (t_vec, v_apply)
        W("sync", t_vec, v_apply)
        v_out_parts.append(DMA("sync", s_out, lambda e, c0=c0, c1=c1:
                               e.dma_start(out=d_out3[:, :, c0:c1],
                                           in_=x_ar3[:, :, PAD + c0:PAD + c1])))

    def emit_final(wi, v_rows, ba=6, bb=7):
        c0 = 512 * wi
        emit_ln_chain([(c0, c0 + 512)], [(None, MB_B, INVB)],
                      lambda _w, a, b, v: final_apply(wi, a, b, v), v_rows,
                      ba=ba, bb=bb)

    v_wd = {}

    def WGUPW(w, slot):
        t = WGUA if w == 0 else WGUB
        return t[:, slot * 4096:(slot + 1) * 4096]

    def WRDW(w, slot):
        t = WRDAT if w == 0 else WRDBT
        return t[:, slot * 2048:(slot + 1) * 2048]

    for k in range(2):
        l0 = 512 * k
        mbf = MB_F if k == 0 else MB_F2
        invb = INVB if k == 0 else INVB2
        W("vector", t_scl, v_ln2bc)
        v_x2n_j = []
        for j in range(DT):
            P("vector", lambda e, j=j, l0=l0, mbf=mbf: e.tensor_sub(
                out=SQT(3)[:, 0:512], in0=Xj(j, PAD + l0, PAD + l0 + 512),
                in1=mbf[:, 0:512]))
            CB("vector")
            P("vector", lambda e, j=j, k=k, invb=invb: e.tensor_mul(
                out=X2NW(k, j), in0=SQT(3)[:, 0:512], in1=invb[:, 0:512]))
            v_x2n_j.append(M("vector"))
            CB("vector")
        slg = (lambda i: i % 2) if k == 0 else (lambda i: (i + 1) % 2)
        if k == 0:
            pass
        else:
            v_wgu[0] = v_wgu_k1
            W("gpsimd", t_pe, t_pe.n)
            v_wgu[1] = DMA("gpsimd", s_wrf[0],
                           lambda e: e.dma_start(out=WGUPW(1, 0),
                                                 in_=d_wgu[1]))
        v_fuse_last = 0
        v_scr_gate = t_scl.n
        for i in range(IT):
            W("tensor", v_wgu[i])
            bg = banks[i % 2]
            bu = banks[2 + i % 2]
            pe_acquire(bg)
            pe_acquire(bu)
            for j in range(DT):
                if i == 0:
                    W("tensor", t_vec, v_x2n_j[j])
                mm(pb[bg.idx][:, 0:512],
                   WGUPW(k, slg(i))[:, j * 128:(j + 1) * 128],
                   X2NW(k, j), j == 0, j == DT - 1)
            bg.w_pe = t_pe.n
            for j in range(DT):
                mm(pb[bu.idx][:, 0:512],
                   WGUPW(k, slg(i))[:, 2048 + j * 128:2048 + (j + 1) * 128],
                   X2NW(k, j), j == 0, j == DT - 1)
            bu.w_pe = t_pe.n
            W("scalar", t_pe, bg.w_pe)
            if i == 0:
                W("scalar", t_scl, v_scr_gate)
            if i >= 1:
                W("scalar", t_vec, v_fuse_last)
            P("scalar", lambda e, i=i, b=bg.idx: e.activation(
                out=GS, in_=pb[b][:, 0:512], func=AF.Silu,
                scale=SGC[:, i:i + 1]))
            vsc = M("scalar")
            bg.r_trk = (t_scl, vsc)
            W("vector", t_scl, vsc)
            W("vector", t_pe, bu.w_pe)
            P("vector", lambda e, i=i, k=k, b=bu.idx: e.scalar_tensor_tensor(
                out=FUSE(k, i), in0=pb[b][:, 0:512], scalar=SUC[:, i:i + 1],
                in1=GS, op0=ALU.mult, op1=ALU.mult))
            v_fuse_last = M("vector")
            bu.r_trk = (t_vec, v_fuse_last)
            if i + 2 < IT:
                W("gpsimd", t_pe, t_pe.n)
                v_wgu[i + 2] = DMA("gpsimd", s_wrf[slg(i)],
                                   lambda e, i=i, k=k, s=slg(i): e.dma_start(
                                       out=WGUPW(k, s), in_=d_wgu[i + 2]))
            if i == 48 and k == 0:
                W("gpsimd", t_pe, t_pe.n)
                for (tt, qq) in [(0, 0), (0, 1)]:
                    v_wd[(tt, qq)] = DMA(
                        "gpsimd", s_wrd[qq % 2],
                        lambda e, t=tt, q=qq, k=k: e.dma_start(
                            out=WRDW(k, q % 2),
                            in_=d_wd[t, q // 2, :, (q % 2) * 2048:
                                     (q % 2 + 1) * 2048]))
        if k == 1:
            W("gpsimd", t_pe, t_pe.n)
            W("gpsimd", t_vec, v_fuse_last)
            for (tt, qq) in [(0, 0), (0, 1)]:
                v_wd[(tt, qq)] = DMA(
                    "gpsimd", s_wrd[qq % 2],
                    lambda e, t=tt, q=qq, k=k: e.dma_start(
                        out=WRDW(k, q % 2),
                        in_=d_wd[t, q // 2, :, (q % 2) * 2048:
                                 (q % 2 + 1) * 2048]))
        W("tensor", t_vec, v_fuse_last)
        pe_acquire(banks[6 if k == 0 else 7])
        pe_acquire(banks[0 if k == 0 else 1])
        if k == 0:
            W("gpsimd", t_pe, t_pe.n)
            v_wgu_k1 = DMA("gpsimd", s_wrf[1],
                           lambda e: e.dma_start(out=WGUPW(1, 1),
                                                 in_=d_wgu[0]))
        if k == 1:
            act_table("natural_log_exp_and_others")
            ln_state["mb_busy"] = (t_vec, v_fuse_last)
            emit_final(0, v_fuse_last, ba=2, bb=3)
        sqd_prev = {0: 0, 1: 0}
        pend_sum = None
        for t in range(DT):
            bk = banks[4 + t % 2]
            pe_acquire(bk)
            for q in range(4):
                W("tensor", v_wd[(t, q)])
                for kk in range(16):
                    ki = q * 16 + kk
                    mm(pb[bk.idx][:, 0:512],
                       WRDW(k, q % 2)[:, kk * 128:(kk + 1) * 128],
                       FUSE(k, ki), ki == 0, ki == IT - 1)
                pre = (t + (q + 2) // 4, (q + 2) % 4)
                if pre[0] < DT and pre not in v_wd:
                    W("gpsimd", t_pe, t_pe.n)
                    v_wd[pre] = DMA(
                        "gpsimd", s_wrd[pre[1] % 2],
                        lambda e, p=pre, k=k: e.dma_start(
                            out=WRDW(k, p[1] % 2),
                            in_=d_wd[p[0], p[1] // 2, :, (p[1] % 2) * 2048:
                                     (p[1] % 2 + 1) * 2048]))
            bk.w_pe = t_pe.n
            if pend_sum is not None:
                pend_sum()
            sl = t % 2
            W("scalar", t_pe, bk.w_pe)
            W("scalar", t_pe, sqd_prev[sl])
            P("scalar", lambda e, t=t, k=k, b=bk.idx: e.activation(
                out=Y2W(k, t), in_=pb[b][:, 0:512], func=AF.Identity,
                scale=SDC[:, t:t + 1]))
            CB("scalar")
            P("scalar", lambda e, t=t, k=k, sl=sl: e.activation(
                out=SQT(sl)[:, 0:512], in_=Y2W(k, t), func=AF.Square))
            vsc = M("scalar")
            bk.r_trk = (t_scl, vsc)

            def pend_sum(t=t, k=k, sl=sl, vsc=vsc):
                W("tensor", t_scl, vsc)
                mm(pb[6 if k == 0 else 7][0:1, 0:512], ONESB, Y2W(k, t),
                   t == 0, t == DT - 1)
                sqd_prev[sl] = mm(pb[0 if k == 0 else 1][0:1, 0:512], ONESB,
                                  SQT(sl)[:, 0:512], t == 0, t == DT - 1)
        pend_sum()
        banks[6 if k == 0 else 7].w_pe = t_pe.n
        banks[0 if k == 0 else 1].w_pe = t_pe.n
        W("vector", t_pe, t_pe.n)
        P("vector", lambda e, k=k, l0=l0: e.tensor_copy(
            out=SRA[0:1, l0:l0 + 512], in_=pb[6 if k == 0 else 7][0:1, 0:512]))
        P("vector", lambda e, k=k, l0=l0: e.tensor_copy(
            out=SRB[0:1, l0:l0 + 512], in_=pb[0 if k == 0 else 1][0:1, 0:512]))
        vrf = M("vector")
        banks[6 if k == 0 else 7].r_trk = (t_vec, vrf)
        banks[0 if k == 0 else 1].r_trk = (t_vec, vrf)
        v_wd = {}
        v_rows_f[k] = t_vec.n
        if k == 0:
            tap("y2b", Y2AT)
        else:
            emit_final(1, v_rows_f[1])
    v_y2_all = t_scl.n

    W("gpsimd", s_out, s_out.n)
    if s_tap.n:
        W("gpsimd", s_tap, s_tap.n)

    with nc.Block() as block:
        @block.sync
        def _(e):
            for f in plan["sync"]:
                f(e)

        @block.gpsimd
        def _(e):
            for f in plan["gpsimd"]:
                f(e)

        @block.scalar
        def _(e):
            for f in plan["scalar"]:
                f(e)

        @block.vector
        def _(e):
            for f in plan["vector"]:
                f(e)

        @block.tensor
        def _(e):
            for f in plan["tensor"]:
                f(e)

    return nc



def _prep_shared(inp):
    f32 = np.float32
    bf = ml_dtypes.bfloat16

    def tile_kblocks(w, nblk):
        K = w.shape[0]
        out = np.empty((nblk, 128, K), w.dtype)
        for t in range(nblk):
            blk = w[:, t * 128:(t + 1) * 128]
            out[t] = blk.reshape(K // 128, 128, 128).transpose(1, 0, 2).reshape(
                128, K)
        return np.ascontiguousarray(out)

    w_in = np.asarray(inp["L0_attn_proj_in_w"], f32)
    w_out = np.asarray(inp["L0_attn_proj_out_w"], f32)
    sh = {}
    sh["w_in_t"] = tile_kblocks(np.ascontiguousarray(w_in.T), DT)
    sh["w_out_t"] = tile_kblocks(np.ascontiguousarray(w_out.T), DT)
    bwt = np.ascontiguousarray(np.asarray(inp["L0_attn_B_w"], f32).T)
    sh["b_w_t"] = np.ascontiguousarray(
        bwt.reshape(16, 128, 16).transpose(1, 0, 2).reshape(128, 256))
    sh["c_w_t"] = np.ascontiguousarray(np.asarray(inp["L0_attn_C_w"], f32).T)
    a = np.asarray(inp["L0_attn_A"], f32)
    sh["a_t"] = np.ascontiguousarray(
        a.reshape(16, 128, 16).transpose(1, 0, 2).reshape(128, 256))
    for nm, key in [("dcol", "L0_attn_D_param"), ("bincol", "L0_attn_proj_in_b"),
                    ("boutcol", "L0_attn_proj_out_b"), ("sdcol", "L0_ffn_s_down")]:
        sh[nm] = np.ascontiguousarray(np.asarray(inp[key], f32).reshape(16, 128).T)
    for nm, key in [("sgcol", "L0_ffn_s_gate"), ("sucol", "L0_ffn_s_up")]:
        sh[nm] = np.ascontiguousarray(np.asarray(inp[key], f32).reshape(64, 128).T)
    wg_t = tile_kblocks(np.asarray(inp["L0_ffn_w_gate"]), IT)
    wu_t = tile_kblocks(np.asarray(inp["L0_ffn_w_up"]), IT)
    sh["wgu_t"] = np.ascontiguousarray(
        np.concatenate([wg_t, wu_t], axis=2))
    wd = np.asarray(inp["L0_ffn_w_down"])
    wd_t = np.empty((DT, 2, 128, 4096), np.int8)
    for t in range(DT):
        for hf in range(2):
            blk = wd[hf * 4096:(hf + 1) * 4096, t * 128:(t + 1) * 128]
            wd_t[t, hf] = blk.reshape(32, 128, 128).transpose(1, 0, 2).reshape(
                128, 4096)
    sh["wd_t"] = np.ascontiguousarray(wd_t)
    rep = np.zeros((16, 128), bf)
    for c in range(128):
        rep[c % 16, c] = 1.0
    sh["rep16b"] = rep
    sh["ones_b"] = np.ones((128, 1), bf)
    sh["ones_f"] = np.ones((1, 128), f32)
    sh["ood_f"] = np.full((1, 128), 1.0 / D, f32)
    sh["ood_b"] = np.full((1, 128), 1.0 / D, bf)
    sh["ident_f"] = np.eye(128, dtype=f32)
    return sh


def _prep_core(x, b, h):
    f32 = np.float32
    bf = ml_dtypes.bfloat16
    xt = np.ascontiguousarray(x[b].T.astype(f32))
    if h == 0:
        sl = np.concatenate([np.zeros((D, PAD), f32), xt[:, 0:TOUT]], axis=1)
    else:
        sl = xt[:, TOUT - PAD:2 * TOUT]
    xT = np.ascontiguousarray(
        sl.reshape(16, 128, T).transpose(1, 0, 2).reshape(128, DT * T))
    mask = np.ones((16, T), bf)
    if h == 0:
        mask[:, 0:PAD] = 0
    return {"xT": xT, "maskb": mask}


def kernel(**inputs):
    x = np.asarray(inputs["L0_attn_x"], np.float32)
    sh = _prep_shared(inputs)
    in_maps = []
    for c in range(8):
        b, h = divmod(c, 2)
        m = dict(sh)
        m.update(_prep_core(x, b, h))
        in_maps.append(m)
    nc = build_program()
    res = run_bass_kernel_spmd(nc, in_maps, core_ids=list(range(8)))
    out = np.empty((4, 2048, 2048), np.float32)
    for c in range(8):
        b, h = divmod(c, 2)
        o = np.asarray(res.results[c]["out_xT"])
        ot = o.reshape(128, DT, TOUT).transpose(1, 0, 2).reshape(D, TOUT)
        out[b, h * TOUT:(h + 1) * TOUT, :] = ot.T
    return out


# version 08
# speedup vs baseline: 1.2966x; 1.0076x over previous
import contextlib
import numpy as np
import ml_dtypes
import concourse.bass as bass
import concourse.mybir as mybir
from concourse.bass_utils import run_bass_kernel_spmd

F32 = mybir.dt.float32
BF16 = mybir.dt.bfloat16
I8 = mybir.dt.int8
U8 = mybir.dt.uint8
AF = mybir.ActivationFunctionType
ALU = mybir.AluOpType

D = 2048
DT = 16
S = 16
TOUT = 1024
PAD = 64
T = PAD + TOUT
NO = 8
NG = 6
INTER = 8192
IT = 64
WIN3 = [(0, 512), (512, 1024), (1024, T)]
EPS = 1e-5
SBUF_BYTES = 212832

ENGINES = ("sync", "gpsimd", "scalar", "vector", "tensor")


class Trk:
    def __init__(self, name, step=1):
        self.name, self.step, self.n, self.h = name, step, 0, None


class Alloc:
    def __init__(self, nbytes):
        self.cap = nbytes
        self.items = []

    def pin(self, off, size, t0, t1, name):
        size = (size + 63) & ~63
        for (o, s, a0, a1, n) in self.items:
            if not (a1 <= t0 or t1 <= a0) and not (o + s <= off or off + size <= o):
                raise MemoryError(f"pin clash: {name} vs {n}")
        self.items.append((off, size, t0, t1, name))
        return off

    def alloc(self, size, t0, t1, name):
        size = (size + 63) & ~63
        spans = sorted((o, o + s) for (o, s, a0, a1, _n) in self.items
                       if not (a1 <= t0 or t1 <= a0))
        holes, cur = [], 0
        for (s0, s1) in spans:
            if s0 > cur:
                holes.append((cur, s0))
            cur = max(cur, s1)
        if cur < self.cap:
            holes.append((cur, self.cap))
        cand = [h for h in holes if h[1] - h[0] >= size]
        if not cand:
            raise MemoryError(f"SBUF overflow: {name} {size}B t[{t0},{t1}]")
        off = min(cand, key=lambda h: (h[1] - h[0], h[0]))[0]
        self.items.append((off, size, t0, t1, name))
        return off


def build_program(taps=()):
    nc = bass.Bass("TRN2", target_bir_lowering=False, debug=False)

    d_x = nc.dram_tensor("xT", [128, DT * T], F32, kind="ExternalInput").ap()
    d_win = nc.dram_tensor("w_in_t", [DT, 128, 2048], F32, kind="ExternalInput").ap()
    d_wout = nc.dram_tensor("w_out_t", [DT, 128, 2048], F32, kind="ExternalInput").ap()
    d_bwt = nc.dram_tensor("b_w_t", [128, 256], F32, kind="ExternalInput").ap()
    d_cwt = nc.dram_tensor("c_w_t", [16, 2048], F32, kind="ExternalInput").ap()
    d_a = nc.dram_tensor("a_t", [128, 256], F32, kind="ExternalInput").ap()
    d_dcol = nc.dram_tensor("dcol", [128, 16], F32, kind="ExternalInput").ap()
    d_bin = nc.dram_tensor("bincol", [128, 16], F32, kind="ExternalInput").ap()
    d_bout = nc.dram_tensor("boutcol", [128, 16], F32, kind="ExternalInput").ap()
    d_sd = nc.dram_tensor("sdcol", [128, 16], F32, kind="ExternalInput").ap()
    d_sg = nc.dram_tensor("sgcol", [128, 64], F32, kind="ExternalInput").ap()
    d_su = nc.dram_tensor("sucol", [128, 64], F32, kind="ExternalInput").ap()
    d_wgu = nc.dram_tensor("wgu_t", [IT, 128, 4096], I8, kind="ExternalInput").ap()
    d_wd = nc.dram_tensor("wd_t", [DT, 2, 128, 4096], I8, kind="ExternalInput").ap()
    d_mask = nc.dram_tensor("maskb", [16, T], BF16, kind="ExternalInput").ap()
    d_rep16 = nc.dram_tensor("rep16b", [16, 128], BF16, kind="ExternalInput").ap()
    d_ones_b = nc.dram_tensor("ones_b", [128, 1], BF16, kind="ExternalInput").ap()
    d_ones_f = nc.dram_tensor("ones_f", [1, 128], F32, kind="ExternalInput").ap()
    d_oodf = nc.dram_tensor("ood_f", [1, 128], F32, kind="ExternalInput").ap()
    d_oodb = nc.dram_tensor("ood_b", [1, 128], BF16, kind="ExternalInput").ap()
    d_identf = nc.dram_tensor("ident_f", [128, 128], F32, kind="ExternalInput").ap()
    d_out = nc.dram_tensor("out_xT", [128, DT * TOUT], F32, kind="ExternalOutput").ap()
    tap_out = {}
    for tname, tshape, tdt in taps:
        tap_out[tname] = nc.dram_tensor("tap_" + tname, tshape, tdt,
                                        kind="ExternalOutput").ap()

    al = Alloc(SBUF_BYTES)
    o_x = al.pin(0, DT * T * 4, 0, 12, "x")
    o_scr = al.pin(192512, 1024 * 4, 0, 12, "scratch")
    o_sq = al.pin(196608, 4 * 512 * 2, 0, 12, "sqscr")
    o_srown = al.pin(200704, 2 * 1024 * 2, 4, 12, "sumrows")
    o_bcb = al.pin(204800, 2 * 1024, 5, 12, "bcastB")
    o_onesf = al.pin(207872, 128 * 4 + 256, 0, 12, "onesf+oodb")
    o_const = al.pin(208640, 2 * 64 + 2 * 256 + 64, 0, 12, "consts")
    o_fuseA = al.pin(69632, IT * 512 * 2, 6, 8, "fuseA")
    o_wri = al.pin(163840, 2 * 4096, 0, 2, "wring_in")
    o_srow = al.pin(172032, 2 * T * 2, 0, 2, "sumrows_wide")
    o_wro = al.pin(172032, 2 * 4096, 5, 6, "wring_out")
    o_cwt = al.pin(172032, 2048 * 2, 4, 5, "cwtb")
    o_ypre0 = al.pin(139264, DT * TOUT * 2, 4, 6, "ypre")
    o_ypre = [o_ypre0 + t * 2048 for t in range(DT)]
    o_xn1 = al.pin(69632, DT * T * 2, 0, 6, "xn1b")
    o_xpb = al.pin(104448, DT * T * 2, 1, 6, "x_projb/y_attnb")
    o_pkf = al.pin(104448, 2 * 8192, 0, 1, "pkf")
    o_pk = al.pin(139264, DT * NG * 128 * 2, 0, 4, "packs")
    o_fuseB = al.pin(69632, IT * 512 * 2, 8, 11, "fuseB")
    o_xy1 = al.pin(135168, DT * 512 * 2, 6, 10, "x2na+y2a")
    o_x2na = o_xy1
    o_y2a = o_xy1
    o_xy2 = al.pin(151552, DT * 512 * 2, 8, 12, "x2nb+y2b")
    o_x2nb = o_xy2
    o_y2b = o_xy2
    o_wguA = al.pin(151552, 2 * 8192, 6, 8, "wguA")
    o_wguB = al.pin(167936, 8192, 8, 11, "wguB_s0")
    al.pin(176128, 8192, 7, 11, "wguB_s1")
    o_wrdA = al.pin(167936, 2 * 4096, 6, 8, "wrdA")
    o_wrdB = al.pin(184320, 2 * 4096, 9, 12, "wrdB")
    o_gh = al.pin(163840, NG * 576 * 4, 3, 4, "ghat")
    o_dr = al.pin(177664, T * 4, 2, 4, "deltar")
    o_acc1 = al.pin(186112, 520 * 4, 3, 4, "acc1")
    o_shb = al.pin(188224, TOUT * 2, 3, 5, "S_hb")
    o_bca = al.pin(176384, 2048 + 1024, 0, 2, "bcastA")
    o_mask = al.pin(186112, T * 2, 0, 3, "maskb")
    o_db = al.pin(163840, T * 2, 2, 3, "deltab")
    o_sig = al.pin(120832, 2 * 1024, 0, 1, "sig")
    o_mbf1 = al.pin(186112, 2048, 5, 7, "mbf_w0")
    o_mbf2 = al.pin(188160, 2048, 5, 9, "mbf_w1")
    o_bcb2 = al.pin(190208, 1024, 5, 9, "bcastB2")
    o_bwt = al.pin(188288, 256 * 2, 0, 3, "bwtb")
    o_rep16 = al.pin(188800, 256, 0, 3, "rep16")
    o_dcol = al.pin(190784, 64, 0, 5, "dcol")
    o_binc = al.pin(180480, 64, 0, 2, "binc")
    o_identf = al.pin(190272, 128 * 4, 0, 4, "identf")
    o_sq4 = al.pin(179456, 512 * 2, 0, 2, "sqt4")
    o_xrows = al.pin(182016, 2 * 1024 * 2, 1, 7, "xrows")
    o_xyt = al.pin(180224, 512 * 2, 4, 6, "xyt0")
    o_xyt2 = al.pin(209344, 512 * 2, 4, 6, "xyt1")
    o_xbt = al.pin(191232, 512 * 2, 4, 6, "xbt")
    o_invb0 = al.pin(206848, 512 * 2, 5, 7, "invb0")
    import os as _os
    if _os.environ.get("KDEBUG_ALLOC"):
        for (o, s, a0, a1, n) in sorted(al.items):
            print(f"{o:7d} +{s:6d} t[{a0:2d},{a1:2d}] {n}")
        raise SystemExit(0)

    plan = {e: [] for e in ENGINES}

    def Praw(eng, fn):
        plan[eng].append(fn)

    t_pe, t_vec, t_scl = Trk("t_pe"), Trk("t_vec"), Trk("t_scl")
    s_const = Trk("s_const", 16)
    s_x = [Trk(f"s_x{i}", 16) for i in range(3)]
    s_bwt, s_cwt = Trk("s_bwt", 16), Trk("s_cwt", 16)
    s_wri = [Trk(f"s_wri{i}", 16) for i in range(2)]
    s_wro = [Trk(f"s_wro{i}", 16) for i in range(2)]
    s_wrf = [Trk(f"s_wrf{i}", 16) for i in range(4)]
    s_wrd = [Trk(f"s_wrd{i}", 16) for i in range(2)]
    s_out, s_tap = Trk("s_out", 16), Trk("s_tap", 16)
    DMAT = ([s_const] + s_x + [s_bwt, s_cwt] + s_wri + s_wro + s_wrf + s_wrd
            + [s_out, s_tap])
    TRKS = [t_pe, t_vec, t_scl] + DMAT
    ETRK = {"tensor": t_pe, "vector": t_vec, "scalar": t_scl}

    def DMA(queue, trk, build):
        trk.n += trk.step
        plan[queue].append(lambda e, b=build, t=trk: b(e).then_inc(t.h, t.step))
        return (trk, trk.n)

    def W(eng, trk, val=None):
        if val is None:
            trk, val = trk
        if val and val > 0:
            Praw(eng, lambda e, t=trk, v=val: e.wait_ge(t.h, v))

    def P(eng, fn):
        trk = ETRK[eng]
        trk.n += trk.step
        plan[eng].append(lambda e, b=fn, t=trk: b(e).then_inc(t.h, t.step))
        return trk.n

    def E(eng, build, inc=False):
        if inc:
            return P(eng, build)
        Praw(eng, build)
        return None

    def M(eng):
        return ETRK[eng].n

    def CB(eng):
        trk = ETRK[eng]
        Praw(eng, lambda e, t=trk, v=trk.n: e.wait_ge(t.h, v))

    def BAR():
        for eng in ("tensor", "vector", "scalar"):
            for trk in (t_pe, t_vec, t_scl):
                if trk.n > 0:
                    Praw(eng, lambda e, t=trk, v=trk.n: e.wait_ge(t.h, v))

    class Bank:
        def __init__(self, idx):
            self.idx, self.w_pe, self.r_trk = idx, 0, None

    banks = [Bank(i) for i in range(8)]

    def pe_acquire(bk):
        if bk.r_trk is not None:
            W("tensor", bk.r_trk[0], bk.r_trk[1])

    def mm(out_ap, lhsT, rhs, start, stop, inc=True):
        b = (lambda e, o=out_ap, l=lhsT, r=rhs, st=start, sp=stop:
             e.matmul(o, l, r, start=st, stop=sp))
        t_pe.n += 1
        plan["tensor"].append(lambda e, bb=b: bb(e).then_inc(t_pe.h, 1))
        return t_pe.n

    def act_table(set_name):
        from concourse.hw_specs import get_activation_tables
        sid = list(get_activation_tables(nc.m.arch).keys()).index(set_name)

        Praw("scalar", lambda e, sid=sid: e.add_instruction(
            mybir.InstLoadActFuncSet(
                name=nc.get_next_instruction_name(), act_func_set_id=sid,
                ins=[], outs=[])))
        for _ in range(12):
            P("scalar", lambda e: e.copy(out=DLY, in_=DLY))
            CB("scalar")

    es = contextlib.ExitStack()
    nc._kernel_es = es
    ar = es.enter_context(nc.sbuf_tensor("arena", [128, SBUF_BYTES], U8))
    pb = [es.enter_context(nc.psum_tensor(f"pb{i}", [128, 512], F32))
          for i in range(8)]
    for trk in TRKS:
        trk.h = es.enter_context(nc.semaphore(trk.name))

    arena_base = None
    for a in nc.m.functions[0].allocations:
        mls = getattr(a, "memorylocations", None)
        if not mls:
            continue
        if mls[0].name == "arena":
            arena_base = mls[0].addr
    assert arena_base is not None

    def REG(name, off, part, cols, dtype):
        return nc.alloc_sbuf_tensor_at(name, [part, cols], dtype,
                                       offset=arena_base + off).ap()

    XT = REG("XT", o_x, 128, DT * T, F32)
    XN1T = REG("XN1T", o_xn1, 128, DT * T, BF16)
    XPBT = REG("XPBT", o_xpb, 128, DT * T, BF16)
    PKT = REG("PKT", o_pk, 128, DT * NG * 128, BF16)
    GHT = REG("GHT", o_gh, 128, NG * 576, F32)
    DR = REG("DRT", o_dr, 128, T, F32)
    DB = REG("DBT", o_db, 16, T, BF16)
    MASK = REG("MASKT", o_mask, 16, T, BF16)
    ACC1 = REG("ACC1T", o_acc1, 128, 520, F32)
    SHB = REG("SHBT", o_shb, 16, TOUT, BF16)
    SCRT = REG("SCRT", o_scr, 128, 1024, F32)
    SCR = SCRT
    SRT = REG("SRT", o_srow, 1, 2 * T, BF16)
    SRA1 = SRT[:, 0:T]
    SRB1 = SRT[:, T:2 * T]
    SRNT = REG("SRNT", o_srown, 1, 2048, BF16)
    SRA = SRNT[:, 0:1024]
    SRB = SRNT[:, 1024:2048]
    CHS = SCRT[:, 0:512]
    CH2 = SCRT[:, 512:1024]
    GS = SCRT[:, 0:512]
    SQTT = REG("SQTT", o_sq, 128, 2048, BF16)
    SQT4 = REG("SQT4", o_sq4, 128, 512, BF16)
    XROWS = REG("XROWS", o_xrows, 1, 2048, BF16)
    XROWA = XROWS[:, 0:1024]
    XROWB = XROWS[:, 1024:2048]
    XYT0 = REG("XYT0", o_xyt, 128, 512, BF16)
    XYT1 = REG("XYT1", o_xyt2, 128, 512, BF16)
    XYR0 = REG("XYR0", o_xyt, 1, 512, BF16)
    XYR1 = REG("XYR1", o_xyt2, 1, 512, BF16)
    XBT = REG("XBT", o_xbt, 128, 512, BF16)
    INVB0 = REG("INVB0", o_invb0, 128, 512, BF16)

    def SQT(i):
        return SQTT[:, i * 512:(i + 1) * 512]

    DLY = SQTT[:, 0:128]

    MB_F = REG("MBF1", o_mbf1, 128, 512, F32)
    MB_F2 = REG("MBF2", o_mbf2, 128, 512, F32)
    INVBT = REG("INVBT", o_bcb, 128, 1024, BF16)
    INVB = INVBT[:, 0:512]
    MB_B = INVBT[:, 512:1024]
    INVB2 = REG("INVB2", o_bcb2, 128, 512, BF16)
    MB1F = REG("MB1F", o_bca, 128, 512, F32)
    INV1B = REG("INV1B", o_bca + 2048, 128, 512, BF16)
    WRIT = REG("WRIT", o_wri, 128, 4096, BF16)
    WROT = REG("WROT", o_wro, 128, 4096, BF16)
    WGUA = REG("WGUA", o_wguA, 128, 8192, BF16)
    WGUB = REG("WGUB", o_wguB, 128, 8192, BF16)

    def WRI(i):
        return WRIT[:, i * 2048:(i + 1) * 2048]

    def WRO(i):
        return WROT[:, i * 2048:(i + 1) * 2048]

    BWTB = REG("BWTT", o_bwt, 128, 256, BF16)
    CWTB = REG("CWTT", o_cwt, 16, 2048, BF16)
    oc = o_const
    DCOL = REG("DCOL", o_dcol, 128, 16, F32)
    BINC = REG("BINC", o_binc, 128, 16, F32)
    BOUTC = REG("BOUTC", oc, 128, 16, F32); oc += 64
    SDC = REG("SDC", oc, 128, 16, F32); oc += 64
    SGC = REG("SGC", oc, 128, 64, F32); oc += 256
    SUC = REG("SUC", oc, 128, 64, F32); oc += 256
    REP16 = REG("REP16", o_rep16, 16, 128, BF16)
    ONESB = REG("ONESB", oc, 128, 1, BF16); oc += 64
    ONESF = REG("ONESF", o_onesf, 1, 128, F32)
    OODB = REG("OODB", o_onesf + 512, 1, 128, BF16)
    IDENTF = REG("IDENTF", o_identf, 128, 128, F32)
    SIGA = REG("SIGA", o_sig, 128, 256, F32)
    SIG8 = REG("SIG8", o_sig + 1024, 128, 256, F32)
    PKF = REG("PKF", o_pkf, 128, 2048, F32)
    S8R = REG("S8R", o_pkf + 8192, 128, 2048, F32)
    YPRET = [REG(f"YPRE{t}", o_ypre[t], 128, TOUT, BF16) for t in range(DT)]
    FUSETA = REG("FUSETA", o_fuseA, 128, IT * 512, BF16)
    FUSETB = REG("FUSETB", o_fuseB, 128, IT * 512, BF16)
    X2NAT = REG("X2NAT", o_x2na, 128, DT * 512, BF16)
    WRDAT = REG("WRDAT", o_wrdA, 128, 4096, BF16)
    WRDBT = REG("WRDBT", o_wrdB, 128, 4096, BF16)
    X2NBT = REG("X2NBT", o_x2nb, 128, DT * 512, BF16)
    Y2AT = REG("Y2AT", o_y2a, 128, DT * 512, BF16)
    Y2BT = REG("Y2BT", o_y2b, 128, DT * 512, BF16)

    def Xj(j, c0, c1):
        return XT[:, j * T + c0:j * T + c1]

    def XN1j(j, c0, c1):
        return XN1T[:, j * T + c0:j * T + c1]

    def XPBj(j, c0, c1):
        return XPBT[:, j * T + c0:j * T + c1]

    def YATT(j, l0, l1):
        return XPBT[:, j * T + l0:j * T + l1]

    def PACK(j, g):
        return PKT[:, (j * NG + g) * 128:(j * NG + g + 1) * 128]

    def GH(g, c0, c1):
        return GHT[:, g * 576 + c0:g * 576 + c1]

    def YPRE(t, l0, l1):
        return YPRET[t][:, l0:l1]

    def FUSE(w, i):
        t = FUSETA if w == 0 else FUSETB
        return t[:, i * 512:(i + 1) * 512]

    def X2NW(k, j):
        t = X2NAT if k == 0 else X2NBT
        return t[:, j * 512:(j + 1) * 512]

    def Y2W(k, j):
        t = Y2AT if k == 0 else Y2BT
        return t[:, j * 512:(j + 1) * 512]

    def tap(name, view):
        if name not in tap_out:
            return
        W("sync", t_vec, t_vec.n)
        W("sync", t_scl, t_scl.n)
        W("sync", t_pe, t_pe.n)
        DMA("sync", s_tap, lambda e, d=tap_out[name], v=view:
            e.dma_start(out=d, in_=v))
        for eng in ("vector", "scalar", "tensor", "gpsimd"):
            W(eng, s_tap, s_tap.n)

    for (dst, src) in [(SIGA, d_a), (DCOL, d_dcol), (BINC, d_bin),
                       (BOUTC, d_bout), (SDC, d_sd), (SGC, d_sg), (SUC, d_su),
                       (MASK, d_mask), (REP16, d_rep16), (ONESB, d_ones_b),
                       (ONESF, d_ones_f), (OODB, d_oodb),
                       (IDENTF, d_identf)]:
        DMA("sync", s_const, lambda e, d=dst, s=src: e.dma_start(out=d, in_=s))
    v_consts = (s_const, s_const.n)
    d_x3 = d_x.rearrange("p (j c) -> p j c", j=DT)
    x_ar3 = XT.rearrange("p (j c) -> p j c", j=DT)
    LN1_ORDER = [2, 0, 1]
    v_xw = {}
    for wi in LN1_ORDER:
        w0, w1 = WIN3[wi]
        v_xw[wi] = DMA("gpsimd" if wi == 2 else "sync", s_x[wi],
                       lambda e, a=x_ar3[:, :, w0:w1], s=d_x3[:, :, w0:w1]:
                       e.dma_start(out=a, in_=s))
    v_bwt = DMA("gpsimd", s_bwt, lambda e: e.dma_start(out=BWTB, in_=d_bwt))
    v_win = {}
    for t in range(2):
        v_win[t] = DMA("gpsimd", s_wri[t % 2],
                       lambda e, t=t: e.dma_start(out=WRI(t % 2), in_=d_win[t]))

    P("vector", lambda e: e.memset(DLY, 0.0))
    v_dly = M("vector")
    W("scalar", t_vec, v_dly)
    act_table("natural_log_exp_and_others")

    W("vector", v_consts)
    W("scalar", v_consts)
    P("scalar", lambda e: e.activation(out=SCR[:, 0:256], in_=SIGA, func=AF.Exp,
                                       scale=-1.0))
    v_expA = M("scalar")
    W("vector", t_scl, v_expA)
    P("vector", lambda e: e.tensor_scalar_add(out=SCR[:, 0:256],
                                              in0=SCR[:, 0:256], scalar1=1.0))
    CB("vector")
    P("vector", lambda e: e.reciprocal(out=SIGA, in_=SCR[:, 0:256]))
    CB("vector")
    P("vector", lambda e: e.tensor_mul(out=SCR[:, 0:256], in0=SIGA, in1=SIGA))
    CB("vector")
    P("vector", lambda e: e.tensor_mul(out=SCR[:, 256:512], in0=SCR[:, 0:256],
                                       in1=SCR[:, 0:256]))
    CB("vector")
    P("vector", lambda e: e.tensor_mul(out=SIG8, in0=SCR[:, 256:512],
                                       in1=SCR[:, 256:512]))
    CB("vector")
    s8r4 = S8R.rearrange("p (j o s) -> p j o s", j=16, o=8)
    sig8_3 = SIG8.rearrange("p (j s) -> p j s", j=16)
    for o in range(NO):
        P("vector", lambda e, o=o: e.tensor_copy(out=s8r4[:, :, o, :], in_=sig8_3))
    CB("vector")
    pkf4 = PKF.rearrange("p (j o s) -> p j o s", j=16, o=8)
    siga3 = SIGA.rearrange("p (j s) -> p j s", j=16)
    P("vector", lambda e: e.memset(pkf4[:, :, 0, :], 1.0))
    CB("vector")
    for o in range(1, NO):
        P("vector", lambda e, o=o: e.tensor_tensor(
            out=pkf4[:, :, o, :], in0=pkf4[:, :, o - 1, :], in1=siga3,
            op=ALU.mult))
        CB("vector")
    v_pkf0 = M("vector")
    pk5 = PKT.rearrange("p (j g c) -> p j g c", j=DT, g=NG)
    pkf3 = PKF.rearrange("p (j c) -> p j c", j=16)
    W("scalar", t_vec, v_pkf0)
    for g in range(NG):
        P("scalar", lambda e, g=g: e.copy(out=pk5[:, :, g, :], in_=pkf3))
        if g < NG - 1:
            vv = M("scalar")
            W("vector", t_scl, vv)
            P("vector", lambda e: e.tensor_mul(out=PKF, in0=PKF, in1=S8R))
            CB("vector")
            vv2 = M("vector")
            W("scalar", t_vec, vv2)
    M("scalar")
    v_pack_vec = t_vec.n

    ln_state = {"mb_busy": (t_vec, 0)}

    def emit_ln(windows, src_fn, src_is_f32, outs, apply_fn, wait_src):
        for wi, (c0, c1) in enumerate(windows):
            wl = c1 - c0
            pe_acquire(banks[6])
            pe_acquire(banks[7])
            pe_j = {-2: t_pe.n, -1: t_pe.n}
            if wait_src is not None:
                ws = wait_src if isinstance(wait_src, list) else [wait_src]
                for pair in ([ws[wi]] if len(ws) == len(windows) else ws):
                    W("scalar", pair)
            for j in range(DT):
                W("scalar", t_pe, pe_j[j - 2])
                if src_is_f32:
                    sl = (j % 2) * 2
                    P("scalar", lambda e, j=j, sl=sl, c0=c0, c1=c1:
                      e.copy(out=SQT(sl)[:, 0:c1 - c0], in_=src_fn(j, c0, c1)))
                    P("scalar", lambda e, j=j, sl=sl, c0=c0, c1=c1:
                      e.activation(out=SQT(sl + 1)[:, 0:c1 - c0],
                                   in_=src_fn(j, c0, c1), func=AF.Square))
                    rhs_sum, rhs_sq = SQT(sl)[:, 0:wl], SQT(sl + 1)[:, 0:wl]
                else:
                    sl = j % 2
                    P("scalar", lambda e, j=j, sl=sl, c0=c0, c1=c1:
                      e.activation(out=SQT(sl)[:, 0:c1 - c0],
                                   in_=src_fn(j, c0, c1), func=AF.Square))
                    rhs_sum, rhs_sq = src_fn(j, c0, c1), SQT(sl)[:, 0:wl]
                vs = M("scalar")
                W("tensor", t_scl, vs)
                mm(pb[6][0:1, 0:wl], ONESB, rhs_sum, j == 0, j == DT - 1)
                pe_j[j] = mm(pb[7][0:1, 0:wl], ONESB, rhs_sq,
                             j == 0, j == DT - 1, inc=True)
            banks[6].w_pe = banks[7].w_pe = t_pe.n
            W("vector", t_pe, t_pe.n)
            P("vector", lambda e, c0=c0, c1=c1, wl=wl: e.tensor_copy(
                out=SRA1[0:1, c0:c1], in_=pb[6][0:1, 0:wl]))
            P("vector", lambda e, c0=c0, c1=c1, wl=wl: e.tensor_copy(
                out=SRB1[0:1, c0:c1], in_=pb[7][0:1, 0:wl]))
            vv = M("vector")
            banks[6].r_trk = banks[7].r_trk = (t_vec, vv)
            emit_ln_chain([(c0, c1)], [outs[wi]],
                          (None if apply_fn is None else
                           lambda _w, a, b, v, wi=wi: apply_fn(wi, a, b, v)),
                          vv, SRA1, SRB1)

    def emit_ln_chain(windows, outs, apply_fn, v_rows, ra=None, rb=None,
                      ba=6, bb=7):
        ra = SRA if ra is None else ra
        rb = SRB if rb is None else rb
        ood = OODB
        for wi, (c0, c1) in enumerate(windows):
            wl = c1 - c0
            mbf, mbb, invb = outs[wi]
            W("tensor", t_vec, v_rows)
            pe_acquire(banks[ba])
            mm(pb[ba][:, 0:wl], ood, ra[0:1, c0:c1], True, True, inc=True)
            banks[ba].w_pe = t_pe.n
            W("scalar", t_pe, banks[ba].w_pe)
            W("scalar", ln_state["mb_busy"][0], ln_state["mb_busy"][1])
            CB("scalar")
            P("scalar", lambda e, wl=wl, ba=ba: e.copy(out=CHS[:, 0:wl],
                                                       in_=pb[ba][:, 0:wl]))
            if mbf is not None:
                P("scalar", lambda e, d=mbf, wl=wl, ba=ba: e.copy(
                    out=d[:, 0:wl], in_=pb[ba][:, 0:wl]))
            if mbb is not None:
                P("scalar", lambda e, d=mbb, wl=wl, ba=ba: e.copy(
                    out=d[:, 0:wl], in_=pb[ba][:, 0:wl]))
            v_m = M("scalar")
            banks[ba].r_trk = (t_scl, v_m)
            pe_acquire(banks[bb])
            mm(pb[bb][:, 0:wl], ood, rb[0:1, c0:c1], True, True, inc=True)
            banks[bb].w_pe = t_pe.n
            W("vector", t_scl, v_m)
            P("vector", lambda e, wl=wl: e.tensor_mul(
                out=CH2[:, 0:wl], in0=CHS[:, 0:wl], in1=CHS[:, 0:wl]))
            CB("vector")
            W("vector", t_pe, banks[bb].w_pe)
            P("vector", lambda e, wl=wl, bb=bb: e.tensor_sub(
                out=CHS[:, 0:wl], in0=pb[bb][:, 0:wl], in1=CH2[:, 0:wl]))
            CB("vector")
            P("vector", lambda e, wl=wl: e.tensor_scalar_add(
                out=CHS[:, 0:wl], in0=CHS[:, 0:wl], scalar1=EPS))
            vv2 = M("vector")
            banks[bb].r_trk = (t_vec, vv2)
            W("scalar", t_vec, vv2)
            P("scalar", lambda e, wl=wl: e.activation(
                out=CH2[:, 0:wl], in_=CHS[:, 0:wl], func=AF.Ln))
            CB("scalar")
            P("scalar", lambda e, wl=wl: e.activation(
                out=CHS[:, 0:wl], in_=CH2[:, 0:wl], func=AF.Exp, scale=-0.5))
            CB("scalar")
            P("scalar", lambda e, d=invb, wl=wl: e.copy(
                out=d[:, 0:wl], in_=CHS[:, 0:wl]))
            v_chain = M("scalar")
            if apply_fn is not None:
                apply_fn(wi, c0, c1, v_chain)

    v_xn1_w = {}

    def ln1_apply(wi, c0, c1, v_chain):
        wl = c1 - c0
        W("vector", t_scl, v_chain)
        for j in range(DT):
            P("vector", lambda e, j=j, c0=c0, c1=c1, wl=wl: e.tensor_sub(
                out=SQT4[:, 0:wl], in0=Xj(j, c0, c1), in1=MB1F[:, 0:wl]))
            CB("vector")
            P("vector", lambda e, j=j, c0=c0, c1=c1, wl=wl: e.tensor_mul(
                out=XN1j(j, c0, c1), in0=SQT4[:, 0:wl], in1=INV1B[:, 0:wl]))
            CB("vector")
        v_xn1_w[c0] = M("vector")
        ln_state["mb_busy"] = (t_vec, v_xn1_w[c0])

    emit_ln([WIN3[i] for i in LN1_ORDER], lambda j, c0, c1: Xj(j, c0, c1),
            True, [(MB1F, None, INV1B)] * 3, ln1_apply,
            [v_xw[i] for i in LN1_ORDER])
    v_xn1_done = ln_state["mb_busy"][1]
    CB("vector")
    P("vector", lambda e: e.tensor_copy(out=XROWA[0:1, 0:1024],
                                        in_=SRA1[0:1, PAD:PAD + 1024]))
    P("vector", lambda e: e.tensor_copy(out=XROWB[0:1, 0:1024],
                                        in_=SRB1[0:1, PAD:PAD + 1024]))
    v_xrows = M("vector")
    CB("vector")
    tap("xn1b", XN1T)

    for t in range(DT):
        W("tensor", v_win[t])
        for wi, (w0, w1) in enumerate(WIN3):
            if t == 0:
                W("tensor", t_vec, v_xn1_w[w0])
            wl = w1 - w0
            bk = banks[(t * 3 + wi) % 4]
            pe_acquire(bk)
            for j in range(DT):
                mm(pb[bk.idx][:, 0:wl], WRI(t % 2)[:, j * 128:(j + 1) * 128],
                   XN1j(j, w0, w1), j == 0, j == DT - 1, inc=(j == DT - 1))
            bk.w_pe = t_pe.n
            W("scalar", t_pe, bk.w_pe)
            if t == 0 and wi == 0:
                W("scalar", t_vec, v_pack_vec)
            P("scalar", lambda e, t=t, w0=w0, w1=w1, wl=wl, b=bk.idx:
              e.activation(out=XPBj(t, w0, w1), in_=pb[b][:, 0:wl],
                           func=AF.Identity, bias=BINC[:, t:t + 1]))
            vv = M("scalar")
            bk.r_trk = (t_scl, vv)
        if t + 2 < DT:
            W("gpsimd", t_pe, t_pe.n)
            v_win[t + 2] = DMA("gpsimd", s_wri[t % 2],
                               lambda e, t=t: e.dma_start(out=WRI(t % 2),
                                                          in_=d_win[t + 2]))
    v_xpb_all = t_scl.n
    tap("x_projb", XPBT)

    BAR()
    W("tensor", v_bwt)
    W("tensor", t_scl, v_xpb_all)
    for wi, (w0, w1) in enumerate(WIN3):
        wl = w1 - w0
        bk = banks[4 + (wi % 2)]
        pe_acquire(bk)
        for j in range(DT):
            mm(pb[bk.idx][0:16, 0:wl], BWTB[:, j * 16:(j + 1) * 16],
               XPBj(j, w0, w1), j == 0, j == DT - 1, inc=(j == DT - 1))
        bk.w_pe = t_pe.n
        W("scalar", t_pe, bk.w_pe)
        P("scalar", lambda e, b=bk.idx, wl=wl: e.activation(
            out=SCR[0:16, 0:wl], in_=pb[b][0:16, 0:wl], func=AF.Exp))
        CB("scalar")
        P("scalar", lambda e, wl=wl: e.activation(
            out=SCR[0:16, 512:512 + wl], in_=SCR[0:16, 0:wl], func=AF.Ln,
            bias=1.0))
        CB("scalar")
        P("scalar", lambda e, w0=w0, w1=w1, wl=wl: e.copy(
            out=DB[:, w0:w1], in_=SCR[0:16, 512:512 + wl]))
        vv = M("scalar")
        bk.r_trk = (t_scl, vv)
    W("vector", t_scl, t_scl.n)
    P("vector", lambda e: e.tensor_mul(out=DB, in0=DB, in1=MASK))
    v_dbm = M("vector")
    W("tensor", t_vec, v_dbm)
    for wi, (w0, w1) in enumerate(WIN3):
        wl = w1 - w0
        bk = banks[4 + (wi % 2)]
        pe_acquire(bk)
        mm(pb[bk.idx][:, 0:wl], REP16, DB[:, w0:w1], True, True, inc=True)
        bk.w_pe = t_pe.n
        W("scalar", t_pe, bk.w_pe)
        P("scalar", lambda e, b=bk.idx, w0=w0, w1=w1, wl=wl: e.copy(
            out=DR[:, w0:w1], in_=pb[b][:, 0:wl]))
        vv = M("scalar")
        bk.r_trk = (t_scl, vv)
    v_dr = t_scl.n
    tap("deltab", DB)

    BAR()
    W("tensor", t_scl, v_dr)
    v_stageA_prev = 0
    for k in range(2):
        gbase = 512 * k
        pieces = [(gbase, gbase + 512), (gbase + 512, min(gbase + 576, T))]
        if k == 1:
            W("tensor", t_vec, v_stageA_prev)
        for g in range(NG):
            for pi, (p0, p1) in enumerate(pieces):
                pl = p1 - p0
                bk = banks[(g * 2 + pi) % 4]
                pe_acquire(bk)
                for j in range(DT):
                    mm(pb[bk.idx][:, 0:pl], PACK(j, g), XPBj(j, p0, p1),
                       j == 0, j == DT - 1, inc=(j == DT - 1))
                bk.w_pe = t_pe.n
                W("vector", t_pe, bk.w_pe)
                P("vector", lambda e, g=g, p0=p0, p1=p1, pl=pl, b=bk.idx,
                  gb=gbase: e.tensor_tensor(
                      out=GH(g, p0 - gb, p1 - gb), in0=pb[b][:, 0:pl],
                      in1=DR[:, p0:p1], op=ALU.mult))
                vv = M("vector")
                bk.r_trk = (t_vec, vv)
            CB("vector")
            if g == 0:
                P("vector", lambda e: e.tensor_copy(
                    out=ACC1[:, 0:519], in_=GH(0, 57, 576)))
            else:
                c0 = 57 - 8 * g
                P("vector", lambda e, c0=c0, g=g: e.tensor_add(
                    out=ACC1[:, 0:519], in0=ACC1[:, 0:519],
                    in1=GH(g, c0, c0 + 519)))
            CB("vector")
        v_stageA = M("vector")
        v_stageA_prev = v_stageA
        if k == 1:
            W("gpsimd", t_vec, v_stageA)
            v_cwt = DMA("gpsimd", s_cwt,
                        lambda e: e.dma_start(out=CWTB, in_=d_cwt))
        bkS = banks[4 + k % 2]
        W("tensor", t_vec, v_stageA)
        pe_acquire(bkS)
        for o in range(NO):
            mm(pb[bkS.idx][0:16, 0:512], IDENTF[:, 16 * o:16 * o + 16],
               ACC1[:, 7 - o:519 - o], o == 0, o == NO - 1, inc=(o == NO - 1))
        bkS.w_pe = t_pe.n
        W("vector", t_pe, bkS.w_pe)
        P("vector", lambda e, k=k, b=bkS.idx: e.tensor_copy(
            out=SHB[:, 512 * k:512 * (k + 1)], in_=pb[b][0:16, 0:512]))
        v_stageB = M("vector")
        bkS.r_trk = (t_vec, v_stageB)
    CB("vector")
    v_shb = t_vec.n
    tap("S_h", SHB)

    BAR()
    W("tensor", v_cwt)
    W("tensor", t_vec, v_shb)
    v_dscr_read = 0
    for t in range(DT):
        for wi in range(2):
            l0 = wi * 512
            bk = banks[(t * 2 + wi) % 4]
            pe_acquire(bk)
            mm(pb[bk.idx][:, 0:512], CWTB[:, t * 128:(t + 1) * 128],
               SHB[:, l0:l0 + 512], True, True, inc=True)
            bk.w_pe = t_pe.n
            W("scalar", t_vec, v_dscr_read)
            P("scalar", lambda e, t=t, l0=l0: e.activation(
                out=SCR[:, 0:512], in_=XPBj(t, PAD + l0, PAD + l0 + 512),
                func=AF.Copy, scale=DCOL[:, t:t + 1]))
            vsc = M("scalar")
            W("vector", t_pe, bk.w_pe)
            W("vector", t_scl, vsc)
            P("vector", lambda e, t=t, l0=l0, b=bk.idx: e.tensor_add(
                out=YPRE(t, l0, l0 + 512), in0=pb[b][:, 0:512],
                in1=SCR[:, 0:512]))
            vv = M("vector")
            bk.r_trk = (t_vec, vv)
            v_dscr_read = vv
    v_ypre = t_vec.n
    tap("ypre0", YPRET[0])

    BAR()
    WL2 = [(0, 512), (512, 1024)]
    v_wout = {}
    W("gpsimd", t_pe, t_pe.n)
    for t in range(2):
        v_wout[t] = DMA("gpsimd", s_wro[t % 2],
                        lambda e, t=t: e.dma_start(out=WRO(t % 2),
                                                   in_=d_wout[t]))
    W("tensor", t_vec, v_ypre)
    for b_ in (4, 5, 6, 7):
        pe_acquire(banks[b_])
    v_yatt_prev = {0: 0, 1: 0}
    sq_prev = {0: 0, 1: 0}
    xy_prev = {0: 0}
    xym_prev = {0: 0, 1: 0}
    pend_sum = None
    for t in range(DT):
        W("tensor", v_wout[t])
        for wi in range(2):
            l0 = wi * 512
            bk = banks[(t * 2 + wi) % 4]
            pe_acquire(bk)
            for j in range(DT):
                mm(pb[bk.idx][:, 0:512], WRO(t % 2)[:, j * 128:(j + 1) * 128],
                   YPRE(j, l0, l0 + 512), j == 0, j == DT - 1)
            bk.w_pe = t_pe.n
            if pend_sum is not None:
                pend_sum()
            sl = (t * 2 + wi) % 2
            W("scalar", t_pe, bk.w_pe)
            W("scalar", t_vec, v_yatt_prev[sl])
            P("scalar", lambda e, t=t, b=bk.idx, sl=sl: e.activation(
                out=SQT(sl)[:, 0:512], in_=pb[b][:, 0:512], func=AF.Identity,
                bias=BOUTC[:, t:t + 1]))
            vsc = M("scalar")
            bk.r_trk = (t_scl, vsc)
            W("vector", t_scl, vsc)
            P("vector", lambda e, t=t, l0=l0, sl=sl: e.tensor_add(
                out=YATT(t, l0, l0 + 512), in0=SQT(sl)[:, 0:512],
                in1=XN1j(t, PAD + l0, PAD + l0 + 512)))
            vv = M("vector")
            v_yatt_prev[sl] = vv
            W("scalar", t_vec, vv)
            W("scalar", t_pe, sq_prev[sl])
            P("scalar", lambda e, t=t, l0=l0, sl=sl: e.activation(
                out=SQT(2 + sl)[:, 0:512], in_=YATT(t, l0, l0 + 512),
                func=AF.Square))
            CB("scalar")
            W("scalar", t_vec, xy_prev[0])
            P("scalar", lambda e, t=t, l0=l0: e.copy(
                out=XBT, in_=Xj(t, PAD + l0, PAD + l0 + 512)))
            vsq = M("scalar")
            xyt_t = XYT0 if sl == 0 else XYT1
            W("vector", t_scl, vsq)
            W("vector", t_pe, xym_prev[sl])
            P("vector", lambda e, t=t, l0=l0, d=xyt_t: e.tensor_mul(
                out=d, in0=XBT, in1=YATT(t, l0, l0 + 512)))
            vxy = M("vector")
            xy_prev[0] = vxy

            def pend_sum(t=t, wi=wi, l0=l0, sl=sl, vv=vv, vsq=vsq, vxy=vxy,
                         xyt_t=xyt_t):
                W("tensor", t_vec, vv)
                mm(pb[6 if wi == 0 else 7][0:1, 0:512], ONESB,
                   YATT(t, l0, l0 + 512), t == 0, t == DT - 1)
                W("tensor", t_scl, vsq)
                sq_prev[sl] = mm(pb[4 if wi == 0 else 5][0:1, 0:512], ONESB,
                                 SQT(2 + sl)[:, 0:512], t == 0, t == DT - 1)
                W("tensor", t_vec, vxy)
                xym_prev[sl] = mm(pb[6 if wi == 0 else 7][32:33, 0:512],
                                  ONESB, xyt_t, t == 0, t == DT - 1)
        if t + 2 < DT:
            W("gpsimd", t_pe, t_pe.n)
            v_wout[t + 2] = DMA("gpsimd", s_wro[t % 2],
                                lambda e, t=t: e.dma_start(out=WRO(t % 2),
                                                           in_=d_wout[t + 2]))
    pend_sum()
    for b_ in (4, 5, 6, 7):
        banks[b_].w_pe = t_pe.n
    v_wgu = {}
    W("gpsimd", t_pe, t_pe.n)
    for i in range(2):
        v_wgu[i] = DMA("gpsimd", s_wrf[i % 2],
                       lambda e, i=i: e.dma_start(out=WGUPW(0, i % 2),
                                                  in_=d_wgu[i]))
    W("vector", t_pe, t_pe.n)
    P("vector", lambda e: e.tensor_copy(out=SRA[0:1, 0:512], in_=pb[6][0:1, 0:512]))
    P("vector", lambda e: e.tensor_copy(out=SRB[0:1, 0:512], in_=pb[4][0:1, 0:512]))
    P("vector", lambda e: e.tensor_copy(out=SRA[0:1, 512:1024], in_=pb[7][0:1, 0:512]))
    P("vector", lambda e: e.tensor_copy(out=SRB[0:1, 512:1024], in_=pb[5][0:1, 0:512]))
    P("vector", lambda e: e.tensor_copy(out=XYR0, in_=pb[6][32:33, 0:512]))
    P("vector", lambda e: e.tensor_copy(out=XYR1, in_=pb[7][32:33, 0:512]))
    v_rows1 = M("vector")
    for b_ in (4, 5, 6, 7):
        banks[b_].r_trk = (t_vec, v_rows1)
    tap("yatt", XPBT)

    v_x2n = {}
    v_combo = {}

    def ln2_combo(wi, v_chain):
        l0 = 512 * wi
        mbf = MB_F if wi == 0 else MB_F2
        invt = INVB0 if wi == 0 else INVB2
        xyr = XYR0 if wi == 0 else XYR1
        W("tensor", t_vec, v_xrows)
        pe_acquire(banks[0])
        mm(pb[0][:, 0:512], OODB, XROWA[0:1, l0:l0 + 512], True, True)
        pe_acquire(banks[1])
        mm(pb[1][:, 0:512], OODB, XROWB[0:1, l0:l0 + 512], True, True)
        pe_acquire(banks[2])
        mm(pb[2][:, 0:512], OODB, xyr, True, True)
        w_pe = t_pe.n
        W("scalar", t_pe, w_pe)
        P("scalar", lambda e, d=mbf: e.copy(out=d, in_=pb[0][:, 0:512]))
        v_mbf = M("scalar")
        banks[0].r_trk = (t_scl, v_mbf)
        W("vector", t_scl, v_mbf)
        W("vector", t_pe, w_pe)
        CB("vector")
        P("vector", lambda e, d=mbf: e.tensor_mul(
            out=CH2[:, 0:512], in0=d, in1=pb[6][:, 0:512]))
        CB("vector")
        P("vector", lambda e: e.tensor_sub(
            out=CH2[:, 0:512], in0=pb[2][:, 0:512], in1=CH2[:, 0:512]))
        CB("vector")
        P("vector", lambda e: e.tensor_mul(
            out=CH2[:, 0:512], in0=CH2[:, 0:512], in1=CHS[:, 0:512]))
        CB("vector")
        P("vector", lambda e: e.tensor_add(
            out=CH2[:, 0:512], in0=CH2[:, 0:512], in1=CH2[:, 0:512]))
        CB("vector")
        P("vector", lambda e: e.tensor_mul(
            out=CHS[:, 0:512], in0=CHS[:, 0:512], in1=CHS[:, 0:512]))
        CB("vector")
        P("vector", lambda e: e.tensor_scalar_mul(
            out=CHS[:, 0:512], in0=CHS[:, 0:512], scalar1=EPS))
        CB("vector")
        P("vector", lambda e: e.tensor_sub(
            out=CH2[:, 0:512], in0=CH2[:, 0:512], in1=CHS[:, 0:512]))
        CB("vector")
        P("vector", lambda e, d=mbf: e.tensor_mul(
            out=CHS[:, 0:512], in0=d, in1=d))
        CB("vector")
        P("vector", lambda e: e.tensor_sub(
            out=CH2[:, 0:512], in0=CH2[:, 0:512], in1=CHS[:, 0:512]))
        CB("vector")
        P("vector", lambda e: e.tensor_add(
            out=CH2[:, 0:512], in0=CH2[:, 0:512], in1=pb[1][:, 0:512]))
        CB("vector")
        P("vector", lambda e: e.tensor_scalar_add(
            out=CH2[:, 0:512], in0=CH2[:, 0:512], scalar1=1.0 + EPS))
        v_var2 = M("vector")
        CB("vector")
        banks[1].r_trk = (t_vec, v_var2)
        banks[2].r_trk = (t_vec, v_var2)
        banks[6].r_trk = (t_vec, v_var2)
        W("scalar", t_vec, v_var2)
        P("scalar", lambda e: e.activation(
            out=CH2[:, 0:512], in_=CH2[:, 0:512], func=AF.Ln))
        CB("scalar")
        P("scalar", lambda e, d=invt: e.activation(
            out=d, in_=CH2[:, 0:512], func=AF.Exp, scale=-0.5))
        v_combo[wi] = M("scalar")
        return v_combo[wi]

    def outer1_apply(wi, c0, c1, v_chain):
        wl = c1 - c0
        vc = ln2_combo(wi, v_chain)
        mbf = MB_F if wi == 0 else MB_F2
        W("vector", t_scl, vc)
        if wi == 0:
            P("vector", lambda e: e.tensor_copy(out=SQT(0)[:, 0:512],
                                                in_=YATT(14, 512, 1024)))
            P("vector", lambda e: e.tensor_copy(out=SQT(1)[:, 0:512],
                                                in_=YATT(15, 512, 1024)))
            CB("vector")
        jorder = ([14, 15] + list(range(14))) if wi == 0 else range(DT)
        for j in jorder:
            if wi == 1 and j >= 14:
                src = SQT(j - 14)[:, 0:wl]
            else:
                src = YATT(j, c0, c0 + wl)
            P("vector", lambda e, j=j, c0=c0, wl=wl, src=src: e.tensor_sub(
                out=SQT(3)[:, 0:wl], in0=src, in1=MB_B[:, 0:wl]))
            CB("vector")
            P("vector", lambda e, j=j, c0=c0, wl=wl: e.tensor_mul(
                out=CH2[:, 0:wl], in0=SQT(3)[:, 0:wl], in1=INVB[:, 0:wl]))
            CB("vector")
            P("vector", lambda e, j=j, c0=c0, wl=wl: e.tensor_add(
                out=Xj(j, PAD + c0, PAD + c0 + wl),
                in0=Xj(j, PAD + c0, PAD + c0 + wl), in1=CH2[:, 0:wl]))
            CB("vector")
            if wi == 0:
                P("vector", lambda e, j=j, c0=c0, wl=wl, d=mbf: e.tensor_sub(
                    out=SQT(3)[:, 0:wl],
                    in0=Xj(j, PAD + c0, PAD + c0 + wl), in1=d[:, 0:wl]))
                CB("vector")
                P("vector", lambda e, j=j: e.tensor_mul(
                    out=X2NW(0, j), in0=SQT(3)[:, 0:512], in1=INVB0))
                v_x2n[(0, j)] = M("vector")
                CB("vector")
        ln_state["mb_busy"] = (t_vec, M("vector"))

    emit_ln_chain(WL2, [(None, MB_B, INVB)] * 2, outer1_apply, v_rows1)
    tap("x2", XT)

    act_table("silu_and_others")

    d_out3 = d_out.rearrange("p (j c) -> p j c", j=DT)
    v_out_parts = []
    v_rows_f = {}
    v_final_apply = {}

    def final_apply(wi, c0, c1, v_chain):
        wl = c1 - c0
        W("vector", t_scl, v_chain)
        for j in range(DT):
            P("vector", lambda e, j=j, wi=wi, wl=wl: e.tensor_sub(
                out=SQT(2)[:, 0:wl], in0=Y2W(wi, j), in1=MB_B[:, 0:wl]))
            CB("vector")
            P("vector", lambda e, j=j, wl=wl: e.tensor_mul(
                out=CH2[:, 0:wl], in0=SQT(2)[:, 0:wl], in1=INVB[:, 0:wl]))
            CB("vector")
            P("vector", lambda e, j=j, c0=c0, wl=wl: e.tensor_add(
                out=Xj(j, PAD + c0, PAD + c0 + wl),
                in0=Xj(j, PAD + c0, PAD + c0 + wl), in1=CH2[:, 0:wl]))
            CB("vector")
        v_apply = M("vector")
        v_final_apply[wi] = v_apply
        ln_state["mb_busy"] = (t_vec, v_apply)
        W("sync", t_vec, v_apply)
        v_out_parts.append(DMA("sync", s_out, lambda e, c0=c0, c1=c1:
                               e.dma_start(out=d_out3[:, :, c0:c1],
                                           in_=x_ar3[:, :, PAD + c0:PAD + c1])))

    def emit_final(wi, v_rows, ba=6, bb=7):
        c0 = 512 * wi
        emit_ln_chain([(c0, c0 + 512)], [(None, MB_B, INVB)],
                      lambda _w, a, b, v: final_apply(wi, a, b, v), v_rows,
                      ba=ba, bb=bb)

    v_wd = {}

    def WGUPW(w, slot):
        t = WGUA if w == 0 else WGUB
        return t[:, slot * 4096:(slot + 1) * 4096]

    def WRDW(w, slot):
        t = WRDAT if w == 0 else WRDBT
        return t[:, slot * 2048:(slot + 1) * 2048]

    for k in range(2):
        l0 = 512 * k
        if k == 1:
            W("vector", t_scl, v_combo[1])
            for j in range(DT):
                P("vector", lambda e, j=j, l0=l0: e.tensor_sub(
                    out=SQT(3)[:, 0:512], in0=Xj(j, PAD + l0, PAD + l0 + 512),
                    in1=MB_F2[:, 0:512]))
                CB("vector")
                P("vector", lambda e, j=j: e.tensor_mul(
                    out=X2NW(1, j), in0=SQT(3)[:, 0:512], in1=INVB2[:, 0:512]))
                v_x2n[(1, j)] = M("vector")
                CB("vector")
        slg = (lambda i: i % 2) if k == 0 else (lambda i: (i + 1) % 2)
        if k == 0:
            pass
        else:
            v_wgu[0] = v_wgu_k1
            W("gpsimd", t_pe, t_pe.n)
            v_wgu[1] = DMA("gpsimd", s_wrf[0],
                           lambda e: e.dma_start(out=WGUPW(1, 0),
                                                 in_=d_wgu[1]))
        v_fuse_last = 0
        v_scr_gate = t_scl.n
        for i in range(IT):
            W("tensor", v_wgu[i])
            bg = banks[i % 2]
            bu = banks[2 + i % 2]
            pe_acquire(bg)
            pe_acquire(bu)
            for j in range(DT):
                if i == 0:
                    W("tensor", t_vec, v_x2n[(k, j)])
                mm(pb[bg.idx][:, 0:512],
                   WGUPW(k, slg(i))[:, j * 128:(j + 1) * 128],
                   X2NW(k, j), j == 0, j == DT - 1)
            bg.w_pe = t_pe.n
            for j in range(DT):
                mm(pb[bu.idx][:, 0:512],
                   WGUPW(k, slg(i))[:, 2048 + j * 128:2048 + (j + 1) * 128],
                   X2NW(k, j), j == 0, j == DT - 1)
            bu.w_pe = t_pe.n
            W("scalar", t_pe, bg.w_pe)
            if i == 0:
                W("scalar", t_scl, v_scr_gate)
            if i >= 1:
                W("scalar", t_vec, v_fuse_last)
            P("scalar", lambda e, i=i, b=bg.idx: e.activation(
                out=GS, in_=pb[b][:, 0:512], func=AF.Silu,
                scale=SGC[:, i:i + 1]))
            vsc = M("scalar")
            bg.r_trk = (t_scl, vsc)
            W("vector", t_scl, vsc)
            W("vector", t_pe, bu.w_pe)
            P("vector", lambda e, i=i, k=k, b=bu.idx: e.scalar_tensor_tensor(
                out=FUSE(k, i), in0=pb[b][:, 0:512], scalar=SUC[:, i:i + 1],
                in1=GS, op0=ALU.mult, op1=ALU.mult))
            v_fuse_last = M("vector")
            bu.r_trk = (t_vec, v_fuse_last)
            if i + 2 < IT:
                W("gpsimd", t_pe, t_pe.n)
                v_wgu[i + 2] = DMA("gpsimd", s_wrf[slg(i)],
                                   lambda e, i=i, k=k, s=slg(i): e.dma_start(
                                       out=WGUPW(k, s), in_=d_wgu[i + 2]))
            if i == 48 and k == 0:
                W("gpsimd", t_pe, t_pe.n)
                for (tt, qq) in [(0, 0), (0, 1)]:
                    v_wd[(tt, qq)] = DMA(
                        "gpsimd", s_wrd[qq % 2],
                        lambda e, t=tt, q=qq, k=k: e.dma_start(
                            out=WRDW(k, q % 2),
                            in_=d_wd[t, q // 2, :, (q % 2) * 2048:
                                     (q % 2 + 1) * 2048]))
        if k == 1:
            W("gpsimd", t_pe, t_pe.n)
            W("gpsimd", t_vec, v_fuse_last)
            for (tt, qq) in [(0, 0), (0, 1)]:
                v_wd[(tt, qq)] = DMA(
                    "gpsimd", s_wrd[qq % 2],
                    lambda e, t=tt, q=qq, k=k: e.dma_start(
                        out=WRDW(k, q % 2),
                        in_=d_wd[t, q // 2, :, (q % 2) * 2048:
                                 (q % 2 + 1) * 2048]))
        W("tensor", t_vec, v_fuse_last)
        pe_acquire(banks[6 if k == 0 else 7])
        pe_acquire(banks[0 if k == 0 else 1])
        if k == 0:
            W("gpsimd", t_pe, t_pe.n)
            v_wgu_k1 = DMA("gpsimd", s_wrf[1],
                           lambda e: e.dma_start(out=WGUPW(1, 1),
                                                 in_=d_wgu[0]))
        if k == 1:
            act_table("natural_log_exp_and_others")
            ln_state["mb_busy"] = (t_vec, v_fuse_last)
            emit_final(0, v_fuse_last, ba=2, bb=3)
        sqd_prev = {0: 0, 1: 0}
        pend_sum = None
        for t in range(DT):
            bk = banks[4 + t % 2]
            pe_acquire(bk)
            for q in range(4):
                W("tensor", v_wd[(t, q)])
                for kk in range(16):
                    ki = q * 16 + kk
                    mm(pb[bk.idx][:, 0:512],
                       WRDW(k, q % 2)[:, kk * 128:(kk + 1) * 128],
                       FUSE(k, ki), ki == 0, ki == IT - 1)
                pre = (t + (q + 2) // 4, (q + 2) % 4)
                if pre[0] < DT and pre not in v_wd:
                    W("gpsimd", t_pe, t_pe.n)
                    v_wd[pre] = DMA(
                        "gpsimd", s_wrd[pre[1] % 2],
                        lambda e, p=pre, k=k: e.dma_start(
                            out=WRDW(k, p[1] % 2),
                            in_=d_wd[p[0], p[1] // 2, :, (p[1] % 2) * 2048:
                                     (p[1] % 2 + 1) * 2048]))
            bk.w_pe = t_pe.n
            if pend_sum is not None:
                pend_sum()
            sl = t % 2
            W("scalar", t_pe, bk.w_pe)
            W("scalar", t_pe, sqd_prev[sl])
            P("scalar", lambda e, t=t, k=k, b=bk.idx: e.activation(
                out=Y2W(k, t), in_=pb[b][:, 0:512], func=AF.Identity,
                scale=SDC[:, t:t + 1]))
            CB("scalar")
            P("scalar", lambda e, t=t, k=k, sl=sl: e.activation(
                out=SQT(sl)[:, 0:512], in_=Y2W(k, t), func=AF.Square))
            vsc = M("scalar")
            bk.r_trk = (t_scl, vsc)

            def pend_sum(t=t, k=k, sl=sl, vsc=vsc):
                W("tensor", t_scl, vsc)
                mm(pb[6 if k == 0 else 7][0:1, 0:512], ONESB, Y2W(k, t),
                   t == 0, t == DT - 1)
                sqd_prev[sl] = mm(pb[0 if k == 0 else 1][0:1, 0:512], ONESB,
                                  SQT(sl)[:, 0:512], t == 0, t == DT - 1)
        pend_sum()
        banks[6 if k == 0 else 7].w_pe = t_pe.n
        banks[0 if k == 0 else 1].w_pe = t_pe.n
        W("vector", t_pe, t_pe.n)
        P("vector", lambda e, k=k, l0=l0: e.tensor_copy(
            out=SRA[0:1, l0:l0 + 512], in_=pb[6 if k == 0 else 7][0:1, 0:512]))
        P("vector", lambda e, k=k, l0=l0: e.tensor_copy(
            out=SRB[0:1, l0:l0 + 512], in_=pb[0 if k == 0 else 1][0:1, 0:512]))
        vrf = M("vector")
        banks[6 if k == 0 else 7].r_trk = (t_vec, vrf)
        banks[0 if k == 0 else 1].r_trk = (t_vec, vrf)
        v_wd = {}
        v_rows_f[k] = t_vec.n
        if k == 0:
            tap("y2b", Y2AT)
        else:
            emit_final(1, v_rows_f[1])
    v_y2_all = t_scl.n

    W("gpsimd", s_out, s_out.n)
    if s_tap.n:
        W("gpsimd", s_tap, s_tap.n)

    with nc.Block() as block:
        @block.sync
        def _(e):
            for f in plan["sync"]:
                f(e)

        @block.gpsimd
        def _(e):
            for f in plan["gpsimd"]:
                f(e)

        @block.scalar
        def _(e):
            for f in plan["scalar"]:
                f(e)

        @block.vector
        def _(e):
            for f in plan["vector"]:
                f(e)

        @block.tensor
        def _(e):
            for f in plan["tensor"]:
                f(e)

    return nc



def _prep_shared(inp):
    f32 = np.float32
    bf = ml_dtypes.bfloat16

    def tile_kblocks(w, nblk):
        K = w.shape[0]
        out = np.empty((nblk, 128, K), w.dtype)
        for t in range(nblk):
            blk = w[:, t * 128:(t + 1) * 128]
            out[t] = blk.reshape(K // 128, 128, 128).transpose(1, 0, 2).reshape(
                128, K)
        return np.ascontiguousarray(out)

    w_in = np.asarray(inp["L0_attn_proj_in_w"], f32)
    w_out = np.asarray(inp["L0_attn_proj_out_w"], f32)
    sh = {}
    sh["w_in_t"] = tile_kblocks(np.ascontiguousarray(w_in.T), DT)
    sh["w_out_t"] = tile_kblocks(np.ascontiguousarray(w_out.T), DT)
    bwt = np.ascontiguousarray(np.asarray(inp["L0_attn_B_w"], f32).T)
    sh["b_w_t"] = np.ascontiguousarray(
        bwt.reshape(16, 128, 16).transpose(1, 0, 2).reshape(128, 256))
    sh["c_w_t"] = np.ascontiguousarray(np.asarray(inp["L0_attn_C_w"], f32).T)
    a = np.asarray(inp["L0_attn_A"], f32)
    sh["a_t"] = np.ascontiguousarray(
        a.reshape(16, 128, 16).transpose(1, 0, 2).reshape(128, 256))
    for nm, key in [("dcol", "L0_attn_D_param"), ("bincol", "L0_attn_proj_in_b"),
                    ("boutcol", "L0_attn_proj_out_b"), ("sdcol", "L0_ffn_s_down")]:
        sh[nm] = np.ascontiguousarray(np.asarray(inp[key], f32).reshape(16, 128).T)
    for nm, key in [("sgcol", "L0_ffn_s_gate"), ("sucol", "L0_ffn_s_up")]:
        sh[nm] = np.ascontiguousarray(np.asarray(inp[key], f32).reshape(64, 128).T)
    wg_t = tile_kblocks(np.asarray(inp["L0_ffn_w_gate"]), IT)
    wu_t = tile_kblocks(np.asarray(inp["L0_ffn_w_up"]), IT)
    sh["wgu_t"] = np.ascontiguousarray(
        np.concatenate([wg_t, wu_t], axis=2))
    wd = np.asarray(inp["L0_ffn_w_down"])
    wd_t = np.empty((DT, 2, 128, 4096), np.int8)
    for t in range(DT):
        for hf in range(2):
            blk = wd[hf * 4096:(hf + 1) * 4096, t * 128:(t + 1) * 128]
            wd_t[t, hf] = blk.reshape(32, 128, 128).transpose(1, 0, 2).reshape(
                128, 4096)
    sh["wd_t"] = np.ascontiguousarray(wd_t)
    rep = np.zeros((16, 128), bf)
    for c in range(128):
        rep[c % 16, c] = 1.0
    sh["rep16b"] = rep
    sh["ones_b"] = np.ones((128, 1), bf)
    sh["ones_f"] = np.ones((1, 128), f32)
    sh["ood_f"] = np.full((1, 128), 1.0 / D, f32)
    sh["ood_b"] = np.full((1, 128), 1.0 / D, bf)
    sh["ident_f"] = np.eye(128, dtype=f32)
    return sh


def _prep_core(x, b, h):
    f32 = np.float32
    bf = ml_dtypes.bfloat16
    xt = np.ascontiguousarray(x[b].T.astype(f32))
    if h == 0:
        sl = np.concatenate([np.zeros((D, PAD), f32), xt[:, 0:TOUT]], axis=1)
    else:
        sl = xt[:, TOUT - PAD:2 * TOUT]
    xT = np.ascontiguousarray(
        sl.reshape(16, 128, T).transpose(1, 0, 2).reshape(128, DT * T))
    mask = np.ones((16, T), bf)
    if h == 0:
        mask[:, 0:PAD] = 0
    return {"xT": xT, "maskb": mask}


def kernel(**inputs):
    x = np.asarray(inputs["L0_attn_x"], np.float32)
    sh = _prep_shared(inputs)
    in_maps = []
    for c in range(8):
        b, h = divmod(c, 2)
        m = dict(sh)
        m.update(_prep_core(x, b, h))
        in_maps.append(m)
    nc = build_program()
    res = run_bass_kernel_spmd(nc, in_maps, core_ids=list(range(8)))
    out = np.empty((4, 2048, 2048), np.float32)
    for c in range(8):
        b, h = divmod(c, 2)
        o = np.asarray(res.results[c]["out_xT"])
        ot = o.reshape(128, DT, TOUT).transpose(1, 0, 2).reshape(D, TOUT)
        out[b, h * TOUT:(h + 1) * TOUT, :] = ot.T
    return out
